# version 04
# baseline (speedup 1.0000x reference)
import numpy as np
import ml_dtypes
from contextlib import ExitStack
import concourse.bass as bass
import concourse.mybir as mybir
from concourse.bass_utils import run_bass_kernel_spmd

F32 = mybir.dt.float32
BF16 = mybir.dt.bfloat16
AF = mybir.ActivationFunctionType
ALU = mybir.AluOpType
AX = mybir.AxisListType

NCORES = 8
B = 4
S = 4096
D = 2048
SC = S // NCORES
T = B * SC
NTC = T // 128
NSC = SC // 128
EPS_LN = 1e-5
EPS_ATTN = 1e-6
EPS_FFN = 1e-5



def _host_tables(core_id):
    t = {}
    t["ident"] = np.eye(128, dtype=ml_dtypes.bfloat16)
    t["triuT"] = np.triu(np.ones((128, 128), np.float32)).astype(ml_dtypes.bfloat16)
    invA = 10000.0 ** (-np.arange(64, dtype=np.float64) / 64.0)
    cosA = np.zeros((128, NSC * 128), np.float64)
    sinAs = np.zeros((128, NSC * 128), np.float64)
    for sc in range(NSC):
        s_glob = core_id * SC + sc * 128 + np.arange(128)
        ang = s_glob[:, None] * invA[None, :]
        c, sn = np.cos(ang), np.sin(ang)
        cosA[:, sc * 128:sc * 128 + 64] = c
        cosA[:, sc * 128 + 64:sc * 128 + 128] = c
        sinAs[:, sc * 128:sc * 128 + 64] = -sn
        sinAs[:, sc * 128 + 64:sc * 128 + 128] = sn
    t["cosA"] = cosA.astype(ml_dtypes.bfloat16)
    t["sinAs"] = sinAs.astype(ml_dtypes.bfloat16)
    invF = 10000.0 ** (-np.arange(32, dtype=np.float64) / 32.0)
    cosF = np.zeros((128, B * 64), np.float64)
    sinFs = np.zeros((128, B * 64), np.float64)
    for l in range(B):
        ang = l * invF
        c, sn = np.cos(ang), np.sin(ang)
        cosF[:, l * 64:l * 64 + 32] = c[None, :]
        cosF[:, l * 64 + 32:l * 64 + 64] = c[None, :]
        sinFs[:, l * 64:l * 64 + 32] = -sn[None, :]
        sinFs[:, l * 64 + 32:l * 64 + 64] = sn[None, :]
    t["cosF"] = cosF.astype(ml_dtypes.bfloat16)
    t["sinFs"] = sinFs.astype(ml_dtypes.bfloat16)
    cb = np.zeros((128, B * 1024), np.float64)
    sb = np.zeros((128, B * 1024), np.float64)
    for l in range(B):
        for h in range(16):
            cb[:, l * 1024 + h * 64:l * 1024 + h * 64 + 64] = cosF[:, l * 64:(l + 1) * 64]
            sb[:, l * 1024 + h * 64:l * 1024 + h * 64 + 64] = sinFs[:, l * 64:(l + 1) * 64]
    t["cosFb"] = cb.astype(ml_dtypes.bfloat16)
    t["sinFb"] = sb.astype(ml_dtypes.bfloat16)
    m = np.zeros((128, NCORES), np.float32)
    m[:, :core_id] = 1.0
    t["maskc"] = m
    e = np.zeros((128, 4), np.float32)
    e[:, 0] = EPS_LN
    e[:, 1] = EPS_ATTN
    e[:, 2] = EPS_FFN
    t["epsb"] = e
    return t


def _shard_inputs(inputs):
    x = np.asarray(inputs["L0_attn_x"], np.float32)
    c32 = lambda a: np.ascontiguousarray(np.asarray(a, np.float32))
    wqT = c32(np.asarray(inputs["L0_attn_Wq"], np.float32).T)
    wkT = c32(np.asarray(inputs["L0_attn_Wk"], np.float32).T)
    wvT = c32(np.asarray(inputs["L0_attn_Wv"], np.float32).T)
    woT = c32(np.asarray(inputs["L0_attn_Wo"], np.float32).T)
    fwq = c32(inputs["L0_ffn_Wq"])
    fwk = c32(inputs["L0_ffn_Wk"])
    fwv = c32(inputs["L0_ffn_Wv"])
    fwvT = c32(np.asarray(inputs["L0_ffn_Wv"], np.float32).T)
    fwg = c32(inputs["L0_ffn_Wg"])
    fwout = c32(inputs["L0_ffn_Wout"])
    cw = np.asarray(inputs["L0_ffn_conv_w"], np.float32)
    cvw = np.zeros((128, 32), np.float32)
    cvw[:, 0:16] = cw[:, 0].reshape(16, 128).T
    cvw[:, 16:32] = cw[:, 1].reshape(16, 128).T
    rw = np.asarray(inputs["L0_ffn_router_w"], np.float32)
    rwfm = np.zeros((128, 32), np.float32)
    for h in range(16):
        rwfm[:, 2 * h:2 * h + 2] = rw[h]
    in_maps = []
    for c in range(NCORES):
        xs = np.ascontiguousarray(x[:, c * SC:(c + 1) * SC, :].reshape(T, D))
        m = {"x": xs, "wqT": wqT, "wkT": wkT, "wvT": wvT, "woT": woT,
             "fwq": fwq, "fwk": fwk, "fwv": fwv, "fwvT": fwvT, "fwg": fwg,
             "fwout": fwout, "cvw": cvw, "rwfm": rwfm}
        ht = _host_tables(c)
        m.update({("d_" + k if k in ("ident", "triuT", "cosA", "sinAs", "cosF", "sinFs",
                                     "cosFb", "sinFb") else k): v
                  for k, v in ht.items()})
        in_maps.append(m)
    return in_maps



ENGINES = ("tensor", "vector", "scalar", "gpsimd", "sync")


class Sem:
    def __init__(self, handle, owner=None):
        self.h = handle
        self.n = 0
        self.owner = owner


class Prog:
    def __init__(self):
        self.ops = {e: [] for e in ENGINES}
        self.maxwait = {e: {} for e in ENGINES}

    def emit(self, eng, fn, waits=(), inc=None, inck=1):
        ws = []
        for w in waits:
            if w is None:
                continue
            sem, cnt = w
            if cnt <= 0 or sem.owner == eng:
                continue
            prev = self.maxwait[eng].get(id(sem), -1)
            if cnt > prev:
                self.maxwait[eng][id(sem)] = cnt
                ws.append((sem, cnt))
        dep = None
        if inc is not None:
            inc.n += inck
            dep = (inc, inc.n)
        self.ops[eng].append((fn, ws, inc, inck))
        return dep

    def marks(self, sems):
        return [(s, s.n) for s in sems]

    def gate(self, marks):
        for e in ENGINES:
            self.emit(e, None, waits=marks)

    def flush(self, block):
        for e in ENGINES:
            ops = self.ops[e]
            def body(eng, _ops=ops):
                for fn, ws, inc, inck in _ops:
                    for sem, cnt in ws:
                        eng.wait_ge(sem.h, cnt)
                    if fn is None:
                        continue
                    ins = fn(eng)
                    if inc is not None:
                        ins.then_inc(inc.h, inck)
            getattr(block, e)(body)


def rows_fm(w_dram, ncols):
    return w_dram.rearrange("(dc p) n -> p dc n", p=128)[:, :, 0:ncols]



def build_nc(debug=False):
    nc = bass.Bass("TRN2", target_bir_lowering=False)

    def din(name, shape, dt=F32):
        return nc.dram_tensor(name, shape, dt, kind="ExternalInput").ap()

    x = din("x", [T, D])
    wqT = din("wqT", [D, 2048]); wkT = din("wkT", [D, 512]); wvT = din("wvT", [D, 512])
    woT = din("woT", [2048, D])
    fwq = din("fwq", [D, 1024]); fwk = din("fwk", [D, 1024]); fwv = din("fwv", [D, 2048])
    fwvT = din("fwvT", [2048, D]); fwg = din("fwg", [D, 2048]); fwout = din("fwout", [2048, D])
    cvw_d = din("cvw", [128, 32]); rwfm_d = din("rwfm", [128, 32])
    identd = din("d_ident", [128, 128], BF16); triuTd = din("d_triuT", [128, 128], BF16)
    cosAd = din("d_cosA", [128, NSC * 128], BF16); sinAsd = din("d_sinAs", [128, NSC * 128], BF16)
    cosFd = din("d_cosF", [128, B * 64], BF16); sinFsd = din("d_sinFs", [128, B * 64], BF16)
    cosFbd = din("d_cosFb", [128, B * 1024], BF16); sinFbd = din("d_sinFb", [128, B * 1024], BF16)
    maskcd = din("maskc", [128, NCORES]); epsbd = din("epsb", [128, 4])

    out = nc.dram_tensor("out", [T, D], F32, kind="ExternalOutput").ap()

    cc_in = nc.dram_tensor("cc_in", [16 * 128, 129], F32)
    cc_outs = [nc.dram_tensor(f"cc_out{b}", [NCORES * 4 * 128, 129], F32) for b in range(B)]
    mixb = nc.dram_tensor("mixb", [T, 2048], BF16).ap()
    outpb = nc.dram_tensor("outpb", [T, 2048], BF16).ap()

    dbg = {}
    def ddbg(name, shape, dt):
        if debug:
            dbg[name] = nc.dram_tensor("dbg_" + name, shape, dt, kind="ExternalOutput").ap()
    ddbg("xn", [128, 16 * T], BF16)
    ddbg("kf_fm", [128, B * 4 * SC], BF16)
    ddbg("vext", [128, NTC * 516], BF16)
    ddbg("snap", [128, 16 * 3 * 129], BF16)
    ddbg("prefall", [128, 16 * 4 * 129], BF16)
    ddbg("qf0", [128, NTC * 512], BF16)
    ddbg("wvr", [128, 512], BF16)
    ddbg("logits", [128, B * 32], F32)
    ddbg("xc", [128, 16 * T], BF16)
    ddbg("qs4", [128, B * 1024], BF16)
    ddbg("ks4", [128, B * 1024], BF16)
    ddbg("vs4", [128, B * 2048], BF16)
    ddbg("onrm", [128, B * 2048], BF16)
    ddbg("dots", [128, 10 * 16], F32)
    ddbg("qkc", [128, 10 * 16], BF16)
    ddbg("mixb", [T, 2048], BF16)
    ddbg("outpb", [T, 2048], BF16)

    es = ExitStack()
    alloc = lambda name, shape, dt=BF16: es.enter_context(nc.sbuf_tensor(name, shape, dt))

    ident = alloc("ident", [128, 128])
    triuT = alloc("triuT", [128, 128])
    cosA = alloc("cosA", [128, NSC * 128])
    sinAs = alloc("sinAs", [128, NSC * 128])
    cosF = alloc("cosF", [128, B * 64])
    sinFs = alloc("sinFs", [128, B * 64])
    cvws = alloc("cvws", [128, 32], F32)
    rwfm = alloc("rwfm_s", [128, 32])
    maskc = alloc("maskc_s", [128, NCORES], F32)
    epsb = alloc("epsb_s", [128, 4], F32)
    wvr = alloc("wvr", [128, 512])
    junk = alloc("junk", [128, D + 16], F32)
    stat = alloc("stat", [128, 16], F32)

    pb = [es.enter_context(nc.psum_tensor(f"pb{i}", [128, 512], F32)) for i in range(8)]
    bank_w = [None] * 8
    bank_r = [[] for _ in range(8)]

    sems = {}
    for nm, owner in [("sWL", None), ("sXL", None), ("sST", None), ("sPE", "tensor"),
                      ("sV", "vector"), ("sSc", "scalar"), ("sG", "gpsimd"), ("sCC", None)]:
        sems[nm] = Sem(es.enter_context(nc.semaphore(nm)), owner)
    sWL, sXL, sST, sPE, sV, sSc, sG, sCC = (sems[n] for n in
        ["sWL", "sXL", "sST", "sPE", "sV", "sSc", "sG", "sCC"])
    allsems = [sWL, sXL, sST, sPE, sV, sSc, sCC]

    p = Prog()

    def mm(bank_i, c0, c1, lhsT, rhs, start, stop, waits=()):
        ws = list(waits) + bank_r[bank_i]
        def fn(t, _o=pb[bank_i][:, c0:c1], _l=lhsT, _r=rhs, _s=start, _p=stop):
            return t.matmul(_o, _l, _r, start=_s, stop=_p, skip_group_check=True)
        dep = p.emit("tensor", fn, waits=ws, inc=sPE)
        bank_r[bank_i] = []
        bank_w[bank_i] = dep
        return dep

    def bank_read(bank_i, eng, fn, waits=()):
        ws = list(waits)
        if bank_w[bank_i] is not None:
            ws.append(bank_w[bank_i])
        inc = {"vector": sV, "scalar": sSc, "gpsimd": sG}[eng]
        dep = p.emit(eng, fn, waits=ws, inc=inc)
        bank_r[bank_i].append(dep)
        return dep

    def dma(eng, out_ap, in_ap, sem, waits=()):
        def fn(e, _o=out_ap, _i=in_ap):
            return e.dma_start(out=_o, in_=_i)
        return p.emit(eng, fn, waits=waits, inc=sem, inck=16)

    def vop(fn, waits=()):
        return p.emit("vector", fn, waits=waits, inc=sV)

    def sop(fn, waits=()):
        return p.emit("scalar", fn, waits=waits, inc=sSc)

    def vopd(fn, waits=()):
        d = p.emit("vector", fn, waits=waits, inc=sV)
        p.emit("vector", lambda v: v.drain())
        return d

    def sopd(fn, waits=()):
        d = p.emit("scalar", fn, waits=waits, inc=sSc)
        p.emit("scalar", lambda s: s.drain())
        return d

    for dst, src in [(ident, identd), (triuT, triuTd), (cosA, cosAd),
                     (sinAs, sinAsd), (cosF, cosFd), (sinFs, sinFsd),
                     (cvws, cvw_d), (maskc, maskcd), (epsb, epsbd)]:
        dma("sync", dst[:, :], src, sXL)
    rw_dep = dma("gpsimd", rwfm[:, :], rwfm_d, sWL)
    p.gate(p.marks([sXL]) + [rw_dep])

    es_xn = ExitStack()
    xn = es_xn.enter_context(nc.sbuf_tensor("xn", [128, 16 * T], BF16))
    es_w = ExitStack()
    wpA = es_w.enter_context(nc.sbuf_tensor("wpA", [128, 16 * 512], BF16))
    wpB = es_w.enter_context(nc.sbuf_tensor("wpB", [128, 16 * 512], BF16))
    wpC = es_w.enter_context(nc.sbuf_tensor("wpC", [128, 16 * 512], BF16))

    wvr_mm_deps = []
    for h in range(16):
        buf = wpB[:, (h % 2) * 4096:(h % 2) * 4096 + 2048]
        ws = [rw_dep] + ([wvr_mm_deps[h - 2]] if h >= 2 else [])
        ld = dma("gpsimd", buf, fwvT[h * 128:(h + 1) * 128, :], sWL, waits=ws)
        dep = None
        for dc in range(16):
            dep = mm(7, dc * 32 + h * 2, dc * 32 + h * 2 + 2,
                     buf[:, dc * 128:(dc + 1) * 128], rwfm[:, 2 * h:2 * h + 2],
                     start=True, stop=True, waits=[ld])
        wvr_mm_deps.append(dep)
    wvr_dep = bank_read(7, "vector", lambda v: v.tensor_copy(wvr[:, :], pb[7][:, :]))
    if debug:
        dma("sync", dbg["wvr"], wvr[:, :], sST, waits=[wvr_dep])

    def xn_chunk(dc, tc):
        return xn[:, dc * T + tc * 128: dc * T + (tc + 1) * 128]

    es_s1 = ExitStack()
    xb = [es_s1.enter_context(nc.sbuf_tensor(f"xb{i}", [128, D], F32)) for i in range(2)]
    xnb = es_s1.enter_context(nc.sbuf_tensor("xnb", [128, D], BF16))
    xn_ready = [[] for _ in range(NTC)]
    x_cons = [None] * NTC
    xnb_free = None
    for tc in range(NTC):
        ws = [x_cons[tc - 2]] if tc >= 2 else []
        xld = dma("sync", xb[tc % 2][:, :], x[tc * 128:(tc + 1) * 128, :], sXL, waits=ws)
        sop(lambda sce, _b=xb[tc % 2]: sce.activation(junk[:, 0:D], _b[:, :], AF.Identity,
                                                      accum_out=stat[:, 0:1]),
            waits=[xld, x_cons[tc - 1] if tc >= 1 else None])
        st_dep = sop(lambda sce, _b=xb[tc % 2]: sce.activation(junk[:, 0:D], _b[:, :], AF.Square,
                                                               accum_out=stat[:, 1:2]))
        vopd(lambda v: v.tensor_scalar(stat[:, 2:3], stat[:, 0:1], 1.0 / D, None, ALU.mult),
            waits=[st_dep])
        vopd(lambda v: v.tensor_scalar(stat[:, 3:4], stat[:, 1:2], 1.0 / D, None, ALU.mult))
        vopd(lambda v: v.tensor_tensor(stat[:, 4:5], stat[:, 2:3], stat[:, 2:3], ALU.mult))
        var_dep = vopd(lambda v: v.tensor_tensor(stat[:, 5:6], stat[:, 3:4], stat[:, 4:5], ALU.subtract))
        sd_dep = sop(lambda sce: sce.activation(stat[:, 6:7], stat[:, 5:6], AF.Sqrt,
                                                bias=epsb[:, 0:1]), waits=[var_dep])
        vopd(lambda v: v.reciprocal(stat[:, 7:8], stat[:, 6:7]), waits=[sd_dep])
        nrm_dep = vop(lambda v, _b=xb[tc % 2]: v.tensor_scalar(
            xnb[:, :], _b[:, :], stat[:, 2:3], stat[:, 7:8], ALU.subtract, ALU.mult),
            waits=[xnb_free])
        x_cons[tc] = nrm_dep
        tr_dep = None
        for grp in range(4):
            bk = grp
            for k in range(4):
                dcc = grp * 4 + k
                tr_dep = mm(bk, k * 128, (k + 1) * 128,
                            xnb[:, dcc * 128:(dcc + 1) * 128], ident[:, :],
                            start=True, stop=(k == 3), waits=[nrm_dep])
            outv = xn[:, :].rearrange("p (dc t) -> p dc t", t=T)[
                :, grp * 4:grp * 4 + 4, tc * 128:(tc + 1) * 128]
            pv = pb[bk][:, :].rearrange("p (k t) -> p k t", t=128)
            if grp % 2 == 0:
                ed = bank_read(bk, "vector", lambda v, _o=outv, _i=pv: v.tensor_copy(_o, _i))
            else:
                ed = bank_read(bk, "scalar", lambda sce, _o=outv, _i=pv: sce.copy(_o, _i))
            xn_ready[tc].append(ed)
        xnb_free = tr_dep
    if debug:
        gm = p.marks(allsems)
        dma("sync", dbg["xn"], xn[:, :], sST, waits=gm)
    es_s1.close()

    es_attn = ExitStack()
    aallo = lambda name, shape, dt=BF16: es_attn.enter_context(nc.sbuf_tensor(name, shape, dt))
    kf_fm = aallo("kf_fm", [128, B * 4 * SC])
    vext = aallo("vext", [128, NTC * 516])
    prefall = aallo("prefall", [128, 16 * 4 * 129])
    t1b = aallo("t1b", [128, 512])
    t2b = aallo("t2b", [128, 512])
    rawb = aallo("rawb", [128, 1024])
    mnb = aallo("mnb", [128, 512])
    exb = aallo("exb", [128, 512])
    ktmb = aallo("ktmb", [128, 512])

    es_snap = ExitStack()
    snap = es_snap.enter_context(nc.sbuf_tensor("snap", [128, 16 * 3 * 129], BF16))
    gsum = es_snap.enter_context(nc.sbuf_tensor("gsum", [128, 16 * 129], F32))
    gtmp = es_snap.enter_context(nc.sbuf_tensor("gtmp", [128, 4 * 129], F32))
    stot = es_snap.enter_context(nc.sbuf_tensor("stot", [128, 4 * 129], F32))

    ones_view = vext[:, :].rearrange("p (t k c) -> p t k c", k=4, c=129)[:, :, :, 128:129]
    ones_dep = vop(lambda v: v.memset(ones_view, 1.0))
    snap_deps = []

    wk_dep = []
    for q4 in range(4):
        wk_dep.append(dma("gpsimd",
            wpA[:, :].rearrange("p (dc n) -> p dc n", n=512)[:, q4 * 4:q4 * 4 + 4, :],
            rows_fm(wkT, 512)[:, q4 * 4:q4 * 4 + 4, :], sWL,
            waits=[wvr_mm_deps[14], wvr_mm_deps[15]]))
    wv_dep = []
    for q4 in range(4):
        wv_dep.append(dma("gpsimd",
            wpB[:, :].rearrange("p (dc n) -> p dc n", n=512)[:, q4 * 4:q4 * 4 + 4, :],
            rows_fm(wvT, 512)[:, q4 * 4:q4 * 4 + 4, :], sWL,
            waits=[wvr_mm_deps[14], wvr_mm_deps[15], wvr_dep]))

    pre_panel_deps = []
    for q4 in range(4):
        pre_panel_deps.append(dma("gpsimd",
            wpC[:, :].rearrange("p (dc n) -> p dc n", n=512)[:, q4 * 4:q4 * 4 + 4, :],
            rows_fm(wqT, 2048)[:, q4 * 4:q4 * 4 + 4, 0:512], sWL))

    rope_free = [None]

    def rope_elu(sc_i, out_ap, raw_waits, bank_i):
        W = 512
        raw_dep = bank_read(bank_i, "scalar",
                            lambda sce, _b=bank_i: sce.copy(rawb[:, 0:512], pb[_b][:, :]),
                            waits=list(raw_waits) + [rope_free[0]])
        rv = rawb[:, 0:512].rearrange("p (u d) -> p u d", d=128)
        cosb = cosA[:, sc_i * 128:(sc_i + 1) * 128][:, None, :].broadcast_to([128, 4, 128])
        sin_lo = sinAs[:, sc_i * 128:sc_i * 128 + 64][:, None, :].broadcast_to([128, 4, 64])
        sin_hi = sinAs[:, sc_i * 128 + 64:sc_i * 128 + 128][:, None, :].broadcast_to([128, 4, 64])
        t1v = t1b[:, 0:512].rearrange("p (u d) -> p u d", d=128)
        t2v = t2b[:, 0:512].rearrange("p (u d) -> p u d", d=128)
        vop(lambda v, _a=t1v, _b=rv, _c=cosb: v.tensor_tensor(_a, _b, _c, ALU.mult),
            waits=[raw_dep])
        vop(lambda v, _a=t2v, _b=rv, _c=sin_lo: v.tensor_tensor(
            _a[:, :, 0:64], _b[:, :, 64:128], _c, ALU.mult))
        vop(lambda v, _a=t2v, _b=rv, _c=sin_hi: v.tensor_tensor(
            _a[:, :, 64:128], _b[:, :, 0:64], _c, ALU.mult))
        vop(lambda v: v.tensor_tensor(mnb[:, 0:512], t1b[:, 0:512], t2b[:, 0:512], ALU.add))
        vop(lambda v: v.tensor_scalar(rawb[:, 512:1024], mnb[:, 0:512], 0.0, None, ALU.min))
        mn_mark = (sV, sV.n)
        ex_dep = sop(lambda sce: sce.activation(exb[:, 0:512], rawb[:, 512:1024], AF.Exp),
                     waits=[mn_mark])
        out_dep = vop(lambda v, _o=out_ap: v.scalar_tensor_tensor(
            _o, mnb[:, 0:512], 0.0, exb[:, 0:512], ALU.max, ALU.add),
            waits=[ex_dep, (sPE, sPE.n)])
        rope_free[0] = out_dep
        return out_dep

    kfm_ready = [None] * NTC
    vext_ready = [None] * NTC
    ccin_deps = []
    cc_deps = []
    for tc in range(NTC):
        b, sc_i = tc // 4, tc % 4
        for dc in range(16):
            mm(4, 0, 512, xn_chunk(dc, tc), wpA[:, dc * 512:(dc + 1) * 512],
               start=(dc == 0), stop=(dc == 15),
               waits=[wk_dep[dc // 4]] + (xn_ready[tc] if dc == 0 else []))
        kf_dep = rope_elu(sc_i, ktmb[:, 0:512], [], 4)
        for dc in range(16):
            mm(5, 0, 512, xn_chunk(dc, tc), wpB[:, dc * 512:(dc + 1) * 512],
               start=(dc == 0), stop=(dc == 15),
               waits=[wv_dep[dc // 4]] + (xn_ready[tc] if dc == 0 else []))
        vout = vext[:, :].rearrange("p (t k c) -> p t k c", k=4, c=129)[
            :, tc:tc + 1, :, 0:128]
        pv5 = pb[5][:, :].rearrange("p (k d) -> p k d", d=128)[:, None, :, :]
        vext_ready[tc] = bank_read(5, "scalar",
                                   lambda sce, _o=vout, _i=pv5: sce.copy(_o, _i))
        for kvh in range(4):
            mm(6, kvh * 128, (kvh + 1) * 128, ktmb[:, kvh * 128:(kvh + 1) * 128],
               ident[:, :], start=True, stop=(kvh == 3),
               waits=[kf_dep] if kvh == 0 else [])
        kfm_out = kf_fm[:, :].rearrange("p (u s) -> p u s", s=SC)[
            :, b * 4:b * 4 + 4, sc_i * 128:(sc_i + 1) * 128]
        pv6 = pb[6][:, :].rearrange("p (k d) -> p k d", d=128)
        kfm_ready[tc] = bank_read(6, "vector",
                                  lambda v, _o=kfm_out, _i=pv6: v.tensor_copy(_o, _i))
        for kvh in range(4):
            bk = kvh
            if sc_i > 0:
                sn_out = snap[:, ((b * 4 + kvh) * 3 + sc_i - 1) * 129:
                              ((b * 4 + kvh) * 3 + sc_i - 1) * 129 + 129]
                sdep = bank_read(bk, "scalar",
                    lambda sce, _o=sn_out, _b=bk: sce.copy(_o, pb[_b][:, 0:129]))
                snap_deps.append(sdep)
            mm(bk, 0, 129, ktmb[:, kvh * 128:(kvh + 1) * 128],
               vext[:, tc * 516 + kvh * 129:tc * 516 + kvh * 129 + 129],
               start=(sc_i == 0), stop=(sc_i == 3),
               waits=[kf_dep, vext_ready[tc], ones_dep])
        if sc_i == 3:
            ev_deps = []
            for kvh in range(4):
                ev_deps.append(bank_read(kvh, "scalar",
                    lambda sce, _k=kvh: sce.copy(stot[:, _k * 129:(_k + 1) * 129],
                                                 pb[_k][:, 0:129]),
                    waits=[ccin_deps[-1]] if ccin_deps else []))
            ccv = cc_in.ap().rearrange("(u p) c -> p u c", p=128)[:, b * 4:b * 4 + 4, :]
            sv = stot[:, :].rearrange("p (k c) -> p k c", c=129)
            ccin_deps.append(dma("sync", ccv, sv, sST, waits=ev_deps))

            def cc_fn(g, _b=b):
                return g.collective_compute(
                    "AllGather", ALU.bypass, replica_groups=[list(range(NCORES))],
                    ins=[cc_in.ap()[_b * 4 * 128:(_b + 1) * 4 * 128, :].opt()],
                    outs=[cc_outs[_b].ap().opt()])
            cc_deps.append(p.emit("gpsimd", cc_fn, waits=[ccin_deps[-1]], inc=sCC))

    wp_last_use = {"A": (sPE, sPE.n), "B": (sPE, sPE.n)}

    vprev = None
    gsum_deps = [None] * B
    for b in range(B):
        jview = junk[:, 0:4 * 129]
        gsv = gsum[:, b * 4 * 129:(b + 1) * 4 * 129]
        for j in range(NCORES):
            ccv = cc_outs[b].ap().rearrange("(j u p) c -> p j u c", p=128, u=4)[:, j]
            gv = gtmp[:, 0:4 * 129].rearrange("p (u c) -> p u c", c=129)
            ld = dma("sync", gv, ccv, sXL, waits=[cc_deps[b]] + ([vprev] if vprev else []))
            if j == 0:
                vprev = vop(lambda v, _j1=jview: v.tensor_scalar(
                    _j1, gtmp[:, 0:4 * 129], maskc[:, 0:1], None, ALU.mult), waits=[ld])
            else:
                s_, d_ = (jview, gsv) if j % 2 == 1 else (gsv, jview)
                vprev = vop(lambda v, _j=j, _s=s_, _d=d_: v.scalar_tensor_tensor(
                    _d, gtmp[:, 0:4 * 129], maskc[:, _j:_j + 1], _s, ALU.mult, ALU.add),
                    waits=[ld])
        gsum_deps[b] = vprev
    pref_ready = [None] * 16
    for u in range(16):
        svv = snap[:, u * 3 * 129:(u + 1) * 3 * 129].rearrange("p (s c) -> p s c", c=129)
        gvv = gsum[:, u * 129:(u + 1) * 129][:, None, :].broadcast_to([128, 3, 129])
        ovv = prefall[:, (u * 4 + 1) * 129:(u + 1) * 4 * 129].rearrange("p (s c) -> p s c", c=129)
        ws = ([gsum_deps[u // 4]] + snap_deps) if u == 0 else [gsum_deps[u // 4]]
        vop(lambda v, _o=prefall[:, u * 4 * 129:(u * 4 + 1) * 129],
            _g=gsum[:, u * 129:(u + 1) * 129]: v.tensor_copy(_o, _g), waits=ws)
        pref_ready[u] = vop(lambda v, _o=ovv, _s=svv, _g=gvv: v.tensor_tensor(
            _o, _s, _g, ALU.add))
    if debug:
        dma("sync", dbg["snap"], snap[:, :], sST, waits=snap_deps + gsum_deps)
        dma("sync", dbg["prefall"], prefall[:, :], sST, waits=pref_ready)
        dma("sync", dbg["kf_fm"], kf_fm[:, :], sST, waits=p.marks([sV]))
        dma("sync", dbg["vext"], vext[:, :], sST, waits=p.marks([sSc]))
    p.gate(p.marks([sV]))
    es_snap.close()

    es_scan = ExitStack()
    qft = [es_scan.enter_context(nc.sbuf_tensor(f"qft{i}", [128, 512], BF16)) for i in range(2)]
    qffm = es_scan.enter_context(nc.sbuf_tensor("qffm", [128, NTC * 512], BF16))
    mixs2 = [es_scan.enter_context(nc.sbuf_tensor(f"mixs{i}", [128, 4 * 512], BF16))
             for i in range(2)]
    Abb = [es_scan.enter_context(nc.sbuf_tensor(f"Ab{i}", [128, 128], BF16)) for i in range(4)]
    rcpb = es_scan.enter_context(nc.sbuf_tensor("rcpb", [128, 16], F32))

    prev_scan_end = None
    prev_mix_dma = None
    prev_mix_dma2 = [None, None]
    mixb_done = None
    for nb in range(4):
        if nb == 0:
            panel_deps = pre_panel_deps
            wq_buf = wpC
        else:
            wq_buf = wpA
            panel_deps = []
            for q4 in range(4):
                panel_deps.append(dma("gpsimd",
                    wpA[:, :].rearrange("p (dc n) -> p dc n", n=512)[:, q4 * 4:q4 * 4 + 4, :],
                    rows_fm(wqT, 2048)[:, q4 * 4:q4 * 4 + 4, nb * 512:(nb + 1) * 512],
                    sWL, waits=[wp_last_use.get("A")]))
        qffm_ready = [None] * NTC
        last_q_mm = None
        for tc in range(NTC):
            bkq = 4 if tc % 2 == 0 else 5
            for dc in range(16):
                last_q_mm = mm(bkq, 0, 512, xn_chunk(dc, tc), wq_buf[:, dc * 512:(dc + 1) * 512],
                               start=(dc == 0), stop=(dc == 15), waits=[panel_deps[dc // 4]])
            qf_dep = rope_elu(tc % 4, qft[tc % 2][:, :],
                              [prev_scan_end], bkq)
            bkt = 6 if tc % 2 == 0 else 7
            for hh in range(4):
                mm(bkt, hh * 128, (hh + 1) * 128,
                   qft[tc % 2][:, hh * 128:hh * 128 + 128],
                   ident[:, :], start=True, stop=(hh == 3),
                   waits=[qf_dep] if hh == 0 else [])
            qffm_ready[tc] = bank_read(bkt, "scalar",
                lambda sce, _b=bkt, _o=qffm[:, tc * 512:(tc + 1) * 512]:
                sce.copy(_o, pb[_b][:, :]))
        wp_last_use["A"] = (sPE, sPE.n)
        if debug and nb == 0:
            dma("sync", dbg["qf0"], qffm[:, :], sST, waits=[(sSc, sSc.n)])
        it = 0
        mix_deps = []
        prev_md = [None] * 4
        for b in range(B):
            mixs = mixs2[b % 2]
            mix_deps = []
            for hh in range(4):
                for sc_i in range(NSC):
                    tc = b * 4 + sc_i
                    u = b * 4 + nb
                    qv = qffm[:, tc * 512 + hh * 128:tc * 512 + hh * 128 + 128]
                    bka = 4 + it % 4
                    mm(bka, 0, 128, kf_fm[:, u * SC + sc_i * 128:u * SC + (sc_i + 1) * 128],
                       qv, start=True, stop=True,
                       waits=[qffm_ready[tc], kfm_ready[tc]])
                    ad = bank_read(bka, "vector",
                                   lambda v, _o=Abb[it % 4], _b=bka: v.tensor_tensor(
                                       _o[:, :], pb[_b][:, 0:128], triuT[:, :], ALU.mult))
                    bko = it % 4
                    mm(bko, 0, 129, Abb[it % 4][:, :],
                       vext[:, tc * 516 + nb * 129:tc * 516 + nb * 129 + 129],
                       start=True, stop=False, waits=[ad, vext_ready[tc]])
                    mm(bko, 0, 129, qv,
                       prefall[:, (u * 4 + sc_i) * 129:(u * 4 + sc_i) * 129 + 129],
                       start=False, stop=True, waits=[pref_ready[u]])
                    r0, r1 = 2 * (it % 4), 2 * (it % 4) + 1
                    bank_read(bko, "vector",
                              lambda v, _b=bko, _r=r0: v.tensor_scalar(
                                  rcpb[:, _r:_r + 1], pb[_b][:, 128:129], epsb[:, 1:2],
                                  None, ALU.add),
                              waits=[prev_md[it % 4]])
                    p.emit("vector", lambda v: v.drain())
                    rd = vopd(lambda v, _a=r0, _c=r1: v.reciprocal(
                        rcpb[:, _c:_c + 1], rcpb[:, _a:_a + 1]))
                    md = bank_read(bko, "scalar",
                                   lambda sce, _b=bko, _c=r1,
                                   _o=mixs[:, sc_i * 512 + hh * 128:sc_i * 512 + hh * 128 + 128]:
                                   sce.mul(_o, pb[_b][:, 0:128], rcpb[:, _c:_c + 1]),
                                   waits=[rd, prev_mix_dma2[b % 2]])
                    prev_md[it % 4] = md
                    mix_deps.append(md)
                    it += 1
            mv = mixs[:, :].rearrange("p (t c) -> p t c", c=512)
            ov = mixb.rearrange("(t p) c -> p t c", p=128)[
                :, b * 4:b * 4 + 4, nb * 512:(nb + 1) * 512]
            prev_mix_dma2[b % 2] = dma("sync", ov, mv, sST, waits=mix_deps)
        prev_scan_end = (sPE, sPE.n)
        prev_mix_dma = prev_mix_dma2[1]
    mixb_done = (sST, sST.n)
    if debug:
        dma("sync", dbg["mixb"], mixb, sST, waits=[mixb_done])
    p.gate(p.marks(allsems))
    es_scan.close()
    es_attn.close()

    es_ffn = ExitStack()
    fallo = lambda name, shape, dt=BF16: es_ffn.enter_context(nc.sbuf_tensor(name, shape, dt))
    qs4 = fallo("qs4", [128, B * 1024])
    ks4 = fallo("ks4", [128, B * 1024])
    vs4 = fallo("vs4", [128, B * 2048])
    ol4 = fallo("ol4", [128, 2048])
    onrm = fallo("onrm", [128, B * 2048])
    slt = [fallo(f"slt{i}", [128, 2048]) for i in range(3)]
    lgs = fallo("lgs", [128, B * 32], F32)
    sml = junk[:, 0:24 * 16]
    mt4 = junk[:, 24 * 16:24 * 16 + 160]
    dots = junk[:, 24 * 16 + 160:24 * 16 + 320]
    qkc = fallo("qkc", [128, 160])
    sgt = fallo("sgt", [128, 16 * 512])
    ropeF_tab = None
    opg = [fallo(f"opg{i}", [128, 512]) for i in range(2)]
    ft1 = slt[0][:, 0:1024]
    ft2 = slt[2][:, 0:1024]

    def zv(l): return sml[:, (0 + l) * 16:(0 + l) * 16 + 16]
    def pmv(l): return sml[:, (4 + l) * 16:(4 + l) * 16 + 16]
    def cfv(l): return sml[:, (8 + l) * 16:(8 + l) * 16 + 16]
    SM_DIFF, SM_AB, SM_T1, SM_T2, SM_T3, SM_SO, SM_SO2, SM_M, SM_VAR, SM_SD, SM_RS = \
        [(12 + i) * 16 for i in range(11)]
    def sm(off): return sml[:, off:off + 16]

    PAIRS = [(l, m) for l in range(B) for m in range(l + 1)]
    PIDX = {pm_: i for i, pm_ in enumerate(PAIRS)}
    def mtv(pi): return mt4[:, pi * 16:(pi + 1) * 16]
    def dtv(pi): return dots[:, pi * 16:(pi + 1) * 16]
    def qkv(pi): return qkc[:, pi * 16:(pi + 1) * 16]

    c1b = cvws[:, 16:32][:, :, None].broadcast_to([128, 16, 128])
    c0b = cvws[:, 0:16][:, :, None].broadcast_to([128, 16, 128])
    xn3 = xn[:, :].rearrange("p (dc t) -> p dc t", t=T)
    cvt3 = slt[2][:, :].rearrange("p (dc t) -> p dc t", t=128)
    s03 = slt[0][:, :].rearrange("p (dc t) -> p dc t", t=128)
    s13 = slt[1][:, :].rearrange("p (dc t) -> p dc t", t=128)

    prev_opg_dma = [None, None]
    prev_og = None
    panel_rr = [0]
    wp_last_use["C"] = wp_last_use["A"]
    for g in range(4):
        tcs = [l * 4 + g for l in range(B)]
        xc_dep = [None] * (B + 1)
        for l in range(B - 1, -1, -1):
            xl = xn3[:, :, tcs[l] * 128:(tcs[l] + 1) * 128]
            v1 = vop(lambda v, _x=xl: v.tensor_tensor(cvt3, _x, c1b, ALU.mult),
                     waits=[xc_dep[l + 1]])
            if l > 0:
                xp = xn3[:, :, tcs[l - 1] * 128:(tcs[l - 1] + 1) * 128]
                vop(lambda v, _x=xp: v.tensor_tensor(s03, _x, c0b, ALU.mult))
                pre_dep = vop(lambda v: v.tensor_tensor(s13, cvt3, s03, ALU.add))
                src = s13
            else:
                pre_dep = v1
                src = cvt3
            xc_dep[l] = sop(lambda sce, _o=xl, _s=src: sce.activation(_o, _s, AF.Silu),
                            waits=[pre_dep])
        def panels(wd, n_out, npan, evac):
            deps = []
            for pp in range(npan):
                key = ["A", "B", "C"][panel_rr[0] % 3]
                wp = {"A": wpA, "B": wpB, "C": wpC}[key]
                panel_rr[0] += 1
                pdep = []
                for q4 in range(4):
                    pdep.append(dma("gpsimd",
                        wp[:, :].rearrange("p (dc n) -> p dc n", n=512)[:, q4 * 4:q4 * 4 + 4, :],
                        rows_fm(wd, n_out)[:, q4 * 4:q4 * 4 + 4, pp * 512:(pp + 1) * 512],
                        sWL, waits=[wp_last_use.get(key)]))
                for l in range(B):
                    bk = 4 + (l % 2)
                    for dc in range(16):
                        mm(bk, 0, 512, xn_chunk(dc, tcs[l]), wp[:, dc * 512:(dc + 1) * 512],
                           start=(dc == 0), stop=(dc == 15),
                           waits=[pdep[dc // 4], xc_dep[l]])
                    deps.append(evac(pp, l, bk))
                wp_last_use[key] = (sPE, sPE.n)
            return deps

        v_deps = panels(fwv, 2048, 4, lambda pp, l, bk: bank_read(bk, "scalar",
            lambda sce, _o=vs4[:, l * 2048 + pp * 512:l * 2048 + pp * 512 + 512], _b=bk:
            sce.copy(_o, pb[_b][:, :])))
        lg_deps = []
        for l in range(B):
            for dc in range(16):
                mm(6, 0, 32, xn_chunk(dc, tcs[l]), wvr[:, dc * 32:(dc + 1) * 32],
                   start=(dc == 0), stop=(dc == 15), waits=[xc_dep[l]])
            lg_deps.append(bank_read(6, "vector",
                lambda v, _o=lgs[:, l * 32:(l + 1) * 32]: v.tensor_copy(_o, pb[6][:, 0:32])))
            p.emit("vector", lambda v: v.drain())
        q_deps = panels(fwq, 1024, 2, lambda pp, l, bk: bank_read(bk, "scalar",
            lambda sce, _o=qs4[:, l * 1024 + pp * 512:l * 1024 + pp * 512 + 512], _b=bk:
            sce.activation(_o, pb[_b][:, :], AF.Silu)))
        k_deps = panels(fwk, 1024, 2, lambda pp, l, bk: bank_read(bk, "vector",
            lambda v, _o=ks4[:, l * 1024 + pp * 512:l * 1024 + pp * 512 + 512], _b=bk:
            v.tensor_copy(_o, pb[_b][:, :])))
        sg_ready = {}
        gate_deps = {}
        for pp in range(4):
            key = ["A", "B", "C"][panel_rr[0] % 3]
            wp = {"A": wpA, "B": wpB, "C": wpC}[key]
            panel_rr[0] += 1
            pdep = []
            for q4 in range(4):
                pdep.append(dma("gpsimd",
                    wp[:, :].rearrange("p (dc n) -> p dc n", n=512)[:, q4 * 4:q4 * 4 + 4, :],
                    rows_fm(fwg, 2048)[:, q4 * 4:q4 * 4 + 4, pp * 512:(pp + 1) * 512],
                    sWL, waits=[wp_last_use.get(key)]))
            for l in range(B):
                ti = pp * B + l
                bk = ti % 8
                for dc in range(16):
                    mm(bk, 0, 512, xn_chunk(dc, tcs[l]), wp[:, dc * 512:(dc + 1) * 512],
                       start=(dc == 0), stop=(dc == 15), waits=[pdep[dc // 4], xc_dep[l]])
                sg_ready[(pp, l)] = bank_read(bk, "scalar",
                    lambda sce, _b=bk, _o=sgt[:, ti * 512:(ti + 1) * 512]:
                    sce.activation(_o, pb[_b][:, :], AF.Silu))
            wp_last_use[key] = (sPE, sPE.n)
        rope4_deps = [None] * B
        for l in range(B):
            cb1 = cosF[:, l * 64:(l + 1) * 64][:, None, :].broadcast_to([128, 16, 64])
            f1 = ft1[:, :].rearrange("p (h d) -> p h d", d=64)
            f2 = ft2[:, :].rearrange("p (h d) -> p h d", d=64)
            slo = sinFs[:, l * 64:l * 64 + 32][:, None, :].broadcast_to([128, 16, 32])
            shi = sinFs[:, l * 64 + 32:l * 64 + 64][:, None, :].broadcast_to([128, 16, 32])
            for buf, dps in [(qs4, q_deps), (ks4, k_deps)]:
                bv = buf[:, l * 1024:(l + 1) * 1024].rearrange("p (h d) -> p h d", d=64)
                vop(lambda v, _b=bv, _c=cb1:
                    v.tensor_tensor(f1, _b, _c, ALU.mult),
                    waits=list(dps))
                vop(lambda v, _b=bv, _c=slo: v.tensor_tensor(
                    _f2 := f2, _f2[:, :, 0:32] if False else f2[:, :, 0:32],
                    _b[:, :, 32:64], ALU.mult)) if False else None
                vop(lambda v, _b=bv, _c=slo: v.tensor_tensor(
                    f2[:, :, 0:32], _b[:, :, 32:64], _c, ALU.mult))
                vop(lambda v, _b=bv, _c=shi: v.tensor_tensor(
                    f2[:, :, 32:64], _b[:, :, 0:32], _c, ALU.mult))
                rope4_deps[l] = vop(lambda v, _o=buf[:, l * 1024:(l + 1) * 1024]:
                                    v.tensor_tensor(_o, ft1[:, :], ft2[:, :], ALU.add))
        for l in range(B):
            lg = lgs[:, l * 32:(l + 1) * 32].rearrange("p (h r) -> p h r", r=2)
            vopd(lambda v, _l=lg: v.tensor_tensor(sm(SM_DIFF), _l[:, :, 0], _l[:, :, 1],
                                                 ALU.subtract), waits=[lg_deps[l]])
            vopd(lambda v, _z=zv(l): v.tensor_scalar(_z, sm(SM_DIFF), 0.0, None, ALU.is_ge))
            ab_dep = sopd(lambda sce: sce.activation(sm(SM_AB), sm(SM_DIFF), AF.Abs),
                         waits=[(sV, sV.n)])
            pm_dep = sopd(lambda sce, _o=pmv(l): sce.activation(_o, sm(SM_AB), AF.Sigmoid))
            if l == 0:
                vopd(lambda v, _o=cfv(0), _p=pmv(0): v.tensor_scalar(
                    _o, _p, 0.125, None, ALU.mult), waits=[pm_dep])
            else:
                for m_ in range(l):
                    pi = PIDX[(l, m_)]
                    vopd(lambda v, _a=zv(l), _b=zv(m_): v.tensor_tensor(sm(SM_T1), _a, _b, ALU.mult))
                    vopd(lambda v, _a=zv(l), _b=zv(m_): v.tensor_tensor(sm(SM_T2), _a, _b, ALU.add))
                    vopd(lambda v: v.tensor_scalar(sm(SM_T3), sm(SM_T1), 2.0, 1.0, ALU.mult, ALU.add))
                    vopd(lambda v, _o=mtv(pi): v.tensor_tensor(_o, sm(SM_T3), sm(SM_T2), ALU.subtract))
                vopd(lambda v, _m=mtv(PIDX[(l, 0)]): v.tensor_scalar(sm(SM_T1), _m, 1.0, None, ALU.add))
                cur = SM_T1
                for m_ in range(1, l):
                    nxt = SM_T2 if cur == SM_T1 else SM_T1
                    vopd(lambda v, _m=mtv(PIDX[(l, m_)]), _c=cur, _n=nxt:
                        v.tensor_tensor(sm(_n), sm(_c), _m, ALU.add))
                    cur = nxt
                vopd(lambda v, _c=cur: v.reciprocal(sm(SM_T3), sm(_c)))
                vopd(lambda v, _p=pmv(l): v.tensor_tensor(sm(SM_T1), _p, sm(SM_T3), ALU.mult),
                    waits=[pm_dep])
                vopd(lambda v, _o=cfv(l): v.tensor_scalar(_o, sm(SM_T1), 0.125, None, ALU.mult))
        for l in range(B):
            for m_ in range(l + 1):
                pi = PIDX[(l, m_)]
                qv4 = qs4[:, l * 1024:(l + 1) * 1024].rearrange("p (h d) -> p h d", d=64)
                kv4 = ks4[:, m_ * 1024:(m_ + 1) * 1024].rearrange("p (h d) -> p h d", d=64)
                f1 = ft1[:, :].rearrange("p (h d) -> p h d", d=64)
                vop(lambda v, _q=qv4, _k=kv4, _f=f1: v.tensor_tensor(_f, _q, _k, ALU.mult),
                    waits=[rope4_deps[l], rope4_deps[m_]])
                vopd(lambda v, _o=dtv(pi), _f=f1: v.tensor_reduce(_o, _f, AX.X, ALU.add))
                if m_ < l:
                    vopd(lambda v, _d=dtv(pi), _m=mtv(pi): v.tensor_tensor(
                        sm(SM_T1), _d, _m, ALU.mult))
                    vopd(lambda v, _o=qkv(pi), _c=cfv(l): v.tensor_tensor(
                        _o, sm(SM_T1), _c, ALU.mult))
                else:
                    vopd(lambda v, _o=qkv(pi), _d=dtv(pi), _c=cfv(l): v.tensor_tensor(
                        _o, _d, _c, ALU.mult))
        o3 = ol4[:, :].rearrange("p (h d) -> p h d", d=128)
        s0 = slt[0][:, :].rearrange("p (h d) -> p h d", d=128)
        s1 = slt[1][:, :].rearrange("p (h d) -> p h d", d=128)
        s2 = slt[2][:, :].rearrange("p (h d) -> p h d", d=128)
        c3 = onrm[:, 3 * 2048:4 * 2048].rearrange("p (h d) -> p h d", d=128)
        def qkb(pi):
            return qkc[:, pi * 16:(pi + 1) * 16].rearrange("p (h o) -> p h o", o=1) \
                .broadcast_to([128, 16, 128])
        def vv4(m_):
            return vs4[:, m_ * 2048:(m_ + 1) * 2048].rearrange("p (h d) -> p h d", d=128)
        for l in range(B):
            vw = list(v_deps)
            if l == 0:
                odep = vop(lambda v, _v=vv4(0), _q=qkb(PIDX[(0, 0)]):
                           v.tensor_tensor(o3, _v, _q, ALU.mult), waits=vw)
            elif l == 1:
                vop(lambda v, _v=vv4(0), _q=qkb(PIDX[(1, 0)]):
                    v.tensor_tensor(s0, _v, _q, ALU.mult), waits=vw)
                vop(lambda v, _v=vv4(1), _q=qkb(PIDX[(1, 1)]):
                    v.tensor_tensor(s1, _v, _q, ALU.mult))
                odep = vop(lambda v: v.tensor_tensor(o3, s0, s1, ALU.add))
            elif l == 2:
                vop(lambda v, _v=vv4(0), _q=qkb(PIDX[(2, 0)]):
                    v.tensor_tensor(s0, _v, _q, ALU.mult), waits=vw)
                vop(lambda v, _v=vv4(1), _q=qkb(PIDX[(2, 1)]):
                    v.tensor_tensor(s1, _v, _q, ALU.mult))
                vop(lambda v: v.tensor_tensor(s2, s0, s1, ALU.add))
                vop(lambda v, _v=vv4(2), _q=qkb(PIDX[(2, 2)]):
                    v.tensor_tensor(s0, _v, _q, ALU.mult))
                odep = vop(lambda v: v.tensor_tensor(o3, s2, s0, ALU.add))
            else:
                vop(lambda v, _v=vv4(0), _q=qkb(PIDX[(3, 0)]):
                    v.tensor_tensor(s0, _v, _q, ALU.mult), waits=vw)
                vop(lambda v, _v=vv4(1), _q=qkb(PIDX[(3, 1)]):
                    v.tensor_tensor(s1, _v, _q, ALU.mult))
                vop(lambda v: v.tensor_tensor(s2, s0, s1, ALU.add))
                vop(lambda v, _v=vv4(2), _q=qkb(PIDX[(3, 2)]):
                    v.tensor_tensor(s0, _v, _q, ALU.mult))
                vop(lambda v, _v=vv4(3), _q=qkb(PIDX[(3, 3)]):
                    v.tensor_tensor(s1, _v, _q, ALU.mult))
                vop(lambda v: v.tensor_tensor(c3, s0, s1, ALU.add))
                odep = vop(lambda v: v.tensor_tensor(o3, s2, c3, ALU.add))
            vopd(lambda v: v.tensor_reduce(sm(SM_SO), o3, AX.X, ALU.add))
            sq_dep = sop(lambda sce: sce.activation(slt[1][:, :], ol4[:, :], AF.Square),
                         waits=[odep])
            vopd(lambda v, _d=slt[1][:, :].rearrange("p (h d) -> p h d", d=128):
                v.tensor_reduce(sm(SM_SO2), _d, AX.X, ALU.add), waits=[sq_dep])
            vopd(lambda v: v.tensor_scalar(sm(SM_M), sm(SM_SO), 1.0 / 128, None, ALU.mult))
            vopd(lambda v: v.tensor_scalar(sm(SM_T1), sm(SM_SO2), 1.0 / 128, None, ALU.mult))
            vopd(lambda v: v.tensor_tensor(sm(SM_T2), sm(SM_M), sm(SM_M), ALU.mult))
            var_dep = vopd(lambda v: v.tensor_tensor(sm(SM_VAR), sm(SM_T1), sm(SM_T2), ALU.subtract))
            sd_dep = sop(lambda sce: sce.activation(sm(SM_SD), sm(SM_VAR), AF.Sqrt,
                                                    bias=epsb[:, 2:3]), waits=[var_dep])
            vopd(lambda v: v.reciprocal(sm(SM_RS), sm(SM_SD)), waits=[sd_dep])
            mb_ = sm(SM_M)[:, :, None].broadcast_to([128, 16, 128])
            rb_ = sm(SM_RS)[:, :, None].broadcast_to([128, 16, 128])
            vop(lambda v, _m=mb_: v.tensor_tensor(s0, o3, _m, ALU.subtract))
            vop(lambda v, _r=rb_,
                _o=onrm[:, l * 2048:(l + 1) * 2048].rearrange("p (h d) -> p h d", d=128):
                v.tensor_tensor(_o, s0, _r, ALU.mult))
        onrm_mark = (sV, sV.n)
        for pp in range(4):
            for l in range(B):
                ti = pp * B + l
                oi = ti % 2
                og_dep = vop(lambda v, _o=opg[oi],
                             _s=sgt[:, ti * 512:(ti + 1) * 512],
                             _n=onrm[:, l * 2048 + pp * 512:l * 2048 + pp * 512 + 512]:
                             v.tensor_tensor(_o[:, :], _s, _n, ALU.mult),
                             waits=[sg_ready[(pp, l)], onrm_mark, prev_opg_dma[oi]])
                gate_deps[ti] = og_dep
                prev_og = og_dep
                prev_opg_dma[oi] = dma("sync",
                    outpb[tcs[l] * 128:(tcs[l] + 1) * 128, pp * 512:(pp + 1) * 512],
                    opg[oi][:, :], sST, waits=[og_dep])
        if debug and g == 0:
            gm = p.marks(allsems)
            for nmm, buf in [("logits", lgs), ("qs4", qs4), ("ks4", ks4), ("vs4", vs4),
                             ("onrm", onrm), ("dots", dots), ("qkc", qkc)]:
                dma("sync", dbg[nmm], buf[:, :], sST, waits=gm)
    if debug:
        dma("sync", dbg["xc"], xn[:, :], sST, waits=p.marks(allsems))
        dma("sync", dbg["outpb"], outpb, sST, waits=p.marks(allsems))

    p.gate(p.marks(allsems))
    es_ffn.close()
    es_w.close()
    es_xn.close()

    es_s5 = ExitStack()
    s5a = lambda name, shape, dt=BF16: es_s5.enter_context(nc.sbuf_tensor(name, shape, dt))
    w5o = s5a("w5o", [128, 16 * 2048])
    w5f = s5a("w5f", [128, 16 * 2048])
    mtm = s5a("mtm", [128, 2048])
    otm = s5a("otm", [128, 2048])
    mfm = s5a("mfm", [128, 2048])
    ofm = s5a("ofm", [128, 2048])
    xb5 = [s5a(f"xb5{i}", [128, D], F32) for i in range(2)]
    outs = [s5a(f"outs{i}", [128, D], F32) for i in range(2)]

    w5p_deps = []
    for pn in range(4):
        d1 = dma("gpsimd",
                 w5o[:, pn * 8192:(pn + 1) * 8192].rearrange("p (mc n) -> p mc n", n=512),
                 rows_fm(woT, 2048)[:, :, pn * 512:(pn + 1) * 512], sWL)
        d2 = dma("gpsimd",
                 w5f[:, pn * 8192:(pn + 1) * 8192].rearrange("p (mc n) -> p mc n", n=512),
                 rows_fm(fwout, 2048)[:, :, pn * 512:(pn + 1) * 512], sWL)
        w5p_deps.append([d1, d2])

    out_deps = []
    prev_fm = None
    prev_ev = None
    for tc in range(NTC):
        mld = dma("sync", mtm[:, :], mixb[tc * 128:(tc + 1) * 128, :], sST,
                  waits=[mixb_done, prev_fm])
        old = dma("sync", otm[:, :], outpb[tc * 128:(tc + 1) * 128, :], sST,
                  waits=[prev_opg_dma[0], prev_opg_dma[1], prev_fm])
        xld = dma("sync", xb5[tc % 2][:, :], x[tc * 128:(tc + 1) * 128, :], sXL,
                  waits=[prev_ev])
        fm_deps = []
        for src, ldep, dst, bks in [(mtm, mld, mfm, (4, 5)), (otm, old, ofm, (6, 7))]:
            for halfq in range(4):
                bk = bks[halfq % 2]
                for k in range(4):
                    mcc = halfq * 4 + k
                    mm(bk, k * 128, (k + 1) * 128, src[:, mcc * 128:(mcc + 1) * 128],
                       ident[:, :], start=True, stop=(k == 3), waits=[ldep])
                pv = pb[bk][:, :].rearrange("p (k t) -> p k t", t=128)
                dv = dst[:, :].rearrange("p (mc t) -> p mc t", t=128)[
                    :, halfq * 4:halfq * 4 + 4, :]
                fm_deps.append(bank_read(bk, "vector",
                    lambda v, _o=dv, _i=pv: v.tensor_copy(_o, _i)))
        prev_fm = fm_deps[-1]
        ev_deps = []
        for pn in range(4):
            bko = pn % 2
            for mc in range(16):
                mm(bko, 0, 512, mfm[:, mc * 128:(mc + 1) * 128],
                   w5o[:, pn * 8192 + mc * 512:pn * 8192 + mc * 512 + 512],
                   start=(mc == 0), stop=False,
                   waits=(w5p_deps[pn] + fm_deps) if mc == 0 else [])
            for mc in range(16):
                mm(bko, 0, 512, ofm[:, mc * 128:(mc + 1) * 128],
                   w5f[:, pn * 8192 + mc * 512:pn * 8192 + mc * 512 + 512],
                   start=False, stop=(mc == 15))
            ev_deps.append(bank_read(bko, "vector",
                lambda v, _b=bko, _o=outs[tc % 2][:, pn * 512:pn * 512 + 512],
                _x=xb5[tc % 2][:, pn * 512:pn * 512 + 512]:
                v.scalar_tensor_tensor(_o, pb[_b][:, :], 1.0, _x, ALU.mult, ALU.add),
                waits=[xld, out_deps[tc - 2] if tc >= 2 else None]))
        prev_ev = ev_deps[-1]
        out_deps.append(dma("sync", out[tc * 128:(tc + 1) * 128, :], outs[tc % 2][:, :],
                            sST, waits=ev_deps))

    p.emit("sync", None, waits=[out_deps[-1]])

    with nc.Block() as block:
        p.flush(block)

    es_s5.close()
    es.close()
    return nc, dbg



def kernel(**inputs):
    in_maps = _shard_inputs(inputs)
    nc, _ = build_nc(debug=False)
    res = run_bass_kernel_spmd(nc, in_maps, core_ids=list(range(NCORES)))
    outf = np.empty((B, S, D), np.float32)
    for c in range(NCORES):
        outf[:, c * SC:(c + 1) * SC, :] = np.asarray(res.results[c]["out"]).reshape(B, SC, D)
    return outf


# version 05
# speedup vs baseline: 1.0361x; 1.0361x over previous
import numpy as np
import ml_dtypes
from contextlib import ExitStack
import concourse.bass as bass
import concourse.mybir as mybir
from concourse.bass_utils import run_bass_kernel_spmd

F32 = mybir.dt.float32
BF16 = mybir.dt.bfloat16
AF = mybir.ActivationFunctionType
ALU = mybir.AluOpType
AX = mybir.AxisListType
I32 = mybir.dt.int32

NCORES = 8
B = 4
S = 4096
D = 2048
SC = S // NCORES
T = B * SC
NTC = T // 128
NSC = SC // 128
EPS_LN = 1e-5
EPS_ATTN = 1e-6
EPS_FFN = 1e-5



def _host_tables(core_id):
    t = {}
    t["ident"] = np.eye(128, dtype=ml_dtypes.bfloat16)
    t["triuT"] = np.triu(np.ones((128, 128), np.float32)).astype(ml_dtypes.bfloat16)
    invA = 10000.0 ** (-np.arange(64, dtype=np.float64) / 64.0)
    cosA = np.zeros((128, NSC * 128), np.float64)
    sinAs = np.zeros((128, NSC * 128), np.float64)
    for sc in range(NSC):
        s_glob = core_id * SC + sc * 128 + np.arange(128)
        ang = s_glob[:, None] * invA[None, :]
        c, sn = np.cos(ang), np.sin(ang)
        cosA[:, sc * 128:sc * 128 + 64] = c
        cosA[:, sc * 128 + 64:sc * 128 + 128] = c
        sinAs[:, sc * 128:sc * 128 + 64] = -sn
        sinAs[:, sc * 128 + 64:sc * 128 + 128] = sn
    t["cosA"] = cosA.astype(ml_dtypes.bfloat16)
    t["sinAs"] = sinAs.astype(ml_dtypes.bfloat16)
    invF = 10000.0 ** (-np.arange(32, dtype=np.float64) / 32.0)
    cosF = np.zeros((128, B * 64), np.float64)
    sinFs = np.zeros((128, B * 64), np.float64)
    for l in range(B):
        ang = l * invF
        c, sn = np.cos(ang), np.sin(ang)
        cosF[:, l * 64:l * 64 + 32] = c[None, :]
        cosF[:, l * 64 + 32:l * 64 + 64] = c[None, :]
        sinFs[:, l * 64:l * 64 + 32] = -sn[None, :]
        sinFs[:, l * 64 + 32:l * 64 + 64] = sn[None, :]
    t["cosF"] = cosF.astype(ml_dtypes.bfloat16)
    t["sinFs"] = sinFs.astype(ml_dtypes.bfloat16)
    cb = np.zeros((128, B * 1024), np.float64)
    sb = np.zeros((128, B * 1024), np.float64)
    for l in range(B):
        for h in range(16):
            cb[:, l * 1024 + h * 64:l * 1024 + h * 64 + 64] = cosF[:, l * 64:(l + 1) * 64]
            sb[:, l * 1024 + h * 64:l * 1024 + h * 64 + 64] = sinFs[:, l * 64:(l + 1) * 64]
    t["cosFb"] = cb.astype(ml_dtypes.bfloat16)
    t["sinFb"] = sb.astype(ml_dtypes.bfloat16)
    m = np.zeros((128, NCORES), np.float32)
    m[:, :core_id] = 1.0
    t["maskc"] = m
    e = np.zeros((128, 4), np.float32)
    e[:, 0] = EPS_LN
    e[:, 1] = EPS_ATTN
    e[:, 2] = EPS_FFN
    t["epsb"] = e
    return t


def _shard_inputs(inputs):
    x = np.asarray(inputs["L0_attn_x"], np.float32)
    c32 = lambda a: np.ascontiguousarray(np.asarray(a, np.float32))
    wqT = c32(np.asarray(inputs["L0_attn_Wq"], np.float32).T)
    wkT = c32(np.asarray(inputs["L0_attn_Wk"], np.float32).T)
    wvT = c32(np.asarray(inputs["L0_attn_Wv"], np.float32).T)
    woT = c32(np.asarray(inputs["L0_attn_Wo"], np.float32).T)
    fwq = c32(inputs["L0_ffn_Wq"])
    fwk = c32(inputs["L0_ffn_Wk"])
    fwv = c32(inputs["L0_ffn_Wv"])
    fwvT = c32(np.asarray(inputs["L0_ffn_Wv"], np.float32).T)
    fwg = c32(inputs["L0_ffn_Wg"])
    fwout = c32(inputs["L0_ffn_Wout"])
    cw = np.asarray(inputs["L0_ffn_conv_w"], np.float32)
    cvw = np.zeros((128, 32), np.float32)
    cvw[:, 0:16] = cw[:, 0].reshape(16, 128).T
    cvw[:, 16:32] = cw[:, 1].reshape(16, 128).T
    rw = np.asarray(inputs["L0_ffn_router_w"], np.float32)
    rwfm = np.zeros((128, 32), np.float32)
    for h in range(16):
        rwfm[:, 2 * h:2 * h + 2] = rw[h]
    in_maps = []
    for c in range(NCORES):
        xs = np.ascontiguousarray(x[:, c * SC:(c + 1) * SC, :].reshape(T, D))
        m = {"x": xs, "wqT": wqT, "wkT": wkT, "wvT": wvT, "woT": woT,
             "fwq": fwq, "fwk": fwk, "fwv": fwv, "fwvT": fwvT, "fwg": fwg,
             "fwout": fwout, "cvw": cvw, "rwfm": rwfm}
        ht = _host_tables(c)
        m.update({("d_" + k if k in ("ident", "triuT", "cosA", "sinAs", "cosF", "sinFs",
                                     "cosFb", "sinFb") else k): v
                  for k, v in ht.items()})
        in_maps.append(m)
    return in_maps



ENGINES = ("tensor", "vector", "scalar", "gpsimd", "sync")


class Sem:
    def __init__(self, handle, owner=None):
        self.h = handle
        self.n = 0
        self.owner = owner


class Prog:
    def __init__(self):
        self.ops = {e: [] for e in ENGINES}
        self.maxwait = {e: {} for e in ENGINES}

    def emit(self, eng, fn, waits=(), inc=None, inck=1):
        ws = []
        for w in waits:
            if w is None:
                continue
            sem, cnt = w
            if cnt <= 0 or sem.owner == eng:
                continue
            prev = self.maxwait[eng].get(id(sem), -1)
            if cnt > prev:
                self.maxwait[eng][id(sem)] = cnt
                ws.append((sem, cnt))
        dep = None
        if inc is not None:
            inc.n += inck
            dep = (inc, inc.n)
        self.ops[eng].append((fn, ws, inc, inck))
        return dep

    def marks(self, sems):
        return [(s, s.n) for s in sems]

    def gate(self, marks):
        for e in ENGINES:
            self.emit(e, None, waits=marks)

    def flush(self, block):
        for e in ENGINES:
            ops = self.ops[e]
            def body(eng, _ops=ops):
                for fn, ws, inc, inck in _ops:
                    for sem, cnt in ws:
                        eng.wait_ge(sem.h, cnt)
                    if fn is None:
                        continue
                    ins = fn(eng)
                    if inc is not None:
                        ins.then_inc(inc.h, inck)
            getattr(block, e)(body)


def rows_fm(w_dram, ncols):
    return w_dram.rearrange("(dc p) n -> p dc n", p=128)[:, :, 0:ncols]



def build_nc(debug=False):
    nc = bass.Bass("TRN2", target_bir_lowering=False)

    def din(name, shape, dt=F32):
        return nc.dram_tensor(name, shape, dt, kind="ExternalInput").ap()

    x = din("x", [T, D])
    wqT = din("wqT", [D, 2048]); wkT = din("wkT", [D, 512]); wvT = din("wvT", [D, 512])
    woT = din("woT", [2048, D])
    fwq = din("fwq", [D, 1024]); fwk = din("fwk", [D, 1024]); fwv = din("fwv", [D, 2048])
    fwvT = din("fwvT", [2048, D]); fwg = din("fwg", [D, 2048]); fwout = din("fwout", [2048, D])
    cvw_d = din("cvw", [128, 32]); rwfm_d = din("rwfm", [128, 32])
    identd = din("d_ident", [128, 128], BF16); triuTd = din("d_triuT", [128, 128], BF16)
    cosAd = din("d_cosA", [128, NSC * 128], BF16); sinAsd = din("d_sinAs", [128, NSC * 128], BF16)
    cosFd = din("d_cosF", [128, B * 64], BF16); sinFsd = din("d_sinFs", [128, B * 64], BF16)
    cosFbd = din("d_cosFb", [128, B * 1024], BF16); sinFbd = din("d_sinFb", [128, B * 1024], BF16)
    maskcd = din("maskc", [128, NCORES]); epsbd = din("epsb", [128, 4])

    out = nc.dram_tensor("out", [T, D], F32, kind="ExternalOutput").ap()

    cc_in = nc.dram_tensor("cc_in", [16 * 128, 129], F32)
    cc_outs = [nc.dram_tensor(f"cc_out{b}", [NCORES * 4 * 128, 129], F32) for b in range(B)]
    mixb = nc.dram_tensor("mixb", [T, 2048], BF16).ap()
    outpb = nc.dram_tensor("outpb", [T, 2048], BF16).ap()

    dbg = {}
    def ddbg(name, shape, dt):
        if debug:
            dbg[name] = nc.dram_tensor("dbg_" + name, shape, dt, kind="ExternalOutput").ap()
    ddbg("xn", [128, 16 * T], BF16)
    ddbg("kf_fm", [128, B * 4 * SC], BF16)
    ddbg("vext", [128, NTC * 516], BF16)
    ddbg("snap", [128, 16 * 3 * 129], BF16)
    ddbg("prefall", [128, 16 * 4 * 129], BF16)
    ddbg("qf0", [128, NTC * 512], BF16)
    ddbg("wvr", [128, 512], BF16)
    ddbg("logits", [128, B * 32], F32)
    ddbg("xc", [128, 16 * T], BF16)
    ddbg("qs4", [128, B * 1024], BF16)
    ddbg("ks4", [128, B * 1024], BF16)
    ddbg("vs4", [128, B * 2048], BF16)
    ddbg("onrm", [128, B * 2048], BF16)
    ddbg("dots", [128, 10 * 16], F32)
    ddbg("qkc", [128, 10 * 16], BF16)
    ddbg("mixb", [T, 2048], BF16)
    ddbg("outpb", [T, 2048], BF16)

    es = ExitStack()
    alloc = lambda name, shape, dt=BF16: es.enter_context(nc.sbuf_tensor(name, shape, dt))

    ident = alloc("ident", [128, 128])
    triuT = alloc("triuT", [128, 128])
    cosA = alloc("cosA", [128, NSC * 128])
    sinAs = alloc("sinAs", [128, NSC * 128])
    cosF = alloc("cosF", [128, B * 64])
    sinFs = alloc("sinFs", [128, B * 64])
    cvws = alloc("cvws", [128, 32], F32)
    rwfm = alloc("rwfm_s", [128, 32])
    maskc = alloc("maskc_s", [128, NCORES], F32)
    epsb = alloc("epsb_s", [128, 4], F32)
    wvr = alloc("wvr", [128, 512])
    junk = alloc("junk", [128, D + 16], F32)
    stat = alloc("stat", [128, 16], F32)

    pb = [es.enter_context(nc.psum_tensor(f"pb{i}", [128, 512], F32)) for i in range(8)]
    bank_w = [None] * 8
    bank_r = [[] for _ in range(8)]

    sems = {}
    for nm, owner in [("sWL", None), ("sXL", None), ("sST", None), ("sPE", "tensor"),
                      ("sV", "vector"), ("sSc", "scalar"), ("sG", "gpsimd"), ("sCC", None)]:
        sems[nm] = Sem(es.enter_context(nc.semaphore(nm)), owner)
    sWL, sXL, sST, sPE, sV, sSc, sG, sCC = (sems[n] for n in
        ["sWL", "sXL", "sST", "sPE", "sV", "sSc", "sG", "sCC"])
    allsems = [sWL, sXL, sST, sPE, sV, sSc, sCC]

    p = Prog()

    def mm(bank_i, c0, c1, lhsT, rhs, start, stop, waits=()):
        ws = list(waits) + bank_r[bank_i]
        def fn(t, _o=pb[bank_i][:, c0:c1], _l=lhsT, _r=rhs, _s=start, _p=stop):
            return t.matmul(_o, _l, _r, start=_s, stop=_p, skip_group_check=True)
        dep = p.emit("tensor", fn, waits=ws, inc=sPE)
        bank_r[bank_i] = []
        bank_w[bank_i] = dep
        return dep

    def bank_read(bank_i, eng, fn, waits=()):
        ws = list(waits)
        if bank_w[bank_i] is not None:
            ws.append(bank_w[bank_i])
        inc = {"vector": sV, "scalar": sSc, "gpsimd": sG}[eng]
        dep = p.emit(eng, fn, waits=ws, inc=inc)
        bank_r[bank_i].append(dep)
        return dep

    def dma(eng, out_ap, in_ap, sem, waits=()):
        def fn(e, _o=out_ap, _i=in_ap):
            return e.dma_start(out=_o, in_=_i)
        return p.emit(eng, fn, waits=waits, inc=sem, inck=16)

    def vop(fn, waits=()):
        return p.emit("vector", fn, waits=waits, inc=sV)

    def sop(fn, waits=()):
        return p.emit("scalar", fn, waits=waits, inc=sSc)

    def vopd(fn, waits=()):
        d = p.emit("vector", fn, waits=waits, inc=sV)
        p.emit("vector", lambda v: v.drain())
        return d

    def sopd(fn, waits=()):
        d = p.emit("scalar", fn, waits=waits, inc=sSc)
        p.emit("scalar", lambda s: s.drain())
        return d

    for dst, src in [(ident, identd), (triuT, triuTd), (cosA, cosAd),
                     (sinAs, sinAsd), (cosF, cosFd), (sinFs, sinFsd),
                     (cvws, cvw_d), (maskc, maskcd), (epsb, epsbd)]:
        dma("sync", dst[:, :], src, sXL)
    rw_dep = dma("gpsimd", rwfm[:, :], rwfm_d, sWL)
    p.gate(p.marks([sXL]) + [rw_dep])

    es_xn = ExitStack()
    xn = es_xn.enter_context(nc.sbuf_tensor("xn", [128, 16 * T], BF16))
    es_w = ExitStack()
    wpA = es_w.enter_context(nc.sbuf_tensor("wpA", [128, 16 * 512], BF16))
    wpB = es_w.enter_context(nc.sbuf_tensor("wpB", [128, 16 * 512], BF16))
    wpC = es_w.enter_context(nc.sbuf_tensor("wpC", [128, 16 * 512], BF16))

    wvr_mm_deps = []
    for h in range(16):
        buf = wpB[:, (h % 2) * 4096:(h % 2) * 4096 + 2048]
        ws = [rw_dep] + ([wvr_mm_deps[h - 2]] if h >= 2 else [])
        ld = dma("gpsimd", buf, fwvT[h * 128:(h + 1) * 128, :], sWL, waits=ws)
        dep = None
        for dc in range(16):
            dep = mm(7, dc * 32 + h * 2, dc * 32 + h * 2 + 2,
                     buf[:, dc * 128:(dc + 1) * 128], rwfm[:, 2 * h:2 * h + 2],
                     start=True, stop=True, waits=[ld])
        wvr_mm_deps.append(dep)
    wvr_dep = bank_read(7, "vector", lambda v: v.tensor_copy(wvr[:, :], pb[7][:, :]))
    if debug:
        dma("sync", dbg["wvr"], wvr[:, :], sST, waits=[wvr_dep])

    def xn_chunk(dc, tc):
        return xn[:, dc * T + tc * 128: dc * T + (tc + 1) * 128]

    es_s1 = ExitStack()
    xb = [es_s1.enter_context(nc.sbuf_tensor(f"xb{i}", [128, D], F32)) for i in range(2)]
    xnb = es_s1.enter_context(nc.sbuf_tensor("xnb", [128, D], BF16))
    xn_ready = [[] for _ in range(NTC)]
    x_cons = [None] * NTC
    xnb_free = None
    for tc in range(NTC):
        ws = [x_cons[tc - 2]] if tc >= 2 else []
        xld = dma("sync", xb[tc % 2][:, :], x[tc * 128:(tc + 1) * 128, :], sXL, waits=ws)
        sop(lambda sce, _b=xb[tc % 2]: sce.activation(junk[:, 0:D], _b[:, :], AF.Identity,
                                                      accum_out=stat[:, 0:1]),
            waits=[xld, x_cons[tc - 1] if tc >= 1 else None])
        st_dep = sop(lambda sce, _b=xb[tc % 2]: sce.activation(junk[:, 0:D], _b[:, :], AF.Square,
                                                               accum_out=stat[:, 1:2]))
        vopd(lambda v: v.tensor_scalar(stat[:, 2:3], stat[:, 0:1], 1.0 / D, None, ALU.mult),
            waits=[st_dep])
        vopd(lambda v: v.tensor_scalar(stat[:, 3:4], stat[:, 1:2], 1.0 / D, None, ALU.mult))
        vopd(lambda v: v.tensor_tensor(stat[:, 4:5], stat[:, 2:3], stat[:, 2:3], ALU.mult))
        vopd(lambda v: v.tensor_tensor(stat[:, 5:6], stat[:, 3:4], stat[:, 4:5], ALU.subtract))
        vopd(lambda v: v.tensor_scalar(stat[:, 6:7], stat[:, 5:6], epsb[:, 0:1], None, ALU.add))
        sti = stat[:, :].bitcast(I32)
        vopd(lambda v: v.tensor_scalar(sti[:, 8:9], sti[:, 6:7], 1, None, ALU.arith_shift_right))
        vopd(lambda v: v.tensor_scalar(sti[:, 9:10], sti[:, 8:9], 0x5f3759df, None, ALU.subtract))
        vopd(lambda v: v.tensor_scalar(sti[:, 10:11], sti[:, 9:10], -1, None, ALU.mult))
        vopd(lambda v: v.tensor_tensor(stat[:, 8:9], stat[:, 10:11], stat[:, 10:11], ALU.mult))
        vopd(lambda v: v.tensor_tensor(stat[:, 9:10], stat[:, 6:7], stat[:, 8:9], ALU.mult))
        vopd(lambda v: v.tensor_scalar(stat[:, 8:9], stat[:, 9:10], -0.5, 1.5, ALU.mult, ALU.add))
        vopd(lambda v: v.tensor_tensor(stat[:, 9:10], stat[:, 10:11], stat[:, 8:9], ALU.mult))
        vopd(lambda v: v.tensor_tensor(stat[:, 8:9], stat[:, 9:10], stat[:, 9:10], ALU.mult))
        vopd(lambda v: v.tensor_tensor(stat[:, 10:11], stat[:, 6:7], stat[:, 8:9], ALU.mult))
        vopd(lambda v: v.tensor_scalar(stat[:, 8:9], stat[:, 10:11], -0.5, 1.5, ALU.mult, ALU.add))
        vopd(lambda v: v.tensor_tensor(stat[:, 7:8], stat[:, 9:10], stat[:, 8:9], ALU.mult))
        nrm_dep = vop(lambda v, _b=xb[tc % 2]: v.tensor_scalar(
            xnb[:, :], _b[:, :], stat[:, 2:3], stat[:, 7:8], ALU.subtract, ALU.mult),
            waits=[xnb_free])
        x_cons[tc] = nrm_dep
        tr_dep = None
        for grp in range(4):
            bk = grp
            for k in range(4):
                dcc = grp * 4 + k
                tr_dep = mm(bk, k * 128, (k + 1) * 128,
                            xnb[:, dcc * 128:(dcc + 1) * 128], ident[:, :],
                            start=True, stop=(k == 3), waits=[nrm_dep])
            outv = xn[:, :].rearrange("p (dc t) -> p dc t", t=T)[
                :, grp * 4:grp * 4 + 4, tc * 128:(tc + 1) * 128]
            pv = pb[bk][:, :].rearrange("p (k t) -> p k t", t=128)
            if grp % 2 == 0:
                ed = bank_read(bk, "vector", lambda v, _o=outv, _i=pv: v.tensor_copy(_o, _i))
            else:
                ed = bank_read(bk, "scalar", lambda sce, _o=outv, _i=pv: sce.copy(_o, _i))
            xn_ready[tc].append(ed)
        xnb_free = tr_dep
    if debug:
        gm = p.marks(allsems)
        dma("sync", dbg["xn"], xn[:, :], sST, waits=gm)
    es_s1.close()

    es_attn = ExitStack()
    aallo = lambda name, shape, dt=BF16: es_attn.enter_context(nc.sbuf_tensor(name, shape, dt))
    kf_fm = aallo("kf_fm", [128, B * 4 * SC])
    vext = aallo("vext", [128, NTC * 516])
    prefall = aallo("prefall", [128, 16 * 4 * 129])
    t1b = aallo("t1b", [128, 512])
    t2b = aallo("t2b", [128, 512])
    rawb = aallo("rawb", [128, 1024])
    mnb = aallo("mnb", [128, 512])
    exb = aallo("exb", [128, 512])
    ktmb = aallo("ktmb", [128, 512])

    es_snap = ExitStack()
    snap = es_snap.enter_context(nc.sbuf_tensor("snap", [128, 16 * 3 * 129], BF16))
    gsum = es_snap.enter_context(nc.sbuf_tensor("gsum", [128, 16 * 129], F32))
    gtmp = es_snap.enter_context(nc.sbuf_tensor("gtmp", [128, 4 * 129], F32))
    stot = es_snap.enter_context(nc.sbuf_tensor("stot", [128, 4 * 129], F32))

    ones_view = vext[:, :].rearrange("p (t k c) -> p t k c", k=4, c=129)[:, :, :, 128:129]
    ones_dep = vop(lambda v: v.memset(ones_view, 1.0))
    snap_deps = []

    wk_dep = []
    for q4 in range(4):
        wk_dep.append(dma("gpsimd",
            wpA[:, :].rearrange("p (dc n) -> p dc n", n=512)[:, q4 * 4:q4 * 4 + 4, :],
            rows_fm(wkT, 512)[:, q4 * 4:q4 * 4 + 4, :], sWL,
            waits=[wvr_mm_deps[14], wvr_mm_deps[15]]))
    wv_dep = []
    for q4 in range(4):
        wv_dep.append(dma("gpsimd",
            wpB[:, :].rearrange("p (dc n) -> p dc n", n=512)[:, q4 * 4:q4 * 4 + 4, :],
            rows_fm(wvT, 512)[:, q4 * 4:q4 * 4 + 4, :], sWL,
            waits=[wvr_mm_deps[14], wvr_mm_deps[15], wvr_dep]))

    pre_panel_deps = []
    for q4 in range(4):
        pre_panel_deps.append(dma("gpsimd",
            wpC[:, :].rearrange("p (dc n) -> p dc n", n=512)[:, q4 * 4:q4 * 4 + 4, :],
            rows_fm(wqT, 2048)[:, q4 * 4:q4 * 4 + 4, 0:512], sWL))

    rope_free = [None]

    def rope_elu(sc_i, out_ap, raw_waits, bank_i):
        W = 512
        raw_dep = bank_read(bank_i, "scalar",
                            lambda sce, _b=bank_i: sce.copy(rawb[:, 0:512], pb[_b][:, :]),
                            waits=list(raw_waits) + [rope_free[0]])
        rv = rawb[:, 0:512].rearrange("p (u d) -> p u d", d=128)
        cosb = cosA[:, sc_i * 128:(sc_i + 1) * 128][:, None, :].broadcast_to([128, 4, 128])
        sin_lo = sinAs[:, sc_i * 128:sc_i * 128 + 64][:, None, :].broadcast_to([128, 4, 64])
        sin_hi = sinAs[:, sc_i * 128 + 64:sc_i * 128 + 128][:, None, :].broadcast_to([128, 4, 64])
        t1v = t1b[:, 0:512].rearrange("p (u d) -> p u d", d=128)
        t2v = t2b[:, 0:512].rearrange("p (u d) -> p u d", d=128)
        vop(lambda v, _a=t1v, _b=rv, _c=cosb: v.tensor_tensor(_a, _b, _c, ALU.mult),
            waits=[raw_dep])
        vop(lambda v, _a=t2v, _b=rv, _c=sin_lo: v.tensor_tensor(
            _a[:, :, 0:64], _b[:, :, 64:128], _c, ALU.mult))
        vop(lambda v, _a=t2v, _b=rv, _c=sin_hi: v.tensor_tensor(
            _a[:, :, 64:128], _b[:, :, 0:64], _c, ALU.mult))
        vop(lambda v: v.tensor_tensor(mnb[:, 0:512], t1b[:, 0:512], t2b[:, 0:512], ALU.add))
        vop(lambda v: v.tensor_scalar(rawb[:, 512:1024], mnb[:, 0:512], 0.0, None, ALU.min))
        mn_mark = (sV, sV.n)
        ex_dep = sop(lambda sce: sce.activation(exb[:, 0:512], rawb[:, 512:1024], AF.Exp),
                     waits=[mn_mark])
        out_dep = vop(lambda v, _o=out_ap: v.scalar_tensor_tensor(
            _o, mnb[:, 0:512], 0.0, exb[:, 0:512], ALU.max, ALU.add),
            waits=[ex_dep, (sPE, sPE.n)])
        rope_free[0] = out_dep
        return out_dep

    kfm_ready = [None] * NTC
    vext_ready = [None] * NTC
    ccin_deps = []
    cc_deps = []
    for tc in range(NTC):
        b, sc_i = tc // 4, tc % 4
        for dc in range(16):
            mm(4, 0, 512, xn_chunk(dc, tc), wpA[:, dc * 512:(dc + 1) * 512],
               start=(dc == 0), stop=(dc == 15),
               waits=[wk_dep[dc // 4]] + (xn_ready[tc] if dc == 0 else []))
        kf_dep = rope_elu(sc_i, ktmb[:, 0:512], [], 4)
        for dc in range(16):
            mm(5, 0, 512, xn_chunk(dc, tc), wpB[:, dc * 512:(dc + 1) * 512],
               start=(dc == 0), stop=(dc == 15),
               waits=[wv_dep[dc // 4]] + (xn_ready[tc] if dc == 0 else []))
        vout = vext[:, :].rearrange("p (t k c) -> p t k c", k=4, c=129)[
            :, tc:tc + 1, :, 0:128]
        pv5 = pb[5][:, :].rearrange("p (k d) -> p k d", d=128)[:, None, :, :]
        vext_ready[tc] = bank_read(5, "scalar",
                                   lambda sce, _o=vout, _i=pv5: sce.copy(_o, _i))
        for kvh in range(4):
            mm(6, kvh * 128, (kvh + 1) * 128, ktmb[:, kvh * 128:(kvh + 1) * 128],
               ident[:, :], start=True, stop=(kvh == 3),
               waits=[kf_dep] if kvh == 0 else [])
        kfm_out = kf_fm[:, :].rearrange("p (u s) -> p u s", s=SC)[
            :, b * 4:b * 4 + 4, sc_i * 128:(sc_i + 1) * 128]
        pv6 = pb[6][:, :].rearrange("p (k d) -> p k d", d=128)
        kfm_ready[tc] = bank_read(6, "vector",
                                  lambda v, _o=kfm_out, _i=pv6: v.tensor_copy(_o, _i))
        for kvh in range(4):
            bk = kvh
            if sc_i > 0:
                sn_out = snap[:, ((b * 4 + kvh) * 3 + sc_i - 1) * 129:
                              ((b * 4 + kvh) * 3 + sc_i - 1) * 129 + 129]
                sdep = bank_read(bk, "scalar",
                    lambda sce, _o=sn_out, _b=bk: sce.copy(_o, pb[_b][:, 0:129]))
                snap_deps.append(sdep)
            mm(bk, 0, 129, ktmb[:, kvh * 128:(kvh + 1) * 128],
               vext[:, tc * 516 + kvh * 129:tc * 516 + kvh * 129 + 129],
               start=(sc_i == 0), stop=(sc_i == 3),
               waits=[kf_dep, vext_ready[tc], ones_dep])
        if sc_i == 3:
            ev_deps = []
            for kvh in range(4):
                ev_deps.append(bank_read(kvh, "scalar",
                    lambda sce, _k=kvh: sce.copy(stot[:, _k * 129:(_k + 1) * 129],
                                                 pb[_k][:, 0:129]),
                    waits=[ccin_deps[-1]] if ccin_deps else []))
            ccv = cc_in.ap().rearrange("(u p) c -> p u c", p=128)[:, b * 4:b * 4 + 4, :]
            sv = stot[:, :].rearrange("p (k c) -> p k c", c=129)
            ccin_deps.append(dma("sync", ccv, sv, sST, waits=ev_deps))

            def cc_fn(g, _b=b):
                return g.collective_compute(
                    "AllGather", ALU.bypass, replica_groups=[list(range(NCORES))],
                    ins=[cc_in.ap()[_b * 4 * 128:(_b + 1) * 4 * 128, :].opt()],
                    outs=[cc_outs[_b].ap().opt()])
            cc_deps.append(p.emit("gpsimd", cc_fn, waits=[ccin_deps[-1]], inc=sCC))

    wp_last_use = {"A": (sPE, sPE.n), "B": (sPE, sPE.n)}

    vprev = None
    gsum_deps = [None] * B
    for b in range(B):
        jview = junk[:, 0:4 * 129]
        gsv = gsum[:, b * 4 * 129:(b + 1) * 4 * 129]
        for j in range(NCORES):
            ccv = cc_outs[b].ap().rearrange("(j u p) c -> p j u c", p=128, u=4)[:, j]
            gv = gtmp[:, 0:4 * 129].rearrange("p (u c) -> p u c", c=129)
            ld = dma("sync", gv, ccv, sXL, waits=[cc_deps[b]] + ([vprev] if vprev else []))
            if j == 0:
                vprev = vop(lambda v, _j1=jview: v.tensor_scalar(
                    _j1, gtmp[:, 0:4 * 129], maskc[:, 0:1], None, ALU.mult), waits=[ld])
            else:
                s_, d_ = (jview, gsv) if j % 2 == 1 else (gsv, jview)
                vprev = vop(lambda v, _j=j, _s=s_, _d=d_: v.scalar_tensor_tensor(
                    _d, gtmp[:, 0:4 * 129], maskc[:, _j:_j + 1], _s, ALU.mult, ALU.add),
                    waits=[ld])
        gsum_deps[b] = vprev
    pref_ready = [None] * 16
    for u in range(16):
        svv = snap[:, u * 3 * 129:(u + 1) * 3 * 129].rearrange("p (s c) -> p s c", c=129)
        gvv = gsum[:, u * 129:(u + 1) * 129][:, None, :].broadcast_to([128, 3, 129])
        ovv = prefall[:, (u * 4 + 1) * 129:(u + 1) * 4 * 129].rearrange("p (s c) -> p s c", c=129)
        ws = ([gsum_deps[u // 4]] + snap_deps) if u == 0 else [gsum_deps[u // 4]]
        vop(lambda v, _o=prefall[:, u * 4 * 129:(u * 4 + 1) * 129],
            _g=gsum[:, u * 129:(u + 1) * 129]: v.tensor_copy(_o, _g), waits=ws)
        pref_ready[u] = vop(lambda v, _o=ovv, _s=svv, _g=gvv: v.tensor_tensor(
            _o, _s, _g, ALU.add))
    if debug:
        dma("sync", dbg["snap"], snap[:, :], sST, waits=snap_deps + gsum_deps)
        dma("sync", dbg["prefall"], prefall[:, :], sST, waits=pref_ready)
        dma("sync", dbg["kf_fm"], kf_fm[:, :], sST, waits=p.marks([sV]))
        dma("sync", dbg["vext"], vext[:, :], sST, waits=p.marks([sSc]))
    p.gate(p.marks([sV]))
    es_snap.close()

    es_scan = ExitStack()
    qft = [es_scan.enter_context(nc.sbuf_tensor(f"qft{i}", [128, 512], BF16)) for i in range(2)]
    qffm = es_scan.enter_context(nc.sbuf_tensor("qffm", [128, NTC * 512], BF16))
    mixs2 = [es_scan.enter_context(nc.sbuf_tensor(f"mixs{i}", [128, 4 * 512], BF16))
             for i in range(2)]
    Abb = [es_scan.enter_context(nc.sbuf_tensor(f"Ab{i}", [128, 128], BF16)) for i in range(4)]
    rcpb = es_scan.enter_context(nc.sbuf_tensor("rcpb", [128, 16], F32))

    prev_scan_end = None
    prev_mix_dma = None
    prev_mix_dma2 = [None, None]
    mixb_done = None
    for nb in range(4):
        if nb == 0:
            panel_deps = pre_panel_deps
            wq_buf = wpC
        else:
            wq_buf = wpA
            panel_deps = []
            for q4 in range(4):
                panel_deps.append(dma("gpsimd",
                    wpA[:, :].rearrange("p (dc n) -> p dc n", n=512)[:, q4 * 4:q4 * 4 + 4, :],
                    rows_fm(wqT, 2048)[:, q4 * 4:q4 * 4 + 4, nb * 512:(nb + 1) * 512],
                    sWL, waits=[wp_last_use.get("A")]))
        qffm_ready = [None] * NTC
        last_q_mm = None
        for tc in range(NTC):
            bkq = 4 if tc % 2 == 0 else 5
            for dc in range(16):
                last_q_mm = mm(bkq, 0, 512, xn_chunk(dc, tc), wq_buf[:, dc * 512:(dc + 1) * 512],
                               start=(dc == 0), stop=(dc == 15), waits=[panel_deps[dc // 4]])
            qf_dep = rope_elu(tc % 4, qft[tc % 2][:, :],
                              [prev_scan_end], bkq)
            bkt = 6 if tc % 2 == 0 else 7
            for hh in range(4):
                mm(bkt, hh * 128, (hh + 1) * 128,
                   qft[tc % 2][:, hh * 128:hh * 128 + 128],
                   ident[:, :], start=True, stop=(hh == 3),
                   waits=[qf_dep] if hh == 0 else [])
            qffm_ready[tc] = bank_read(bkt, "scalar",
                lambda sce, _b=bkt, _o=qffm[:, tc * 512:(tc + 1) * 512]:
                sce.copy(_o, pb[_b][:, :]))
        wp_last_use["A"] = (sPE, sPE.n)
        if debug and nb == 0:
            dma("sync", dbg["qf0"], qffm[:, :], sST, waits=[(sSc, sSc.n)])
        it = 0
        mix_deps = []
        prev_md = [None] * 4
        for b in range(B):
            mixs = mixs2[b % 2]
            mix_deps = []
            for hh in range(4):
                for sc_i in range(NSC):
                    tc = b * 4 + sc_i
                    u = b * 4 + nb
                    qv = qffm[:, tc * 512 + hh * 128:tc * 512 + hh * 128 + 128]
                    bka = 4 + it % 4
                    mm(bka, 0, 128, kf_fm[:, u * SC + sc_i * 128:u * SC + (sc_i + 1) * 128],
                       qv, start=True, stop=True,
                       waits=[qffm_ready[tc], kfm_ready[tc]])
                    ad = bank_read(bka, "vector",
                                   lambda v, _o=Abb[it % 4], _b=bka: v.tensor_tensor(
                                       _o[:, :], pb[_b][:, 0:128], triuT[:, :], ALU.mult))
                    bko = it % 4
                    mm(bko, 0, 129, Abb[it % 4][:, :],
                       vext[:, tc * 516 + nb * 129:tc * 516 + nb * 129 + 129],
                       start=True, stop=False, waits=[ad, vext_ready[tc]])
                    mm(bko, 0, 129, qv,
                       prefall[:, (u * 4 + sc_i) * 129:(u * 4 + sc_i) * 129 + 129],
                       start=False, stop=True, waits=[pref_ready[u]])
                    r0, r1 = 2 * (it % 4), 2 * (it % 4) + 1
                    bank_read(bko, "vector",
                              lambda v, _b=bko, _r=r0: v.tensor_scalar(
                                  rcpb[:, _r:_r + 1], pb[_b][:, 128:129], epsb[:, 1:2],
                                  None, ALU.add),
                              waits=[prev_md[it % 4]])
                    p.emit("vector", lambda v: v.drain())
                    rd = vopd(lambda v, _a=r0, _c=r1: v.reciprocal(
                        rcpb[:, _c:_c + 1], rcpb[:, _a:_a + 1]))
                    md = bank_read(bko, "scalar",
                                   lambda sce, _b=bko, _c=r1,
                                   _o=mixs[:, sc_i * 512 + hh * 128:sc_i * 512 + hh * 128 + 128]:
                                   sce.mul(_o, pb[_b][:, 0:128], rcpb[:, _c:_c + 1]),
                                   waits=[rd, prev_mix_dma2[b % 2]])
                    prev_md[it % 4] = md
                    mix_deps.append(md)
                    it += 1
            mv = mixs[:, :].rearrange("p (t c) -> p t c", c=512)
            ov = mixb.rearrange("(t p) c -> p t c", p=128)[
                :, b * 4:b * 4 + 4, nb * 512:(nb + 1) * 512]
            prev_mix_dma2[b % 2] = dma("sync", ov, mv, sST, waits=mix_deps)
        prev_scan_end = (sPE, sPE.n)
        prev_mix_dma = prev_mix_dma2[1]
    mixb_done = (sST, sST.n)
    if debug:
        dma("sync", dbg["mixb"], mixb, sST, waits=[mixb_done])
    p.gate(p.marks(allsems))
    es_scan.close()
    es_attn.close()

    es_ffn = ExitStack()
    fallo = lambda name, shape, dt=BF16: es_ffn.enter_context(nc.sbuf_tensor(name, shape, dt))
    qs4 = fallo("qs4", [128, B * 1024])
    ks4 = fallo("ks4", [128, B * 1024])
    vs4 = fallo("vs4", [128, B * 2048])

    onrm = fallo("onrm", [128, B * 2048])
    slt = [fallo(f"slt{i}", [128, 2048]) for i in range(3)]
    lgs = fallo("lgs", [128, B * 32], F32)
    sml = junk[:, 0:24 * 16]
    mt4 = junk[:, 24 * 16:24 * 16 + 160]
    dots = junk[:, 24 * 16 + 160:24 * 16 + 320]
    qkc = fallo("qkc", [128, 160])
    sgt = fallo("sgt", [128, 8 * 512])
    cva = [fallo(f"cva{i}", [128, 2048]) for i in range(3)]
    ol4 = cva[2]
    ropeF_tab = None
    opg = [fallo(f"opg{i}", [128, 512]) for i in range(2)]
    ft1 = slt[0][:, 0:1024]
    ft2 = slt[2][:, 0:1024]

    def zv(l): return sml[:, (0 + l) * 16:(0 + l) * 16 + 16]
    def pmv(l): return sml[:, (4 + l) * 16:(4 + l) * 16 + 16]
    def cfv(l): return sml[:, (8 + l) * 16:(8 + l) * 16 + 16]
    SM_DIFF, SM_AB, SM_T1, SM_T2, SM_T3, SM_SO, SM_SO2, SM_M, SM_VAR, SM_SD, SM_RS = \
        [(12 + i) * 16 for i in range(11)]
    def sm(off): return sml[:, off:off + 16]

    PAIRS = [(l, m) for l in range(B) for m in range(l + 1)]
    PIDX = {pm_: i for i, pm_ in enumerate(PAIRS)}
    def mtv(pi): return mt4[:, pi * 16:(pi + 1) * 16]
    def dtv(pi): return dots[:, pi * 16:(pi + 1) * 16]
    def qkv(pi): return qkc[:, pi * 16:(pi + 1) * 16]

    c1b = cvws[:, 16:32][:, :, None].broadcast_to([128, 16, 128])
    c0b = cvws[:, 0:16][:, :, None].broadcast_to([128, 16, 128])
    xn3 = xn[:, :].rearrange("p (dc t) -> p dc t", t=T)

    prev_opg_dma = [None, None]
    prev_og = None
    panel_rr = [0]
    wp_last_use["C"] = wp_last_use["A"]
    s4bk = [0]

    def nxbk():
        b = s4bk[0] % 8
        s4bk[0] += 1
        return b

    def emit_conv(g):
        tcs = [l * 4 + g for l in range(B)]
        xc_dep = [None] * (B + 1)
        cA = cva[0][:, :].rearrange("p (dc t) -> p dc t", t=128)
        cB = cva[1][:, :].rearrange("p (dc t) -> p dc t", t=128)
        cC = cva[2][:, :].rearrange("p (dc t) -> p dc t", t=128)
        for l in range(B - 1, -1, -1):
            xl = xn3[:, :, tcs[l] * 128:(tcs[l] + 1) * 128]
            v1 = vop(lambda v, _x=xl: v.tensor_tensor(cA, _x, c1b, ALU.mult),
                     waits=[xc_dep[l + 1], conv_free[0]])
            if l > 0:
                xp = xn3[:, :, tcs[l - 1] * 128:(tcs[l - 1] + 1) * 128]
                vop(lambda v, _x=xp: v.tensor_tensor(cB, _x, c0b, ALU.mult))
                pre_dep = vop(lambda v: v.tensor_tensor(cC, cA, cB, ALU.add))
                src_ = cC
            else:
                pre_dep = v1
                src_ = cA
            xc_dep[l] = sop(lambda sce, _o=xl, _s=src_: sce.activation(_o, _s, AF.Silu),
                            waits=[pre_dep])
        conv_free[0] = xc_dep[0]
        return xc_dep

    def emit_panels(g, wd, n_out, npan, evac, xc_dep, extra_evac_wait):
        tcs = [l * 4 + g for l in range(B)]
        deps = []
        for pp in range(npan):
            key = ["A", "B", "C"][panel_rr[0] % 3]
            wp = {"A": wpA, "B": wpB, "C": wpC}[key]
            panel_rr[0] += 1
            pdep = []
            for q4 in range(4):
                pdep.append(dma("gpsimd",
                    wp[:, :].rearrange("p (dc n) -> p dc n", n=512)[:, q4 * 4:q4 * 4 + 4, :],
                    rows_fm(wd, n_out)[:, q4 * 4:q4 * 4 + 4, pp * 512:(pp + 1) * 512],
                    sWL, waits=[wp_last_use.get(key)]))
            for l in range(B):
                bk = nxbk()
                for dc in range(16):
                    mm(bk, 0, 512, xn_chunk(dc, tcs[l]), wp[:, dc * 512:(dc + 1) * 512],
                       start=(dc == 0), stop=(dc == 15),
                       waits=[pdep[dc // 4], xc_dep[l]])
                deps.append(evac(pp, l, bk, extra_evac_wait))
            wp_last_use[key] = (sPE, sPE.n)
        return deps

    def emit_gwave(g, pps, xc_dep, sg_ready, gate_slot):
        tcs = [l * 4 + g for l in range(B)]
        for pp in pps:
            key = ["A", "B", "C"][panel_rr[0] % 3]
            wp = {"A": wpA, "B": wpB, "C": wpC}[key]
            panel_rr[0] += 1
            pdep = []
            for q4 in range(4):
                pdep.append(dma("gpsimd",
                    wp[:, :].rearrange("p (dc n) -> p dc n", n=512)[:, q4 * 4:q4 * 4 + 4, :],
                    rows_fm(fwg, 2048)[:, q4 * 4:q4 * 4 + 4, pp * 512:(pp + 1) * 512],
                    sWL, waits=[wp_last_use.get(key)]))
            for l in range(B):
                ti = pp * B + l
                bk = nxbk()
                for dc in range(16):
                    mm(bk, 0, 512, xn_chunk(dc, tcs[l]), wp[:, dc * 512:(dc + 1) * 512],
                       start=(dc == 0), stop=(dc == 15), waits=[pdep[dc // 4], xc_dep[l]])
                sg_ready[(pp, l)] = bank_read(bk, "scalar",
                    lambda sce, _b=bk, _o=sgt[:, (ti % 8) * 512:(ti % 8) * 512 + 512]:
                    sce.activation(_o, pb[_b][:, :], AF.Silu),
                    waits=[gate_slot.get(ti % 8)])
            wp_last_use[key] = (sPE, sPE.n)

    def emit_proj(g, xc_dep, gate_slot, dots_prev, micro_prev):
        tcs = [l * 4 + g for l in range(B)]
        q_deps = emit_panels(g, fwq, 1024, 2, lambda pp, l, bk, w: bank_read(bk, "scalar",
            lambda sce, _o=qs4[:, l * 1024 + pp * 512:l * 1024 + pp * 512 + 512], _b=bk:
            sce.activation(_o, pb[_b][:, :], AF.Silu), waits=[w]), xc_dep, dots_prev)
        k_deps = emit_panels(g, fwk, 1024, 2, lambda pp, l, bk, w: bank_read(bk, "scalar",
            lambda sce, _o=ks4[:, l * 1024 + pp * 512:l * 1024 + pp * 512 + 512], _b=bk:
            sce.copy(_o, pb[_b][:, :]), waits=[w]), xc_dep, dots_prev)
        lg_deps = []
        for l in range(B):
            bk = nxbk()
            for dc in range(16):
                mm(bk, 0, 32, xn_chunk(dc, tcs[l]), wvr[:, dc * 32:(dc + 1) * 32],
                   start=(dc == 0), stop=(dc == 15), waits=[xc_dep[l]])
            lg_deps.append(bank_read(bk, "scalar",
                lambda sce, _o=lgs[:, l * 32:(l + 1) * 32], _b=bk:
                sce.copy(_o, pb[_b][:, 0:32]), waits=[dots_prev]))
        v_deps = emit_panels(g, fwv, 2048, 4, lambda pp, l, bk, w: bank_read(bk, "scalar",
            lambda sce, _o=vs4[:, l * 2048 + pp * 512:l * 2048 + pp * 512 + 512], _b=bk:
            sce.copy(_o, pb[_b][:, :]), waits=[w]), xc_dep, micro_prev)
        sg_ready = {}
        emit_gwave(g, [0, 1], xc_dep, sg_ready, gate_slot)
        return q_deps, k_deps, lg_deps, v_deps, sg_ready

    def emit_rope_coeff_dots(g, q_deps, k_deps, lg_deps):
        rope4_deps = [None] * B
        for l in range(B):
            cb1 = cosF[:, l * 64:(l + 1) * 64][:, None, :].broadcast_to([128, 16, 64])
            f1 = ft1[:, :].rearrange("p (h d) -> p h d", d=64)
            f2 = ft2[:, :].rearrange("p (h d) -> p h d", d=64)
            slo = sinFs[:, l * 64:l * 64 + 32][:, None, :].broadcast_to([128, 16, 32])
            shi = sinFs[:, l * 64 + 32:l * 64 + 64][:, None, :].broadcast_to([128, 16, 32])
            for buf, dps in [(qs4, q_deps), (ks4, k_deps)]:
                bv = buf[:, l * 1024:(l + 1) * 1024].rearrange("p (h d) -> p h d", d=64)
                vop(lambda v, _b=bv, _c=cb1, _f=f1: v.tensor_tensor(_f, _b, _c, ALU.mult),
                    waits=list(dps))
                vop(lambda v, _b=bv, _c=slo, _f=f2: v.tensor_tensor(
                    _f[:, :, 0:32], _b[:, :, 32:64], _c, ALU.mult))
                vop(lambda v, _b=bv, _c=shi, _f=f2: v.tensor_tensor(
                    _f[:, :, 32:64], _b[:, :, 0:32], _c, ALU.mult))
                rope4_deps[l] = vop(lambda v, _o=buf[:, l * 1024:(l + 1) * 1024]:
                                    v.tensor_tensor(_o, ft1[:, :], ft2[:, :], ALU.add))
        for l in range(B):
            lg = lgs[:, l * 32:(l + 1) * 32].rearrange("p (h r) -> p h r", r=2)
            vopd(lambda v, _l=lg: v.tensor_tensor(sm(SM_DIFF), _l[:, :, 0], _l[:, :, 1],
                                                 ALU.subtract), waits=[lg_deps[l]])
            vopd(lambda v, _z=zv(l): v.tensor_scalar(_z, sm(SM_DIFF), 0.0, None, ALU.is_ge))
            ab_dep = sopd(lambda sce: sce.activation(sm(SM_AB), sm(SM_DIFF), AF.Abs),
                         waits=[(sV, sV.n)])
            pm_dep = sopd(lambda sce, _o=pmv(l): sce.activation(_o, sm(SM_AB), AF.Sigmoid))
            if l == 0:
                vopd(lambda v, _o=cfv(0), _p=pmv(0): v.tensor_scalar(
                    _o, _p, 0.125, None, ALU.mult), waits=[pm_dep])
            else:
                for m_ in range(l):
                    pi = PIDX[(l, m_)]
                    vopd(lambda v, _a=zv(l), _b=zv(m_): v.tensor_tensor(sm(SM_T1), _a, _b, ALU.mult))
                    vopd(lambda v, _a=zv(l), _b=zv(m_): v.tensor_tensor(sm(SM_T2), _a, _b, ALU.add))
                    vopd(lambda v: v.tensor_scalar(sm(SM_T3), sm(SM_T1), 2.0, 1.0, ALU.mult, ALU.add))
                    vopd(lambda v, _o=mtv(pi): v.tensor_tensor(_o, sm(SM_T3), sm(SM_T2), ALU.subtract))
                vopd(lambda v, _m=mtv(PIDX[(l, 0)]): v.tensor_scalar(sm(SM_T1), _m, 1.0, None, ALU.add))
                cur = SM_T1
                for m_ in range(1, l):
                    nxt = SM_T2 if cur == SM_T1 else SM_T1
                    vopd(lambda v, _m=mtv(PIDX[(l, m_)]), _c=cur, _n=nxt:
                        v.tensor_tensor(sm(_n), sm(_c), _m, ALU.add))
                    cur = nxt
                vopd(lambda v, _c=cur: v.reciprocal(sm(SM_T3), sm(_c)))
                vopd(lambda v, _p=pmv(l): v.tensor_tensor(sm(SM_T1), _p, sm(SM_T3), ALU.mult),
                    waits=[pm_dep])
                vopd(lambda v, _o=cfv(l): v.tensor_scalar(_o, sm(SM_T1), 0.125, None, ALU.mult))
        for l in range(B):
            for m_ in range(l + 1):
                pi = PIDX[(l, m_)]
                qv4 = qs4[:, l * 1024:(l + 1) * 1024].rearrange("p (h d) -> p h d", d=64)
                kv4 = ks4[:, m_ * 1024:(m_ + 1) * 1024].rearrange("p (h d) -> p h d", d=64)
                f1 = ft1[:, :].rearrange("p (h d) -> p h d", d=64)
                vop(lambda v, _q=qv4, _k=kv4, _f=f1: v.tensor_tensor(_f, _q, _k, ALU.mult),
                    waits=[rope4_deps[l], rope4_deps[m_]])
                vopd(lambda v, _o=dtv(pi), _f=f1: v.tensor_reduce(_o, _f, AX.X, ALU.add))
                if m_ < l:
                    vopd(lambda v, _d=dtv(pi), _m=mtv(pi): v.tensor_tensor(
                        sm(SM_T1), _d, _m, ALU.mult))
                    vopd(lambda v, _o=qkv(pi), _c=cfv(l): v.tensor_tensor(
                        _o, sm(SM_T1), _c, ALU.mult))
                else:
                    vopd(lambda v, _o=qkv(pi), _d=dtv(pi), _c=cfv(l): v.tensor_tensor(
                        _o, _d, _c, ALU.mult))

    def emit_micro_ln(g, v_deps, conv_sil, sg_ready, gate_slot):
        tcs = [l * 4 + g for l in range(B)]
        o3 = ol4[:, :].rearrange("p (h d) -> p h d", d=128)
        s0 = slt[0][:, :].rearrange("p (h d) -> p h d", d=128)
        s1 = slt[1][:, :].rearrange("p (h d) -> p h d", d=128)
        s2 = slt[2][:, :].rearrange("p (h d) -> p h d", d=128)
        c3 = onrm[:, 3 * 2048:4 * 2048].rearrange("p (h d) -> p h d", d=128)
        def qkb(pi):
            return qkc[:, pi * 16:(pi + 1) * 16].rearrange("p (h o) -> p h o", o=1) \
                .broadcast_to([128, 16, 128])
        def vv4(m_):
            return vs4[:, m_ * 2048:(m_ + 1) * 2048].rearrange("p (h d) -> p h d", d=128)
        sq_dep = None
        for l in range(B):
            vw = list(v_deps)
            if l == 0:
                odep = vop(lambda v, _v=vv4(0), _q=qkb(PIDX[(0, 0)]):
                           v.tensor_tensor(o3, _v, _q, ALU.mult),
                           waits=vw + [sq_dep, conv_sil])
            elif l == 1:
                vop(lambda v, _v=vv4(0), _q=qkb(PIDX[(1, 0)]):
                    v.tensor_tensor(s0, _v, _q, ALU.mult), waits=vw)
                vop(lambda v, _v=vv4(1), _q=qkb(PIDX[(1, 1)]):
                    v.tensor_tensor(s1, _v, _q, ALU.mult))
                odep = vop(lambda v: v.tensor_tensor(o3, s0, s1, ALU.add),
                           waits=[sq_dep, conv_sil])
            elif l == 2:
                vop(lambda v, _v=vv4(0), _q=qkb(PIDX[(2, 0)]):
                    v.tensor_tensor(s0, _v, _q, ALU.mult), waits=vw)
                vop(lambda v, _v=vv4(1), _q=qkb(PIDX[(2, 1)]):
                    v.tensor_tensor(s1, _v, _q, ALU.mult))
                vop(lambda v: v.tensor_tensor(s2, s0, s1, ALU.add))
                vop(lambda v, _v=vv4(2), _q=qkb(PIDX[(2, 2)]):
                    v.tensor_tensor(s0, _v, _q, ALU.mult))
                odep = vop(lambda v: v.tensor_tensor(o3, s2, s0, ALU.add),
                           waits=[sq_dep, conv_sil])
            else:
                vop(lambda v, _v=vv4(0), _q=qkb(PIDX[(3, 0)]):
                    v.tensor_tensor(s0, _v, _q, ALU.mult), waits=vw)
                vop(lambda v, _v=vv4(1), _q=qkb(PIDX[(3, 1)]):
                    v.tensor_tensor(s1, _v, _q, ALU.mult))
                vop(lambda v: v.tensor_tensor(s2, s0, s1, ALU.add))
                vop(lambda v, _v=vv4(2), _q=qkb(PIDX[(3, 2)]):
                    v.tensor_tensor(s0, _v, _q, ALU.mult))
                vop(lambda v, _v=vv4(3), _q=qkb(PIDX[(3, 3)]):
                    v.tensor_tensor(s1, _v, _q, ALU.mult))
                vop(lambda v: v.tensor_tensor(c3, s0, s1, ALU.add))
                odep = vop(lambda v: v.tensor_tensor(o3, s2, c3, ALU.add),
                           waits=[sq_dep, conv_sil])
            vopd(lambda v: v.tensor_reduce(sm(SM_SO), o3, AX.X, ALU.add))
            vop(lambda v: v.tensor_tensor(s1, o3, o3, ALU.mult))
            vopd(lambda v: v.tensor_reduce(sm(SM_SO2), s1, AX.X, ALU.add))
            sq_dep = (sV, sV.n)
            vopd(lambda v: v.tensor_scalar(sm(SM_M), sm(SM_SO), 1.0 / 128, None, ALU.mult))
            vopd(lambda v: v.tensor_scalar(sm(SM_T1), sm(SM_SO2), 1.0 / 128, None, ALU.mult))
            vopd(lambda v: v.tensor_tensor(sm(SM_T2), sm(SM_M), sm(SM_M), ALU.mult))
            vopd(lambda v: v.tensor_tensor(sm(SM_VAR), sm(SM_T1), sm(SM_T2), ALU.subtract))
            vopd(lambda v: v.tensor_scalar(sm(SM_SD), sm(SM_VAR), epsb[:, 2:3], None, ALU.add))
            smi = sml.bitcast(I32)
            def smiv(off): return smi[:, off:off + 16]
            vopd(lambda v: v.tensor_scalar(smiv(SM_T1), smiv(SM_SD), 1, None, ALU.arith_shift_right))
            vopd(lambda v: v.tensor_scalar(smiv(SM_T2), smiv(SM_T1), 0x5f3759df, None, ALU.subtract))
            vopd(lambda v: v.tensor_scalar(smiv(SM_T3), smiv(SM_T2), -1, None, ALU.mult))
            vopd(lambda v: v.tensor_tensor(sm(SM_T1), sm(SM_T3), sm(SM_T3), ALU.mult))
            vopd(lambda v: v.tensor_tensor(sm(SM_T2), sm(SM_SD), sm(SM_T1), ALU.mult))
            vopd(lambda v: v.tensor_scalar(sm(SM_T1), sm(SM_T2), -0.5, 1.5, ALU.mult, ALU.add))
            vopd(lambda v: v.tensor_tensor(sm(SM_T2), sm(SM_T3), sm(SM_T1), ALU.mult))
            vopd(lambda v: v.tensor_tensor(sm(SM_T1), sm(SM_T2), sm(SM_T2), ALU.mult))
            vopd(lambda v: v.tensor_tensor(sm(SM_T3), sm(SM_SD), sm(SM_T1), ALU.mult))
            vopd(lambda v: v.tensor_scalar(sm(SM_T1), sm(SM_T3), -0.5, 1.5, ALU.mult, ALU.add))
            vopd(lambda v: v.tensor_tensor(sm(SM_RS), sm(SM_T2), sm(SM_T1), ALU.mult))
            mb_ = sm(SM_M)[:, :, None].broadcast_to([128, 16, 128])
            rb_ = sm(SM_RS)[:, :, None].broadcast_to([128, 16, 128])
            vop(lambda v, _m=mb_: v.tensor_tensor(s0, o3, _m, ALU.subtract))
            vop(lambda v, _r=rb_,
                _o=onrm[:, l * 2048:(l + 1) * 2048].rearrange("p (h d) -> p h d", d=128):
                v.tensor_tensor(_o, s0, _r, ALU.mult))
            onrm_l = (sV, sV.n)
            for pp in range(2):
                ti = pp * B + l
                oi = ti % 2
                og_dep = vop(lambda v, _o=opg[oi],
                             _s=sgt[:, (ti % 8) * 512:(ti % 8) * 512 + 512],
                             _n=onrm[:, l * 2048 + pp * 512:l * 2048 + pp * 512 + 512]:
                             v.tensor_tensor(_o[:, :], _s, _n, ALU.mult),
                             waits=[sg_ready[(pp, l)], onrm_l, prev_opg_dma[oi]])
                gate_slot[ti % 8] = og_dep
                prev_opg_dma[oi] = dma("sync",
                    outpb[tcs[l] * 128:(tcs[l] + 1) * 128, pp * 512:(pp + 1) * 512],
                    opg[oi][:, :], sST, waits=[og_dep])
        return (sV, sV.n)

    def emit_gates_wave2(g, sg_ready, onrm_mark, gate_slot):
        tcs = [l * 4 + g for l in range(B)]
        for pp in (2, 3):
            for l in range(B):
                ti = pp * B + l
                oi = ti % 2
                og_dep = vop(lambda v, _o=opg[oi],
                             _s=sgt[:, (ti % 8) * 512:(ti % 8) * 512 + 512],
                             _n=onrm[:, l * 2048 + pp * 512:l * 2048 + pp * 512 + 512]:
                             v.tensor_tensor(_o[:, :], _s, _n, ALU.mult),
                             waits=[sg_ready[(pp, l)], onrm_mark, prev_opg_dma[oi]])
                gate_slot[ti % 8] = og_dep
                prev_opg_dma[oi] = dma("sync",
                    outpb[tcs[l] * 128:(tcs[l] + 1) * 128, pp * 512:(pp + 1) * 512],
                    opg[oi][:, :], sST, waits=[og_dep])

    conv_free = [None]
    xc_dep_cur = emit_conv(0)
    gate_slot = {}
    dots_prev = None
    micro_prev = None
    for g in range(4):
        xc_this = xc_dep_cur
        q_deps, k_deps, lg_deps, v_deps, sg_ready = emit_proj(
            g, xc_this, gate_slot, dots_prev, micro_prev)
        emit_rope_coeff_dots(g, q_deps, k_deps, lg_deps)
        dots_prev = (sV, sV.n)
        if g < 3:
            xc_dep_cur = emit_conv(g + 1)
        micro_prev = emit_micro_ln(g, v_deps, xc_dep_cur[0] if g < 3 else None,
                                   sg_ready, gate_slot)
        emit_gwave(g, [2, 3], xc_this, sg_ready, gate_slot)
        emit_gates_wave2(g, sg_ready, micro_prev, gate_slot)
        if debug and g == 0:
            gm = p.marks(allsems)
            for nmm, buf in [("logits", lgs), ("qs4", qs4), ("ks4", ks4), ("vs4", vs4),
                             ("onrm", onrm), ("dots", dots), ("qkc", qkc)]:
                dma("sync", dbg[nmm], buf[:, :], sST, waits=gm)
    if debug:
        dma("sync", dbg["xc"], xn[:, :], sST, waits=p.marks(allsems))
        dma("sync", dbg["outpb"], outpb, sST, waits=p.marks(allsems))

    p.gate(p.marks(allsems))
    es_ffn.close()
    es_w.close()
    es_xn.close()

    es_s5 = ExitStack()
    s5a = lambda name, shape, dt=BF16: es_s5.enter_context(nc.sbuf_tensor(name, shape, dt))
    w5o = s5a("w5o", [128, 16 * 2048])
    w5f = s5a("w5f", [128, 16 * 2048])
    mtm = s5a("mtm", [128, 2048])
    otm = s5a("otm", [128, 2048])
    mfm = s5a("mfm", [128, 2048])
    ofm = s5a("ofm", [128, 2048])
    xb5 = [s5a(f"xb5{i}", [128, D], F32) for i in range(2)]
    outs = [s5a(f"outs{i}", [128, D], F32) for i in range(2)]

    w5p_deps = []
    for pn in range(4):
        d1 = dma("gpsimd",
                 w5o[:, pn * 8192:(pn + 1) * 8192].rearrange("p (mc n) -> p mc n", n=512),
                 rows_fm(woT, 2048)[:, :, pn * 512:(pn + 1) * 512], sWL)
        d2 = dma("gpsimd",
                 w5f[:, pn * 8192:(pn + 1) * 8192].rearrange("p (mc n) -> p mc n", n=512),
                 rows_fm(fwout, 2048)[:, :, pn * 512:(pn + 1) * 512], sWL)
        w5p_deps.append([d1, d2])

    out_deps = []
    prev_fm = None
    prev_ev = None
    for tc in range(NTC):
        mld = dma("sync", mtm[:, :], mixb[tc * 128:(tc + 1) * 128, :], sST,
                  waits=[mixb_done, prev_fm])
        old = dma("sync", otm[:, :], outpb[tc * 128:(tc + 1) * 128, :], sST,
                  waits=[prev_opg_dma[0], prev_opg_dma[1], prev_fm])
        xld = dma("sync", xb5[tc % 2][:, :], x[tc * 128:(tc + 1) * 128, :], sXL,
                  waits=[prev_ev])
        fm_deps = []
        for src, ldep, dst, bks in [(mtm, mld, mfm, (4, 5)), (otm, old, ofm, (6, 7))]:
            for halfq in range(4):
                bk = bks[halfq % 2]
                for k in range(4):
                    mcc = halfq * 4 + k
                    mm(bk, k * 128, (k + 1) * 128, src[:, mcc * 128:(mcc + 1) * 128],
                       ident[:, :], start=True, stop=(k == 3), waits=[ldep])
                pv = pb[bk][:, :].rearrange("p (k t) -> p k t", t=128)
                dv = dst[:, :].rearrange("p (mc t) -> p mc t", t=128)[
                    :, halfq * 4:halfq * 4 + 4, :]
                fm_deps.append(bank_read(bk, "vector",
                    lambda v, _o=dv, _i=pv: v.tensor_copy(_o, _i)))
        prev_fm = fm_deps[-1]
        ev_deps = []
        for pn in range(4):
            bko = pn % 2
            for mc in range(16):
                mm(bko, 0, 512, mfm[:, mc * 128:(mc + 1) * 128],
                   w5o[:, pn * 8192 + mc * 512:pn * 8192 + mc * 512 + 512],
                   start=(mc == 0), stop=False,
                   waits=(w5p_deps[pn] + fm_deps) if mc == 0 else [])
            for mc in range(16):
                mm(bko, 0, 512, ofm[:, mc * 128:(mc + 1) * 128],
                   w5f[:, pn * 8192 + mc * 512:pn * 8192 + mc * 512 + 512],
                   start=False, stop=(mc == 15))
            ev_deps.append(bank_read(bko, "vector",
                lambda v, _b=bko, _o=outs[tc % 2][:, pn * 512:pn * 512 + 512],
                _x=xb5[tc % 2][:, pn * 512:pn * 512 + 512]:
                v.scalar_tensor_tensor(_o, pb[_b][:, :], 1.0, _x, ALU.mult, ALU.add),
                waits=[xld, out_deps[tc - 2] if tc >= 2 else None]))
        prev_ev = ev_deps[-1]
        out_deps.append(dma("sync", out[tc * 128:(tc + 1) * 128, :], outs[tc % 2][:, :],
                            sST, waits=ev_deps))

    p.emit("sync", None, waits=[out_deps[-1]])

    with nc.Block() as block:
        p.flush(block)

    es_s5.close()
    es.close()
    return nc, dbg



def kernel(**inputs):
    in_maps = _shard_inputs(inputs)
    nc, _ = build_nc(debug=False)
    res = run_bass_kernel_spmd(nc, in_maps, core_ids=list(range(NCORES)))
    outf = np.empty((B, S, D), np.float32)
    for c in range(NCORES):
        outf[:, c * SC:(c + 1) * SC, :] = np.asarray(res.results[c]["out"]).reshape(B, SC, D)
    return outf


# version 06
# speedup vs baseline: 1.0828x; 1.0451x over previous
import numpy as np
import ml_dtypes
from contextlib import ExitStack
import concourse.bass as bass
import concourse.mybir as mybir
from concourse.bass_utils import run_bass_kernel_spmd

F32 = mybir.dt.float32
BF16 = mybir.dt.bfloat16
AF = mybir.ActivationFunctionType
ALU = mybir.AluOpType
AX = mybir.AxisListType
I32 = mybir.dt.int32

NCORES = 8
B = 4
S = 4096
D = 2048
SC = S // NCORES
T = B * SC
NTC = T // 128
NSC = SC // 128
EPS_LN = 1e-5
EPS_ATTN = 1e-6
EPS_FFN = 1e-5



def _host_tables(core_id):
    t = {}
    t["ident"] = np.eye(128, dtype=ml_dtypes.bfloat16)
    t["triuT"] = np.triu(np.ones((128, 128), np.float32)).astype(ml_dtypes.bfloat16)
    invA = 10000.0 ** (-np.arange(64, dtype=np.float64) / 64.0)
    cosA = np.zeros((128, NSC * 128), np.float64)
    sinAs = np.zeros((128, NSC * 128), np.float64)
    for sc in range(NSC):
        s_glob = core_id * SC + sc * 128 + np.arange(128)
        ang = s_glob[:, None] * invA[None, :]
        c, sn = np.cos(ang), np.sin(ang)
        cosA[:, sc * 128:sc * 128 + 64] = c
        cosA[:, sc * 128 + 64:sc * 128 + 128] = c
        sinAs[:, sc * 128:sc * 128 + 64] = -sn
        sinAs[:, sc * 128 + 64:sc * 128 + 128] = sn
    t["cosA"] = cosA.astype(ml_dtypes.bfloat16)
    t["sinAs"] = sinAs.astype(ml_dtypes.bfloat16)
    invF = 10000.0 ** (-np.arange(32, dtype=np.float64) / 32.0)
    cosF = np.zeros((128, B * 64), np.float64)
    sinFs = np.zeros((128, B * 64), np.float64)
    for l in range(B):
        ang = l * invF
        c, sn = np.cos(ang), np.sin(ang)
        cosF[:, l * 64:l * 64 + 32] = c[None, :]
        cosF[:, l * 64 + 32:l * 64 + 64] = c[None, :]
        sinFs[:, l * 64:l * 64 + 32] = -sn[None, :]
        sinFs[:, l * 64 + 32:l * 64 + 64] = sn[None, :]
    t["cosF"] = cosF.astype(ml_dtypes.bfloat16)
    t["sinFs"] = sinFs.astype(ml_dtypes.bfloat16)
    cb = np.zeros((128, B * 1024), np.float64)
    sb = np.zeros((128, B * 1024), np.float64)
    for l in range(B):
        for h in range(16):
            cb[:, l * 1024 + h * 64:l * 1024 + h * 64 + 64] = cosF[:, l * 64:(l + 1) * 64]
            sb[:, l * 1024 + h * 64:l * 1024 + h * 64 + 64] = sinFs[:, l * 64:(l + 1) * 64]
    t["cosFb"] = cb.astype(ml_dtypes.bfloat16)
    t["sinFb"] = sb.astype(ml_dtypes.bfloat16)
    m = np.zeros((128, NCORES), np.float32)
    m[:, :core_id] = 1.0
    t["maskc"] = m
    e = np.zeros((128, 4), np.float32)
    e[:, 0] = EPS_LN
    e[:, 1] = EPS_ATTN
    e[:, 2] = EPS_FFN
    t["epsb"] = e
    return t


def _shard_inputs(inputs):
    x = np.asarray(inputs["L0_attn_x"], np.float32)
    c32 = lambda a: np.ascontiguousarray(np.asarray(a, np.float32))
    wqT = c32(np.asarray(inputs["L0_attn_Wq"], np.float32).T)
    wkT = c32(np.asarray(inputs["L0_attn_Wk"], np.float32).T)
    wvT = c32(np.asarray(inputs["L0_attn_Wv"], np.float32).T)
    woT = c32(np.asarray(inputs["L0_attn_Wo"], np.float32).T)
    fwq = c32(inputs["L0_ffn_Wq"])
    fwk = c32(inputs["L0_ffn_Wk"])
    fwv = c32(inputs["L0_ffn_Wv"])
    fwvT = c32(np.asarray(inputs["L0_ffn_Wv"], np.float32).T)
    fwg = c32(inputs["L0_ffn_Wg"])
    fwout = c32(inputs["L0_ffn_Wout"])
    cw = np.asarray(inputs["L0_ffn_conv_w"], np.float32)
    cvw = np.zeros((128, 32), np.float32)
    cvw[:, 0:16] = cw[:, 0].reshape(16, 128).T
    cvw[:, 16:32] = cw[:, 1].reshape(16, 128).T
    rw = np.asarray(inputs["L0_ffn_router_w"], np.float32)
    rwfm = np.zeros((128, 32), np.float32)
    for h in range(16):
        rwfm[:, 2 * h:2 * h + 2] = rw[h]
    in_maps = []
    for c in range(NCORES):
        xs = np.ascontiguousarray(x[:, c * SC:(c + 1) * SC, :].reshape(T, D))
        m = {"x": xs, "wqT": wqT, "wkT": wkT, "wvT": wvT, "woT": woT,
             "fwq": fwq, "fwk": fwk, "fwv": fwv, "fwvT": fwvT, "fwg": fwg,
             "fwout": fwout, "cvw": cvw, "rwfm": rwfm}
        ht = _host_tables(c)
        m.update({("d_" + k if k in ("ident", "triuT", "cosA", "sinAs", "cosF", "sinFs",
                                     "cosFb", "sinFb") else k): v
                  for k, v in ht.items()})
        in_maps.append(m)
    return in_maps



ENGINES = ("tensor", "vector", "scalar", "gpsimd", "sync")


class Sem:
    def __init__(self, handle, owner=None):
        self.h = handle
        self.n = 0
        self.owner = owner


class Prog:
    def __init__(self):
        self.ops = {e: [] for e in ENGINES}
        self.maxwait = {e: {} for e in ENGINES}

    def emit(self, eng, fn, waits=(), inc=None, inck=1):
        ws = []
        for w in waits:
            if w is None:
                continue
            sem, cnt = w
            if cnt <= 0 or sem.owner == eng:
                continue
            prev = self.maxwait[eng].get(id(sem), -1)
            if cnt > prev:
                self.maxwait[eng][id(sem)] = cnt
                ws.append((sem, cnt))
        dep = None
        if inc is not None:
            inc.n += inck
            dep = (inc, inc.n)
        self.ops[eng].append((fn, ws, inc, inck))
        return dep

    def marks(self, sems):
        return [(s, s.n) for s in sems]

    def gate(self, marks):
        for e in ENGINES:
            self.emit(e, None, waits=marks)

    def flush(self, block):
        for e in ENGINES:
            ops = self.ops[e]
            def body(eng, _ops=ops):
                for fn, ws, inc, inck in _ops:
                    for sem, cnt in ws:
                        eng.wait_ge(sem.h, cnt)
                    if fn is None:
                        continue
                    ins = fn(eng)
                    if inc is not None:
                        ins.then_inc(inc.h, inck)
            getattr(block, e)(body)


def rows_fm(w_dram, ncols):
    return w_dram.rearrange("(dc p) n -> p dc n", p=128)[:, :, 0:ncols]



def build_nc(debug=False):
    nc = bass.Bass("TRN2", target_bir_lowering=False)

    def din(name, shape, dt=F32):
        return nc.dram_tensor(name, shape, dt, kind="ExternalInput").ap()

    x = din("x", [T, D])
    wqT = din("wqT", [D, 2048]); wkT = din("wkT", [D, 512]); wvT = din("wvT", [D, 512])
    woT = din("woT", [2048, D])
    fwq = din("fwq", [D, 1024]); fwk = din("fwk", [D, 1024]); fwv = din("fwv", [D, 2048])
    fwvT = din("fwvT", [2048, D]); fwg = din("fwg", [D, 2048]); fwout = din("fwout", [2048, D])
    cvw_d = din("cvw", [128, 32]); rwfm_d = din("rwfm", [128, 32])
    identd = din("d_ident", [128, 128], BF16); triuTd = din("d_triuT", [128, 128], BF16)
    cosAd = din("d_cosA", [128, NSC * 128], BF16); sinAsd = din("d_sinAs", [128, NSC * 128], BF16)
    cosFd = din("d_cosF", [128, B * 64], BF16); sinFsd = din("d_sinFs", [128, B * 64], BF16)
    cosFbd = din("d_cosFb", [128, B * 1024], BF16); sinFbd = din("d_sinFb", [128, B * 1024], BF16)
    maskcd = din("maskc", [128, NCORES]); epsbd = din("epsb", [128, 4])

    out = nc.dram_tensor("out", [T, D], F32, kind="ExternalOutput").ap()

    cc_in = nc.dram_tensor("cc_in", [16 * 128, 129], F32)
    cc_outs = [nc.dram_tensor(f"cc_out{b}", [NCORES * 4 * 128, 129], F32) for b in range(B)]
    mixb = nc.dram_tensor("mixb", [T, 2048], BF16).ap()
    outpb = nc.dram_tensor("outpb", [T, 2048], BF16).ap()

    dbg = {}
    def ddbg(name, shape, dt):
        if debug:
            dbg[name] = nc.dram_tensor("dbg_" + name, shape, dt, kind="ExternalOutput").ap()
    ddbg("xn", [128, 16 * T], BF16)
    ddbg("kf_fm", [128, B * 4 * SC], BF16)
    ddbg("vext", [128, NTC * 516], BF16)
    ddbg("snap", [128, 16 * 3 * 129], BF16)
    ddbg("prefall", [128, 16 * 4 * 129], BF16)
    ddbg("qf0", [128, NTC * 512], BF16)
    ddbg("wvr", [128, 512], BF16)
    ddbg("logits", [128, B * 32], F32)
    ddbg("xc", [128, 16 * T], BF16)
    ddbg("qs4", [128, B * 1024], BF16)
    ddbg("ks4", [128, B * 1024], BF16)
    ddbg("vs4", [128, B * 2048], BF16)
    ddbg("onrm", [128, B * 2048], BF16)
    ddbg("dots", [128, 10 * 16], F32)
    ddbg("qkc", [128, 10 * 16], BF16)
    ddbg("mixb", [T, 2048], BF16)
    ddbg("outpb", [T, 2048], BF16)

    es = ExitStack()
    alloc = lambda name, shape, dt=BF16: es.enter_context(nc.sbuf_tensor(name, shape, dt))

    ident = alloc("ident", [128, 128])
    triuT = alloc("triuT", [128, 128])
    cosA = alloc("cosA", [128, NSC * 128])
    sinAs = alloc("sinAs", [128, NSC * 128])
    cosF = alloc("cosF", [128, B * 64])
    sinFs = alloc("sinFs", [128, B * 64])
    cvws = alloc("cvws", [128, 32], F32)
    rwfm = alloc("rwfm_s", [128, 32])
    maskc = alloc("maskc_s", [128, NCORES], F32)
    epsb = alloc("epsb_s", [128, 4], F32)
    wvr = alloc("wvr", [128, 512])
    junk = alloc("junk", [128, D + 16], F32)
    stat = alloc("stat", [128, 16], F32)

    pb = [es.enter_context(nc.psum_tensor(f"pb{i}", [128, 512], F32)) for i in range(8)]
    bank_w = [None] * 8
    bank_r = [[] for _ in range(8)]

    sems = {}
    for nm, owner in [("sWL", None), ("sXL", None), ("sST", None), ("sPE", "tensor"),
                      ("sV", "vector"), ("sSc", "scalar"), ("sG", "gpsimd"), ("sCC", None)]:
        sems[nm] = Sem(es.enter_context(nc.semaphore(nm)), owner)
    sWL, sXL, sST, sPE, sV, sSc, sG, sCC = (sems[n] for n in
        ["sWL", "sXL", "sST", "sPE", "sV", "sSc", "sG", "sCC"])
    allsems = [sWL, sXL, sST, sPE, sV, sSc, sCC]

    p = Prog()

    def mm(bank_i, c0, c1, lhsT, rhs, start, stop, waits=()):
        ws = list(waits) + bank_r[bank_i]
        def fn(t, _o=pb[bank_i][:, c0:c1], _l=lhsT, _r=rhs, _s=start, _p=stop):
            return t.matmul(_o, _l, _r, start=_s, stop=_p, skip_group_check=True)
        dep = p.emit("tensor", fn, waits=ws, inc=sPE)
        bank_r[bank_i] = []
        bank_w[bank_i] = dep
        return dep

    def bank_read(bank_i, eng, fn, waits=()):
        ws = list(waits)
        if bank_w[bank_i] is not None:
            ws.append(bank_w[bank_i])
        inc = {"vector": sV, "scalar": sSc, "gpsimd": sG}[eng]
        dep = p.emit(eng, fn, waits=ws, inc=inc)
        bank_r[bank_i].append(dep)
        return dep

    def dma(eng, out_ap, in_ap, sem, waits=()):
        def fn(e, _o=out_ap, _i=in_ap):
            return e.dma_start(out=_o, in_=_i)
        return p.emit(eng, fn, waits=waits, inc=sem, inck=16)

    def vop(fn, waits=()):
        return p.emit("vector", fn, waits=waits, inc=sV)

    def sop(fn, waits=()):
        return p.emit("scalar", fn, waits=waits, inc=sSc)

    def vopd(fn, waits=()):
        d = p.emit("vector", fn, waits=waits, inc=sV)
        p.emit("vector", lambda v: v.drain())
        return d

    def sopd(fn, waits=()):
        d = p.emit("scalar", fn, waits=waits, inc=sSc)
        p.emit("scalar", lambda s: s.drain())
        return d

    for dst, src in [(ident, identd), (triuT, triuTd), (cosA, cosAd),
                     (sinAs, sinAsd), (cosF, cosFd), (sinFs, sinFsd),
                     (cvws, cvw_d), (maskc, maskcd), (epsb, epsbd)]:
        dma("sync", dst[:, :], src, sXL)
    rw_dep = dma("gpsimd", rwfm[:, :], rwfm_d, sWL)
    p.gate(p.marks([sXL]) + [rw_dep])

    es_xn = ExitStack()
    xn = es_xn.enter_context(nc.sbuf_tensor("xn", [128, 16 * T], BF16))
    es_w = ExitStack()
    wpA = es_w.enter_context(nc.sbuf_tensor("wpA", [128, 16 * 512], BF16))
    wpB = es_w.enter_context(nc.sbuf_tensor("wpB", [128, 16 * 512], BF16))
    wpC = es_w.enter_context(nc.sbuf_tensor("wpC", [128, 16 * 512], BF16))

    wvr_mm_deps = []
    for h in range(16):
        buf = wpB[:, (h % 2) * 4096:(h % 2) * 4096 + 2048]
        ws = [rw_dep] + ([wvr_mm_deps[h - 2]] if h >= 2 else [])
        ld = dma("gpsimd", buf, fwvT[h * 128:(h + 1) * 128, :], sWL, waits=ws)
        dep = None
        for dc in range(16):
            dep = mm(7, dc * 32 + h * 2, dc * 32 + h * 2 + 2,
                     buf[:, dc * 128:(dc + 1) * 128], rwfm[:, 2 * h:2 * h + 2],
                     start=True, stop=True, waits=[ld])
        wvr_mm_deps.append(dep)
    wvr_dep = bank_read(7, "vector", lambda v: v.tensor_copy(wvr[:, :], pb[7][:, :]))
    if debug:
        dma("sync", dbg["wvr"], wvr[:, :], sST, waits=[wvr_dep])

    def xn_chunk(dc, tc):
        return xn[:, dc * T + tc * 128: dc * T + (tc + 1) * 128]

    es_attn = ExitStack()
    aallo = lambda name, shape, dt=BF16: es_attn.enter_context(nc.sbuf_tensor(name, shape, dt))
    kf_fm = aallo("kf_fm", [128, B * 4 * SC])
    vext = aallo("vext", [128, NTC * 516])
    prefall = aallo("prefall", [128, 16 * 4 * 129])
    t1b = aallo("t1b", [128, 512])
    t2b = aallo("t2b", [128, 512])
    rawb = aallo("rawb", [128, 512])
    mnb = aallo("mnb", [128, 512])
    exb = aallo("exb", [128, 512])
    ktmb = aallo("ktmb", [128, 512])
    es_sn = ExitStack()
    snap = es_sn.enter_context(nc.sbuf_tensor("snap", [128, 16 * 3 * 129], BF16))
    stot = es_sn.enter_context(nc.sbuf_tensor("stot", [128, 4 * 129], F32))

    es_s1 = ExitStack()
    xb1 = es_s1.enter_context(nc.sbuf_tensor("xb1", [128, D], F32))
    xnb = es_s1.enter_context(nc.sbuf_tensor("xnb", [128, D], BF16))

    ones_view = vext[:, :].rearrange("p (t k c) -> p t k c", k=4, c=129)[:, :, :, 128:129]
    ones_dep = vop(lambda v: v.memset(ones_view, 1.0))
    snap_deps = []

    wk_dep = []
    for q4 in range(4):
        wk_dep.append(dma("gpsimd",
            wpA[:, :].rearrange("p (dc n) -> p dc n", n=512)[:, q4 * 4:q4 * 4 + 4, :],
            rows_fm(wkT, 512)[:, q4 * 4:q4 * 4 + 4, :], sWL,
            waits=[wvr_mm_deps[14], wvr_mm_deps[15]]))
    wv_dep = []
    for q4 in range(4):
        wv_dep.append(dma("gpsimd",
            wpB[:, :].rearrange("p (dc n) -> p dc n", n=512)[:, q4 * 4:q4 * 4 + 4, :],
            rows_fm(wvT, 512)[:, q4 * 4:q4 * 4 + 4, :], sWL,
            waits=[wvr_mm_deps[14], wvr_mm_deps[15], wvr_dep]))
    pre_panel_deps = []
    for q4 in range(4):
        pre_panel_deps.append(dma("gpsimd",
            wpC[:, :].rearrange("p (dc n) -> p dc n", n=512)[:, q4 * 4:q4 * 4 + 4, :],
            rows_fm(wqT, 2048)[:, q4 * 4:q4 * 4 + 4, 0:512], sWL))

    rope_free = [None]

    def rope_elu(sc_i, out_ap, raw_waits, bank_i):
        raw_dep = bank_read(bank_i, "scalar",
                            lambda sce, _b=bank_i: sce.copy(rawb[:, 0:512], pb[_b][:, :]),
                            waits=list(raw_waits) + [rope_free[0]])
        rv = rawb[:, 0:512].rearrange("p (u d) -> p u d", d=128)
        cosb = cosA[:, sc_i * 128:(sc_i + 1) * 128][:, None, :].broadcast_to([128, 4, 128])
        sin_lo = sinAs[:, sc_i * 128:sc_i * 128 + 64][:, None, :].broadcast_to([128, 4, 64])
        sin_hi = sinAs[:, sc_i * 128 + 64:sc_i * 128 + 128][:, None, :].broadcast_to([128, 4, 64])
        t1v = t1b[:, 0:512].rearrange("p (u d) -> p u d", d=128)
        t2v = t2b[:, 0:512].rearrange("p (u d) -> p u d", d=128)
        vop(lambda v, _a=t1v, _b=rv, _c=cosb: v.tensor_tensor(_a, _b, _c, ALU.mult),
            waits=[raw_dep])
        vop(lambda v, _a=t2v, _b=rv, _c=sin_lo: v.tensor_tensor(
            _a[:, :, 0:64], _b[:, :, 64:128], _c, ALU.mult))
        vop(lambda v, _a=t2v, _b=rv, _c=sin_hi: v.tensor_tensor(
            _a[:, :, 64:128], _b[:, :, 0:64], _c, ALU.mult))
        vop(lambda v: v.tensor_tensor(mnb[:, 0:512], t1b[:, 0:512], t2b[:, 0:512], ALU.add))
        vop(lambda v: v.tensor_scalar(t1b[:, 0:512], mnb[:, 0:512], 0.0, None, ALU.min))
        mn_mark = (sV, sV.n)
        ex_dep = sop(lambda sce: sce.activation(exb[:, 0:512], t1b[:, 0:512], AF.Exp),
                     waits=[mn_mark])
        out_dep = vop(lambda v, _o=out_ap: v.scalar_tensor_tensor(
            _o, mnb[:, 0:512], 0.0, exb[:, 0:512], ALU.max, ALU.add),
            waits=[ex_dep, (sPE, sPE.n)])
        rope_free[0] = out_dep
        return out_dep

    xn_ready = [[] for _ in range(NTC)]
    x_cons = [None] * NTC
    xnb_free = [None]

    def ln_chain(tc):
        ws = [x_cons[tc - 1]] if tc >= 1 else []
        xld = dma("sync", xb1[:, :], x[tc * 128:(tc + 1) * 128, :], sXL, waits=ws)
        sop(lambda sce: sce.activation(junk[:, 0:D], xb1[:, :], AF.Identity,
                                       accum_out=stat[:, 0:1]),
            waits=[xld])
        st_dep = sop(lambda sce: sce.activation(junk[:, 0:D], xb1[:, :], AF.Square,
                                                accum_out=stat[:, 1:2]))
        vopd(lambda v: v.tensor_scalar(stat[:, 2:3], stat[:, 0:1], 1.0 / D, None, ALU.mult),
            waits=[st_dep])
        vopd(lambda v: v.tensor_scalar(stat[:, 3:4], stat[:, 1:2], 1.0 / D, None, ALU.mult))
        vopd(lambda v: v.tensor_tensor(stat[:, 4:5], stat[:, 2:3], stat[:, 2:3], ALU.mult))
        vopd(lambda v: v.tensor_tensor(stat[:, 5:6], stat[:, 3:4], stat[:, 4:5], ALU.subtract))
        vopd(lambda v: v.tensor_scalar(stat[:, 6:7], stat[:, 5:6], epsb[:, 0:1], None, ALU.add))
        sti = stat[:, :].bitcast(I32)
        vopd(lambda v: v.tensor_scalar(sti[:, 8:9], sti[:, 6:7], 1, None, ALU.arith_shift_right))
        vopd(lambda v: v.tensor_scalar(sti[:, 9:10], sti[:, 8:9], 0x5f3759df, None, ALU.subtract))
        vopd(lambda v: v.tensor_scalar(sti[:, 10:11], sti[:, 9:10], -1, None, ALU.mult))
        vopd(lambda v: v.tensor_tensor(stat[:, 8:9], stat[:, 10:11], stat[:, 10:11], ALU.mult))
        vopd(lambda v: v.tensor_tensor(stat[:, 9:10], stat[:, 6:7], stat[:, 8:9], ALU.mult))
        vopd(lambda v: v.tensor_scalar(stat[:, 8:9], stat[:, 9:10], -0.5, 1.5, ALU.mult, ALU.add))
        vopd(lambda v: v.tensor_tensor(stat[:, 9:10], stat[:, 10:11], stat[:, 8:9], ALU.mult))
        vopd(lambda v: v.tensor_tensor(stat[:, 8:9], stat[:, 9:10], stat[:, 9:10], ALU.mult))
        vopd(lambda v: v.tensor_tensor(stat[:, 10:11], stat[:, 6:7], stat[:, 8:9], ALU.mult))
        vopd(lambda v: v.tensor_scalar(stat[:, 8:9], stat[:, 10:11], -0.5, 1.5, ALU.mult, ALU.add))
        vopd(lambda v: v.tensor_tensor(stat[:, 7:8], stat[:, 9:10], stat[:, 8:9], ALU.mult))
        nrm_dep = vop(lambda v: v.tensor_scalar(
            xnb[:, :], xb1[:, :], stat[:, 2:3], stat[:, 7:8], ALU.subtract, ALU.mult),
            waits=[xnb_free[0]])
        x_cons[tc] = nrm_dep
        tr_dep = None
        for grp in range(4):
            bk = 6 if grp % 2 == 0 else 7
            for k in range(4):
                dcc = grp * 4 + k
                tr_dep = mm(bk, k * 128, (k + 1) * 128,
                            xnb[:, dcc * 128:(dcc + 1) * 128], ident[:, :],
                            start=True, stop=(k == 3), waits=[nrm_dep])
            outv = xn[:, :].rearrange("p (dc t) -> p dc t", t=T)[
                :, grp * 4:grp * 4 + 4, tc * 128:(tc + 1) * 128]
            pv = pb[bk][:, :].rearrange("p (k t) -> p k t", t=128)
            if grp % 2 == 0:
                ed = bank_read(bk, "vector", lambda v, _o=outv, _i=pv: v.tensor_copy(_o, _i))
            else:
                ed = bank_read(bk, "scalar", lambda sce, _o=outv, _i=pv: sce.copy(_o, _i))
            xn_ready[tc].append(ed)
        xnb_free[0] = tr_dep

    kfm_ready = [None] * NTC
    vext_ready = [None] * NTC
    ccin_deps = []
    cc_deps = []
    ln_chain(0)
    for tc in range(NTC):
        if tc + 1 < NTC:
            ln_chain(tc + 1)
        b, sc_i = tc // 4, tc % 4
        for dc in range(16):
            mm(4, 0, 512, xn_chunk(dc, tc), wpA[:, dc * 512:(dc + 1) * 512],
               start=(dc == 0), stop=(dc == 15),
               waits=[wk_dep[dc // 4]] + (xn_ready[tc] if dc == 0 else []))
        kf_dep = rope_elu(sc_i, ktmb[:, 0:512], [], 4)
        for dc in range(16):
            mm(5, 0, 512, xn_chunk(dc, tc), wpB[:, dc * 512:(dc + 1) * 512],
               start=(dc == 0), stop=(dc == 15),
               waits=[wv_dep[dc // 4]] + (xn_ready[tc] if dc == 0 else []))
        vout = vext[:, :].rearrange("p (t k c) -> p t k c", k=4, c=129)[
            :, tc:tc + 1, :, 0:128]
        pv5 = pb[5][:, :].rearrange("p (k d) -> p k d", d=128)[:, None, :, :]
        vext_ready[tc] = bank_read(5, "scalar",
                                   lambda sce, _o=vout, _i=pv5: sce.copy(_o, _i))
        for kvh in range(4):
            mm(6, kvh * 128, (kvh + 1) * 128, ktmb[:, kvh * 128:(kvh + 1) * 128],
               ident[:, :], start=True, stop=(kvh == 3),
               waits=[kf_dep] if kvh == 0 else [])
        kfm_out = kf_fm[:, :].rearrange("p (u s) -> p u s", s=SC)[
            :, b * 4:b * 4 + 4, sc_i * 128:(sc_i + 1) * 128]
        pv6 = pb[6][:, :].rearrange("p (k d) -> p k d", d=128)
        kfm_ready[tc] = bank_read(6, "vector",
                                  lambda v, _o=kfm_out, _i=pv6: v.tensor_copy(_o, _i))
        for kvh in range(4):
            bk = kvh
            if sc_i > 0:
                sn_out = snap[:, ((b * 4 + kvh) * 3 + sc_i - 1) * 129:
                              ((b * 4 + kvh) * 3 + sc_i - 1) * 129 + 129]
                sdep = bank_read(bk, "scalar",
                    lambda sce, _o=sn_out, _b=bk: sce.copy(_o, pb[_b][:, 0:129]))
                snap_deps.append(sdep)
            mm(bk, 0, 129, ktmb[:, kvh * 128:(kvh + 1) * 128],
               vext[:, tc * 516 + kvh * 129:tc * 516 + kvh * 129 + 129],
               start=(sc_i == 0), stop=(sc_i == 3),
               waits=[kf_dep, vext_ready[tc], ones_dep])
        if sc_i == 3:
            ev_deps = []
            for kvh in range(4):
                ev_deps.append(bank_read(kvh, "scalar",
                    lambda sce, _k=kvh: sce.copy(stot[:, _k * 129:(_k + 1) * 129],
                                                 pb[_k][:, 0:129]),
                    waits=[ccin_deps[-1]] if ccin_deps else []))
            ccv = cc_in.ap().rearrange("(u p) c -> p u c", p=128)[:, b * 4:b * 4 + 4, :]
            sv = stot[:, :].rearrange("p (k c) -> p k c", c=129)
            ccin_deps.append(dma("sync", ccv, sv, sST, waits=ev_deps))

            def cc_fn(g, _b=b):
                return g.collective_compute(
                    "AllGather", ALU.bypass, replica_groups=[list(range(NCORES))],
                    ins=[cc_in.ap()[_b * 4 * 128:(_b + 1) * 4 * 128, :].opt()],
                    outs=[cc_outs[_b].ap().opt()])
            cc_deps.append(p.emit("gpsimd", cc_fn, waits=[ccin_deps[-1]], inc=sCC))

    if debug:
        gm = p.marks(allsems)
        dma("sync", dbg["xn"], xn[:, :], sST, waits=gm)
    es_s1.close()
    es_snap = ExitStack()
    gsum = es_snap.enter_context(nc.sbuf_tensor("gsum", [128, 16 * 129], F32))
    gtmp = es_snap.enter_context(nc.sbuf_tensor("gtmp", [128, 4 * 129], F32))

    wp_last_use = {"A": (sPE, sPE.n), "B": (sPE, sPE.n)}
    vprev = None
    gsum_deps = [None] * B
    for b in range(B):
        jview = junk[:, 0:4 * 129]
        gsv = gsum[:, b * 4 * 129:(b + 1) * 4 * 129]
        for j in range(NCORES):
            ccv = cc_outs[b].ap().rearrange("(j u p) c -> p j u c", p=128, u=4)[:, j]
            gv = gtmp[:, 0:4 * 129].rearrange("p (u c) -> p u c", c=129)
            ld = dma("sync", gv, ccv, sXL, waits=[cc_deps[b]] + ([vprev] if vprev else []))
            if j == 0:
                vprev = vop(lambda v, _j1=jview: v.tensor_scalar(
                    _j1, gtmp[:, 0:4 * 129], maskc[:, 0:1], None, ALU.mult), waits=[ld])
            else:
                s_, d_ = (jview, gsv) if j % 2 == 1 else (gsv, jview)
                vprev = vop(lambda v, _j=j, _s=s_, _d=d_: v.scalar_tensor_tensor(
                    _d, gtmp[:, 0:4 * 129], maskc[:, _j:_j + 1], _s, ALU.mult, ALU.add),
                    waits=[ld])
        gsum_deps[b] = vprev
    pref_ready = [None] * 16
    for u in range(16):
        svv = snap[:, u * 3 * 129:(u + 1) * 3 * 129].rearrange("p (s c) -> p s c", c=129)
        gvv = gsum[:, u * 129:(u + 1) * 129][:, None, :].broadcast_to([128, 3, 129])
        ovv = prefall[:, (u * 4 + 1) * 129:(u + 1) * 4 * 129].rearrange("p (s c) -> p s c", c=129)
        ws = ([gsum_deps[u // 4]] + snap_deps) if u == 0 else [gsum_deps[u // 4]]
        vop(lambda v, _o=prefall[:, u * 4 * 129:(u * 4 + 1) * 129],
            _g=gsum[:, u * 129:(u + 1) * 129]: v.tensor_copy(_o, _g), waits=ws)
        pref_ready[u] = vop(lambda v, _o=ovv, _s=svv, _g=gvv: v.tensor_tensor(
            _o, _s, _g, ALU.add))
    if debug:
        dma("sync", dbg["snap"], snap[:, :], sST, waits=snap_deps + gsum_deps)
        dma("sync", dbg["prefall"], prefall[:, :], sST, waits=pref_ready)
        dma("sync", dbg["kf_fm"], kf_fm[:, :], sST, waits=p.marks([sV]))
        dma("sync", dbg["vext"], vext[:, :], sST, waits=p.marks([sSc]))
    p.gate(p.marks([sV]))
    es_snap.close()
    es_sn.close()

    es_scan = ExitStack()
    qft = [es_scan.enter_context(nc.sbuf_tensor(f"qft{i}", [128, 512], BF16)) for i in range(2)]
    qffm = es_scan.enter_context(nc.sbuf_tensor("qffm", [128, NTC * 512], BF16))
    mixs2 = [es_scan.enter_context(nc.sbuf_tensor(f"mixs{i}", [128, 4 * 512], BF16))
             for i in range(2)]
    Abb = [es_scan.enter_context(nc.sbuf_tensor(f"Ab{i}", [128, 128], BF16)) for i in range(4)]
    rcpb = es_scan.enter_context(nc.sbuf_tensor("rcpb", [128, 16], F32))

    prev_scan_end = None
    prev_mix_dma = None
    prev_mix_dma2 = [None, None]
    mixb_done = None
    for nb in range(4):
        if nb == 0:
            panel_deps = pre_panel_deps
            wq_buf = wpC
        else:
            wq_buf = wpA
            panel_deps = []
            for q4 in range(4):
                panel_deps.append(dma("gpsimd",
                    wpA[:, :].rearrange("p (dc n) -> p dc n", n=512)[:, q4 * 4:q4 * 4 + 4, :],
                    rows_fm(wqT, 2048)[:, q4 * 4:q4 * 4 + 4, nb * 512:(nb + 1) * 512],
                    sWL, waits=[wp_last_use.get("A")]))
        qffm_ready = [None] * NTC
        last_q_mm = None
        for tc in range(NTC):
            bkq = 4 if tc % 2 == 0 else 5
            for dc in range(16):
                last_q_mm = mm(bkq, 0, 512, xn_chunk(dc, tc), wq_buf[:, dc * 512:(dc + 1) * 512],
                               start=(dc == 0), stop=(dc == 15), waits=[panel_deps[dc // 4]])
            qf_dep = rope_elu(tc % 4, qft[tc % 2][:, :],
                              [prev_scan_end], bkq)
            bkt = 6 if tc % 2 == 0 else 7
            for hh in range(4):
                mm(bkt, hh * 128, (hh + 1) * 128,
                   qft[tc % 2][:, hh * 128:hh * 128 + 128],
                   ident[:, :], start=True, stop=(hh == 3),
                   waits=[qf_dep] if hh == 0 else [])
            qffm_ready[tc] = bank_read(bkt, "scalar",
                lambda sce, _b=bkt, _o=qffm[:, tc * 512:(tc + 1) * 512]:
                sce.copy(_o, pb[_b][:, :]))
        wp_last_use["A"] = (sPE, sPE.n)
        if debug and nb == 0:
            dma("sync", dbg["qf0"], qffm[:, :], sST, waits=[(sSc, sSc.n)])
        its = [(b, hh, sc_i) for b in range(B) for hh in range(4) for sc_i in range(NSC)]
        LOOK = 4
        mask_done = [None] * 64
        out_read = [None] * 64
        prev_md = [None] * 4
        mix_deps_b = [[] for _ in range(B)]
        dma_done_b = [None] * B
        for i in range(64 + LOOK):
            if i < 64:
                b, hh, sc_i = its[i]
                tc = b * 4 + sc_i
                u = b * 4 + nb
                qv = qffm[:, tc * 512 + hh * 128:tc * 512 + hh * 128 + 128]
                bka = 4 + i % 4
                mm(bka, 0, 128, kf_fm[:, u * SC + sc_i * 128:u * SC + (sc_i + 1) * 128],
                   qv, start=True, stop=True,
                   waits=[qffm_ready[tc], kfm_ready[tc]])
                mask_done[i] = bank_read(bka, "vector",
                    lambda v, _o=Abb[i % 4], _b=bka: v.tensor_tensor(
                        _o[:, :], pb[_b][:, 0:128], triuT[:, :], ALU.mult),
                    waits=[out_read[i - 4] if i >= 4 else None])
            j = i - LOOK
            if j >= 0:
                b, hh, sc_i = its[j]
                tc = b * 4 + sc_i
                u = b * 4 + nb
                qv = qffm[:, tc * 512 + hh * 128:tc * 512 + hh * 128 + 128]
                bko = j % 4
                out_read[j] = mm(bko, 0, 129, Abb[j % 4][:, :],
                   vext[:, tc * 516 + nb * 129:tc * 516 + nb * 129 + 129],
                   start=True, stop=False, waits=[mask_done[j], vext_ready[tc]])
                mm(bko, 0, 129, qv,
                   prefall[:, (u * 4 + sc_i) * 129:(u * 4 + sc_i) * 129 + 129],
                   start=False, stop=True, waits=[pref_ready[u]])
                r0, r1 = 2 * (j % 4), 2 * (j % 4) + 1
                bank_read(bko, "vector",
                          lambda v, _b=bko, _r=r0: v.tensor_scalar(
                              rcpb[:, _r:_r + 1], pb[_b][:, 128:129], epsb[:, 1:2],
                              None, ALU.add),
                          waits=[prev_md[j % 4]])
                p.emit("vector", lambda v: v.drain())
                rd = vopd(lambda v, _a=r0, _c=r1: v.reciprocal(
                    rcpb[:, _c:_c + 1], rcpb[:, _a:_a + 1]))
                md = bank_read(bko, "scalar",
                               lambda sce, _b=bko, _c=r1,
                               _o=mixs2[b % 2][:, sc_i * 512 + hh * 128:sc_i * 512 + hh * 128 + 128]:
                               sce.mul(_o, pb[_b][:, 0:128], rcpb[:, _c:_c + 1]),
                               waits=[rd, prev_mix_dma2[b % 2]])
                prev_md[j % 4] = md
                mix_deps_b[b].append(md)
                if len(mix_deps_b[b]) == 16:
                    mixs = mixs2[b % 2]
                    mv = mixs[:, :].rearrange("p (t c) -> p t c", c=512)
                    ov = mixb.rearrange("(t p) c -> p t c", p=128)[
                        :, b * 4:b * 4 + 4, nb * 512:(nb + 1) * 512]
                    prev_mix_dma2[b % 2] = dma("sync", ov, mv, sST, waits=mix_deps_b[b])
        prev_scan_end = (sPE, sPE.n)
        prev_mix_dma = prev_mix_dma2[1]
    mixb_done = (sST, sST.n)
    if debug:
        dma("sync", dbg["mixb"], mixb, sST, waits=[mixb_done])
    p.gate(p.marks(allsems))
    es_scan.close()
    es_attn.close()

    es_ffn = ExitStack()
    fallo = lambda name, shape, dt=BF16: es_ffn.enter_context(nc.sbuf_tensor(name, shape, dt))
    qs4 = fallo("qs4", [128, B * 1024])
    ks4 = fallo("ks4", [128, B * 1024])
    vs4 = fallo("vs4", [128, B * 2048])

    onrm = fallo("onrm", [128, B * 2048])
    slt = [fallo(f"slt{i}", [128, 2048]) for i in range(3)]
    lgs = fallo("lgs", [128, B * 32], F32)
    sml = junk[:, 0:24 * 16]
    mt4 = junk[:, 24 * 16:24 * 16 + 160]
    dots = junk[:, 24 * 16 + 160:24 * 16 + 320]
    qkc = fallo("qkc", [128, 160])
    sgt = fallo("sgt", [128, 8 * 512])
    cva = [fallo(f"cva{i}", [128, 2048]) for i in range(3)]
    ol4 = cva[2]
    ropeF_tab = None
    opg = [fallo(f"opg{i}", [128, 512]) for i in range(2)]
    ft1 = slt[0][:, 0:1024]
    ft2 = slt[2][:, 0:1024]

    def zv(l): return sml[:, (0 + l) * 16:(0 + l) * 16 + 16]
    def pmv(l): return sml[:, (4 + l) * 16:(4 + l) * 16 + 16]
    def cfv(l): return sml[:, (8 + l) * 16:(8 + l) * 16 + 16]
    SM_DIFF, SM_AB, SM_T1, SM_T2, SM_T3, SM_SO, SM_SO2, SM_M, SM_VAR, SM_SD, SM_RS = \
        [(12 + i) * 16 for i in range(11)]
    def sm(off): return sml[:, off:off + 16]

    PAIRS = [(l, m) for l in range(B) for m in range(l + 1)]
    PIDX = {pm_: i for i, pm_ in enumerate(PAIRS)}
    def mtv(pi): return mt4[:, pi * 16:(pi + 1) * 16]
    def dtv(pi): return dots[:, pi * 16:(pi + 1) * 16]
    def qkv(pi): return qkc[:, pi * 16:(pi + 1) * 16]

    c1b = cvws[:, 16:32][:, :, None].broadcast_to([128, 16, 128])
    c0b = cvws[:, 0:16][:, :, None].broadcast_to([128, 16, 128])
    xn3 = xn[:, :].rearrange("p (dc t) -> p dc t", t=T)

    prev_opg_dma = [None, None]
    prev_og = None
    panel_rr = [0]
    wp_last_use["C"] = wp_last_use["A"]
    s4bk = [0]

    def nxbk():
        b = s4bk[0] % 8
        s4bk[0] += 1
        return b

    def emit_conv(g):
        tcs = [l * 4 + g for l in range(B)]
        xc_dep = [None] * (B + 1)
        cA = cva[0][:, :].rearrange("p (dc t) -> p dc t", t=128)
        cB = cva[1][:, :].rearrange("p (dc t) -> p dc t", t=128)
        cC = cva[2][:, :].rearrange("p (dc t) -> p dc t", t=128)
        for l in range(B - 1, -1, -1):
            xl = xn3[:, :, tcs[l] * 128:(tcs[l] + 1) * 128]
            v1 = vop(lambda v, _x=xl: v.tensor_tensor(cA, _x, c1b, ALU.mult),
                     waits=[xc_dep[l + 1], conv_free[0]])
            if l > 0:
                xp = xn3[:, :, tcs[l - 1] * 128:(tcs[l - 1] + 1) * 128]
                vop(lambda v, _x=xp: v.tensor_tensor(cB, _x, c0b, ALU.mult))
                pre_dep = vop(lambda v: v.tensor_tensor(cC, cA, cB, ALU.add))
                src_ = cC
            else:
                pre_dep = v1
                src_ = cA
            xc_dep[l] = sop(lambda sce, _o=xl, _s=src_: sce.activation(_o, _s, AF.Silu),
                            waits=[pre_dep])
        conv_free[0] = xc_dep[0]
        return xc_dep

    def emit_panels(g, wd, n_out, npan, evac, xc_dep, extra_evac_wait):
        tcs = [l * 4 + g for l in range(B)]
        deps = []
        for pp in range(npan):
            key = ["A", "B", "C"][panel_rr[0] % 3]
            wp = {"A": wpA, "B": wpB, "C": wpC}[key]
            panel_rr[0] += 1
            pdep = []
            for q4 in range(4):
                pdep.append(dma("gpsimd",
                    wp[:, :].rearrange("p (dc n) -> p dc n", n=512)[:, q4 * 4:q4 * 4 + 4, :],
                    rows_fm(wd, n_out)[:, q4 * 4:q4 * 4 + 4, pp * 512:(pp + 1) * 512],
                    sWL, waits=[wp_last_use.get(key)]))
            for l in range(B):
                bk = nxbk()
                for dc in range(16):
                    mm(bk, 0, 512, xn_chunk(dc, tcs[l]), wp[:, dc * 512:(dc + 1) * 512],
                       start=(dc == 0), stop=(dc == 15),
                       waits=[pdep[dc // 4], xc_dep[l]])
                deps.append(evac(pp, l, bk, extra_evac_wait))
            wp_last_use[key] = (sPE, sPE.n)
        return deps

    def emit_gwave(g, pps, xc_dep, sg_ready, gate_slot):
        tcs = [l * 4 + g for l in range(B)]
        for pp in pps:
            key = ["A", "B", "C"][panel_rr[0] % 3]
            wp = {"A": wpA, "B": wpB, "C": wpC}[key]
            panel_rr[0] += 1
            pdep = []
            for q4 in range(4):
                pdep.append(dma("gpsimd",
                    wp[:, :].rearrange("p (dc n) -> p dc n", n=512)[:, q4 * 4:q4 * 4 + 4, :],
                    rows_fm(fwg, 2048)[:, q4 * 4:q4 * 4 + 4, pp * 512:(pp + 1) * 512],
                    sWL, waits=[wp_last_use.get(key)]))
            for l in range(B):
                ti = pp * B + l
                bk = nxbk()
                for dc in range(16):
                    mm(bk, 0, 512, xn_chunk(dc, tcs[l]), wp[:, dc * 512:(dc + 1) * 512],
                       start=(dc == 0), stop=(dc == 15), waits=[pdep[dc // 4], xc_dep[l]])
                sg_ready[(pp, l)] = bank_read(bk, "scalar",
                    lambda sce, _b=bk, _o=sgt[:, (ti % 8) * 512:(ti % 8) * 512 + 512]:
                    sce.activation(_o, pb[_b][:, :], AF.Silu),
                    waits=[gate_slot.get(ti % 8)])
            wp_last_use[key] = (sPE, sPE.n)

    def emit_proj(g, xc_dep, gate_slot, dots_prev, micro_prev):
        tcs = [l * 4 + g for l in range(B)]
        q_deps = emit_panels(g, fwq, 1024, 2, lambda pp, l, bk, w: bank_read(bk, "scalar",
            lambda sce, _o=qs4[:, l * 1024 + pp * 512:l * 1024 + pp * 512 + 512], _b=bk:
            sce.activation(_o, pb[_b][:, :], AF.Silu), waits=[w]), xc_dep, dots_prev)
        k_deps = emit_panels(g, fwk, 1024, 2, lambda pp, l, bk, w: bank_read(bk, "scalar",
            lambda sce, _o=ks4[:, l * 1024 + pp * 512:l * 1024 + pp * 512 + 512], _b=bk:
            sce.copy(_o, pb[_b][:, :]), waits=[w]), xc_dep, dots_prev)
        lg_deps = []
        for l in range(B):
            bk = nxbk()
            for dc in range(16):
                mm(bk, 0, 32, xn_chunk(dc, tcs[l]), wvr[:, dc * 32:(dc + 1) * 32],
                   start=(dc == 0), stop=(dc == 15), waits=[xc_dep[l]])
            lg_deps.append(bank_read(bk, "scalar",
                lambda sce, _o=lgs[:, l * 32:(l + 1) * 32], _b=bk:
                sce.copy(_o, pb[_b][:, 0:32]), waits=[dots_prev]))
        v_deps = emit_panels(g, fwv, 2048, 4, lambda pp, l, bk, w: bank_read(bk, "scalar",
            lambda sce, _o=vs4[:, l * 2048 + pp * 512:l * 2048 + pp * 512 + 512], _b=bk:
            sce.copy(_o, pb[_b][:, :]), waits=[w]), xc_dep, micro_prev)
        sg_ready = {}
        emit_gwave(g, [0, 1], xc_dep, sg_ready, gate_slot)
        return q_deps, k_deps, lg_deps, v_deps, sg_ready

    def emit_rope_coeff_dots(g, q_deps, k_deps, lg_deps):
        rope4_deps = [None] * B
        for l in range(B):
            cb1 = cosF[:, l * 64:(l + 1) * 64][:, None, :].broadcast_to([128, 16, 64])
            f1 = ft1[:, :].rearrange("p (h d) -> p h d", d=64)
            f2 = ft2[:, :].rearrange("p (h d) -> p h d", d=64)
            slo = sinFs[:, l * 64:l * 64 + 32][:, None, :].broadcast_to([128, 16, 32])
            shi = sinFs[:, l * 64 + 32:l * 64 + 64][:, None, :].broadcast_to([128, 16, 32])
            for buf, dps in [(qs4, q_deps), (ks4, k_deps)]:
                bv = buf[:, l * 1024:(l + 1) * 1024].rearrange("p (h d) -> p h d", d=64)
                vop(lambda v, _b=bv, _c=cb1, _f=f1: v.tensor_tensor(_f, _b, _c, ALU.mult),
                    waits=list(dps))
                vop(lambda v, _b=bv, _c=slo, _f=f2: v.tensor_tensor(
                    _f[:, :, 0:32], _b[:, :, 32:64], _c, ALU.mult))
                vop(lambda v, _b=bv, _c=shi, _f=f2: v.tensor_tensor(
                    _f[:, :, 32:64], _b[:, :, 0:32], _c, ALU.mult))
                rope4_deps[l] = vop(lambda v, _o=buf[:, l * 1024:(l + 1) * 1024]:
                                    v.tensor_tensor(_o, ft1[:, :], ft2[:, :], ALU.add))
        for l in range(B):
            lg = lgs[:, l * 32:(l + 1) * 32].rearrange("p (h r) -> p h r", r=2)
            vopd(lambda v, _l=lg: v.tensor_tensor(sm(SM_DIFF), _l[:, :, 0], _l[:, :, 1],
                                                 ALU.subtract), waits=[lg_deps[l]])
            vopd(lambda v, _z=zv(l): v.tensor_scalar(_z, sm(SM_DIFF), 0.0, None, ALU.is_ge))
            ab_dep = sopd(lambda sce: sce.activation(sm(SM_AB), sm(SM_DIFF), AF.Abs),
                         waits=[(sV, sV.n)])
            pm_dep = sopd(lambda sce, _o=pmv(l): sce.activation(_o, sm(SM_AB), AF.Sigmoid))
            if l == 0:
                vopd(lambda v, _o=cfv(0), _p=pmv(0): v.tensor_scalar(
                    _o, _p, 0.125, None, ALU.mult), waits=[pm_dep])
            else:
                for m_ in range(l):
                    pi = PIDX[(l, m_)]
                    vopd(lambda v, _a=zv(l), _b=zv(m_): v.tensor_tensor(sm(SM_T1), _a, _b, ALU.mult))
                    vopd(lambda v, _a=zv(l), _b=zv(m_): v.tensor_tensor(sm(SM_T2), _a, _b, ALU.add))
                    vopd(lambda v: v.tensor_scalar(sm(SM_T3), sm(SM_T1), 2.0, 1.0, ALU.mult, ALU.add))
                    vopd(lambda v, _o=mtv(pi): v.tensor_tensor(_o, sm(SM_T3), sm(SM_T2), ALU.subtract))
                vopd(lambda v, _m=mtv(PIDX[(l, 0)]): v.tensor_scalar(sm(SM_T1), _m, 1.0, None, ALU.add))
                cur = SM_T1
                for m_ in range(1, l):
                    nxt = SM_T2 if cur == SM_T1 else SM_T1
                    vopd(lambda v, _m=mtv(PIDX[(l, m_)]), _c=cur, _n=nxt:
                        v.tensor_tensor(sm(_n), sm(_c), _m, ALU.add))
                    cur = nxt
                vopd(lambda v, _c=cur: v.reciprocal(sm(SM_T3), sm(_c)))
                vopd(lambda v, _p=pmv(l): v.tensor_tensor(sm(SM_T1), _p, sm(SM_T3), ALU.mult),
                    waits=[pm_dep])
                vopd(lambda v, _o=cfv(l): v.tensor_scalar(_o, sm(SM_T1), 0.125, None, ALU.mult))
        for l in range(B):
            for m_ in range(l + 1):
                pi = PIDX[(l, m_)]
                qv4 = qs4[:, l * 1024:(l + 1) * 1024].rearrange("p (h d) -> p h d", d=64)
                kv4 = ks4[:, m_ * 1024:(m_ + 1) * 1024].rearrange("p (h d) -> p h d", d=64)
                f1 = ft1[:, :].rearrange("p (h d) -> p h d", d=64)
                vop(lambda v, _q=qv4, _k=kv4, _f=f1: v.tensor_tensor(_f, _q, _k, ALU.mult),
                    waits=[rope4_deps[l], rope4_deps[m_]])
                vopd(lambda v, _o=dtv(pi), _f=f1: v.tensor_reduce(_o, _f, AX.X, ALU.add))
                if m_ < l:
                    vopd(lambda v, _d=dtv(pi), _m=mtv(pi): v.tensor_tensor(
                        sm(SM_T1), _d, _m, ALU.mult))
                    vopd(lambda v, _o=qkv(pi), _c=cfv(l): v.tensor_tensor(
                        _o, sm(SM_T1), _c, ALU.mult))
                else:
                    vopd(lambda v, _o=qkv(pi), _d=dtv(pi), _c=cfv(l): v.tensor_tensor(
                        _o, _d, _c, ALU.mult))

    def emit_micro_ln(g, v_deps, conv_sil, sg_ready, gate_slot):
        tcs = [l * 4 + g for l in range(B)]
        o3 = ol4[:, :].rearrange("p (h d) -> p h d", d=128)
        s0 = slt[0][:, :].rearrange("p (h d) -> p h d", d=128)
        s1 = slt[1][:, :].rearrange("p (h d) -> p h d", d=128)
        s2 = slt[2][:, :].rearrange("p (h d) -> p h d", d=128)
        c3 = onrm[:, 3 * 2048:4 * 2048].rearrange("p (h d) -> p h d", d=128)
        def qkb(pi):
            return qkc[:, pi * 16:(pi + 1) * 16].rearrange("p (h o) -> p h o", o=1) \
                .broadcast_to([128, 16, 128])
        def vv4(m_):
            return vs4[:, m_ * 2048:(m_ + 1) * 2048].rearrange("p (h d) -> p h d", d=128)
        sq_dep = None
        for l in range(B):
            vw = list(v_deps)
            if l == 0:
                odep = vop(lambda v, _v=vv4(0), _q=qkb(PIDX[(0, 0)]):
                           v.tensor_tensor(o3, _v, _q, ALU.mult),
                           waits=vw + [sq_dep, conv_sil])
            elif l == 1:
                vop(lambda v, _v=vv4(0), _q=qkb(PIDX[(1, 0)]):
                    v.tensor_tensor(s0, _v, _q, ALU.mult), waits=vw)
                vop(lambda v, _v=vv4(1), _q=qkb(PIDX[(1, 1)]):
                    v.tensor_tensor(s1, _v, _q, ALU.mult))
                odep = vop(lambda v: v.tensor_tensor(o3, s0, s1, ALU.add),
                           waits=[sq_dep, conv_sil])
            elif l == 2:
                vop(lambda v, _v=vv4(0), _q=qkb(PIDX[(2, 0)]):
                    v.tensor_tensor(s0, _v, _q, ALU.mult), waits=vw)
                vop(lambda v, _v=vv4(1), _q=qkb(PIDX[(2, 1)]):
                    v.tensor_tensor(s1, _v, _q, ALU.mult))
                vop(lambda v: v.tensor_tensor(s2, s0, s1, ALU.add))
                vop(lambda v, _v=vv4(2), _q=qkb(PIDX[(2, 2)]):
                    v.tensor_tensor(s0, _v, _q, ALU.mult))
                odep = vop(lambda v: v.tensor_tensor(o3, s2, s0, ALU.add),
                           waits=[sq_dep, conv_sil])
            else:
                vop(lambda v, _v=vv4(0), _q=qkb(PIDX[(3, 0)]):
                    v.tensor_tensor(s0, _v, _q, ALU.mult), waits=vw)
                vop(lambda v, _v=vv4(1), _q=qkb(PIDX[(3, 1)]):
                    v.tensor_tensor(s1, _v, _q, ALU.mult))
                vop(lambda v: v.tensor_tensor(s2, s0, s1, ALU.add))
                vop(lambda v, _v=vv4(2), _q=qkb(PIDX[(3, 2)]):
                    v.tensor_tensor(s0, _v, _q, ALU.mult))
                vop(lambda v, _v=vv4(3), _q=qkb(PIDX[(3, 3)]):
                    v.tensor_tensor(s1, _v, _q, ALU.mult))
                vop(lambda v: v.tensor_tensor(c3, s0, s1, ALU.add))
                odep = vop(lambda v: v.tensor_tensor(o3, s2, c3, ALU.add),
                           waits=[sq_dep, conv_sil])
            vopd(lambda v: v.tensor_reduce(sm(SM_SO), o3, AX.X, ALU.add))
            vop(lambda v: v.tensor_tensor(s1, o3, o3, ALU.mult))
            vopd(lambda v: v.tensor_reduce(sm(SM_SO2), s1, AX.X, ALU.add))
            sq_dep = (sV, sV.n)
            vopd(lambda v: v.tensor_scalar(sm(SM_M), sm(SM_SO), 1.0 / 128, None, ALU.mult))
            vopd(lambda v: v.tensor_scalar(sm(SM_T1), sm(SM_SO2), 1.0 / 128, None, ALU.mult))
            vopd(lambda v: v.tensor_tensor(sm(SM_T2), sm(SM_M), sm(SM_M), ALU.mult))
            vopd(lambda v: v.tensor_tensor(sm(SM_VAR), sm(SM_T1), sm(SM_T2), ALU.subtract))
            vopd(lambda v: v.tensor_scalar(sm(SM_SD), sm(SM_VAR), epsb[:, 2:3], None, ALU.add))
            smi = sml.bitcast(I32)
            def smiv(off): return smi[:, off:off + 16]
            vopd(lambda v: v.tensor_scalar(smiv(SM_T1), smiv(SM_SD), 1, None, ALU.arith_shift_right))
            vopd(lambda v: v.tensor_scalar(smiv(SM_T2), smiv(SM_T1), 0x5f3759df, None, ALU.subtract))
            vopd(lambda v: v.tensor_scalar(smiv(SM_T3), smiv(SM_T2), -1, None, ALU.mult))
            vopd(lambda v: v.tensor_tensor(sm(SM_T1), sm(SM_T3), sm(SM_T3), ALU.mult))
            vopd(lambda v: v.tensor_tensor(sm(SM_T2), sm(SM_SD), sm(SM_T1), ALU.mult))
            vopd(lambda v: v.tensor_scalar(sm(SM_T1), sm(SM_T2), -0.5, 1.5, ALU.mult, ALU.add))
            vopd(lambda v: v.tensor_tensor(sm(SM_T2), sm(SM_T3), sm(SM_T1), ALU.mult))
            vopd(lambda v: v.tensor_tensor(sm(SM_T1), sm(SM_T2), sm(SM_T2), ALU.mult))
            vopd(lambda v: v.tensor_tensor(sm(SM_T3), sm(SM_SD), sm(SM_T1), ALU.mult))
            vopd(lambda v: v.tensor_scalar(sm(SM_T1), sm(SM_T3), -0.5, 1.5, ALU.mult, ALU.add))
            vopd(lambda v: v.tensor_tensor(sm(SM_RS), sm(SM_T2), sm(SM_T1), ALU.mult))
            mb_ = sm(SM_M)[:, :, None].broadcast_to([128, 16, 128])
            rb_ = sm(SM_RS)[:, :, None].broadcast_to([128, 16, 128])
            vop(lambda v, _m=mb_: v.tensor_tensor(s0, o3, _m, ALU.subtract))
            vop(lambda v, _r=rb_,
                _o=onrm[:, l * 2048:(l + 1) * 2048].rearrange("p (h d) -> p h d", d=128):
                v.tensor_tensor(_o, s0, _r, ALU.mult))
            onrm_l = (sV, sV.n)
            for pp in range(2):
                ti = pp * B + l
                oi = ti % 2
                og_dep = vop(lambda v, _o=opg[oi],
                             _s=sgt[:, (ti % 8) * 512:(ti % 8) * 512 + 512],
                             _n=onrm[:, l * 2048 + pp * 512:l * 2048 + pp * 512 + 512]:
                             v.tensor_tensor(_o[:, :], _s, _n, ALU.mult),
                             waits=[sg_ready[(pp, l)], onrm_l, prev_opg_dma[oi]])
                gate_slot[ti % 8] = og_dep
                prev_opg_dma[oi] = dma("sync",
                    outpb[tcs[l] * 128:(tcs[l] + 1) * 128, pp * 512:(pp + 1) * 512],
                    opg[oi][:, :], sST, waits=[og_dep])
        return (sV, sV.n)

    def emit_gates_wave2(g, sg_ready, onrm_mark, gate_slot):
        tcs = [l * 4 + g for l in range(B)]
        for pp in (2, 3):
            for l in range(B):
                ti = pp * B + l
                oi = ti % 2
                og_dep = vop(lambda v, _o=opg[oi],
                             _s=sgt[:, (ti % 8) * 512:(ti % 8) * 512 + 512],
                             _n=onrm[:, l * 2048 + pp * 512:l * 2048 + pp * 512 + 512]:
                             v.tensor_tensor(_o[:, :], _s, _n, ALU.mult),
                             waits=[sg_ready[(pp, l)], onrm_mark, prev_opg_dma[oi]])
                gate_slot[ti % 8] = og_dep
                prev_opg_dma[oi] = dma("sync",
                    outpb[tcs[l] * 128:(tcs[l] + 1) * 128, pp * 512:(pp + 1) * 512],
                    opg[oi][:, :], sST, waits=[og_dep])

    conv_free = [None]
    xc_dep_cur = emit_conv(0)
    gate_slot = {}
    dots_prev = None
    micro_prev = None
    for g in range(4):
        xc_this = xc_dep_cur
        q_deps, k_deps, lg_deps, v_deps, sg_ready = emit_proj(
            g, xc_this, gate_slot, dots_prev, micro_prev)
        emit_rope_coeff_dots(g, q_deps, k_deps, lg_deps)
        dots_prev = (sV, sV.n)
        if g < 3:
            xc_dep_cur = emit_conv(g + 1)
        micro_prev = emit_micro_ln(g, v_deps, xc_dep_cur[0] if g < 3 else None,
                                   sg_ready, gate_slot)
        emit_gwave(g, [2, 3], xc_this, sg_ready, gate_slot)
        emit_gates_wave2(g, sg_ready, micro_prev, gate_slot)
        if debug and g == 0:
            gm = p.marks(allsems)
            for nmm, buf in [("logits", lgs), ("qs4", qs4), ("ks4", ks4), ("vs4", vs4),
                             ("onrm", onrm), ("dots", dots), ("qkc", qkc)]:
                dma("sync", dbg[nmm], buf[:, :], sST, waits=gm)
    if debug:
        dma("sync", dbg["xc"], xn[:, :], sST, waits=p.marks(allsems))
        dma("sync", dbg["outpb"], outpb, sST, waits=p.marks(allsems))

    p.gate(p.marks(allsems))
    es_ffn.close()
    es_w.close()
    es_xn.close()

    es_s5 = ExitStack()
    s5a = lambda name, shape, dt=BF16: es_s5.enter_context(nc.sbuf_tensor(name, shape, dt))
    w5o = s5a("w5o", [128, 16 * 2048])
    w5f = s5a("w5f", [128, 16 * 2048])
    mtm = s5a("mtm", [128, 2048])
    otm = s5a("otm", [128, 2048])
    mfm = s5a("mfm", [128, 2048])
    ofm = s5a("ofm", [128, 2048])
    xb5 = [s5a(f"xb5{i}", [128, D], F32) for i in range(2)]
    outs = [s5a(f"outs{i}", [128, D], F32) for i in range(2)]

    w5p_deps = []
    for pn in range(4):
        d1 = dma("gpsimd",
                 w5o[:, pn * 8192:(pn + 1) * 8192].rearrange("p (mc n) -> p mc n", n=512),
                 rows_fm(woT, 2048)[:, :, pn * 512:(pn + 1) * 512], sWL)
        d2 = dma("gpsimd",
                 w5f[:, pn * 8192:(pn + 1) * 8192].rearrange("p (mc n) -> p mc n", n=512),
                 rows_fm(fwout, 2048)[:, :, pn * 512:(pn + 1) * 512], sWL)
        w5p_deps.append([d1, d2])

    out_deps = []
    prev_fm = None
    prev_ev = None
    for tc in range(NTC):
        mld = dma("sync", mtm[:, :], mixb[tc * 128:(tc + 1) * 128, :], sST,
                  waits=[mixb_done, prev_fm])
        old = dma("sync", otm[:, :], outpb[tc * 128:(tc + 1) * 128, :], sST,
                  waits=[prev_opg_dma[0], prev_opg_dma[1], prev_fm])
        xld = dma("sync", xb5[tc % 2][:, :], x[tc * 128:(tc + 1) * 128, :], sXL,
                  waits=[prev_ev])
        fm_deps = []
        for src, ldep, dst, bks in [(mtm, mld, mfm, (4, 5)), (otm, old, ofm, (6, 7))]:
            for halfq in range(4):
                bk = bks[halfq % 2]
                for k in range(4):
                    mcc = halfq * 4 + k
                    mm(bk, k * 128, (k + 1) * 128, src[:, mcc * 128:(mcc + 1) * 128],
                       ident[:, :], start=True, stop=(k == 3), waits=[ldep])
                pv = pb[bk][:, :].rearrange("p (k t) -> p k t", t=128)
                dv = dst[:, :].rearrange("p (mc t) -> p mc t", t=128)[
                    :, halfq * 4:halfq * 4 + 4, :]
                fm_deps.append(bank_read(bk, "vector",
                    lambda v, _o=dv, _i=pv: v.tensor_copy(_o, _i)))
        prev_fm = fm_deps[-1]
        ev_deps = []
        for pn in range(4):
            bko = pn % 2
            for mc in range(16):
                mm(bko, 0, 512, mfm[:, mc * 128:(mc + 1) * 128],
                   w5o[:, pn * 8192 + mc * 512:pn * 8192 + mc * 512 + 512],
                   start=(mc == 0), stop=False,
                   waits=(w5p_deps[pn] + fm_deps) if mc == 0 else [])
            for mc in range(16):
                mm(bko, 0, 512, ofm[:, mc * 128:(mc + 1) * 128],
                   w5f[:, pn * 8192 + mc * 512:pn * 8192 + mc * 512 + 512],
                   start=False, stop=(mc == 15))
            ev_deps.append(bank_read(bko, "vector",
                lambda v, _b=bko, _o=outs[tc % 2][:, pn * 512:pn * 512 + 512],
                _x=xb5[tc % 2][:, pn * 512:pn * 512 + 512]:
                v.scalar_tensor_tensor(_o, pb[_b][:, :], 1.0, _x, ALU.mult, ALU.add),
                waits=[xld, out_deps[tc - 2] if tc >= 2 else None]))
        prev_ev = ev_deps[-1]
        out_deps.append(dma("sync", out[tc * 128:(tc + 1) * 128, :], outs[tc % 2][:, :],
                            sST, waits=ev_deps))

    p.emit("sync", None, waits=[out_deps[-1]])

    with nc.Block() as block:
        p.flush(block)

    es_s5.close()
    es.close()
    return nc, dbg



def kernel(**inputs):
    in_maps = _shard_inputs(inputs)
    nc, _ = build_nc(debug=False)
    res = run_bass_kernel_spmd(nc, in_maps, core_ids=list(range(NCORES)))
    outf = np.empty((B, S, D), np.float32)
    for c in range(NCORES):
        outf[:, c * SC:(c + 1) * SC, :] = np.asarray(res.results[c]["out"]).reshape(B, SC, D)
    return outf


# version 10
# speedup vs baseline: 1.1443x; 1.0567x over previous
import numpy as np
import ml_dtypes
from contextlib import ExitStack
import concourse.bass as bass
import concourse.mybir as mybir
from concourse.bass_utils import run_bass_kernel_spmd

F32 = mybir.dt.float32
BF16 = mybir.dt.bfloat16
AF = mybir.ActivationFunctionType
ALU = mybir.AluOpType
AX = mybir.AxisListType
I32 = mybir.dt.int32

NCORES = 8
B = 4
S = 4096
D = 2048
SC = S // NCORES
T = B * SC
NTC = T // 128
NSC = SC // 128
EPS_LN = 1e-5
EPS_ATTN = 1e-6
EPS_FFN = 1e-5



def _host_tables(core_id):
    t = {}
    t["ident"] = np.eye(128, dtype=ml_dtypes.bfloat16)
    t["triuT"] = np.triu(np.ones((128, 128), np.float32)).astype(ml_dtypes.bfloat16)
    invA = 10000.0 ** (-np.arange(64, dtype=np.float64) / 64.0)
    cosA = np.zeros((128, NSC * 128), np.float64)
    sinAs = np.zeros((128, NSC * 128), np.float64)
    for sc in range(NSC):
        s_glob = core_id * SC + sc * 128 + np.arange(128)
        ang = s_glob[:, None] * invA[None, :]
        c, sn = np.cos(ang), np.sin(ang)
        cosA[:, sc * 128:sc * 128 + 64] = c
        cosA[:, sc * 128 + 64:sc * 128 + 128] = c
        sinAs[:, sc * 128:sc * 128 + 64] = -sn
        sinAs[:, sc * 128 + 64:sc * 128 + 128] = sn
    t["cosA"] = cosA.astype(ml_dtypes.bfloat16)
    t["sinAs"] = sinAs.astype(ml_dtypes.bfloat16)
    invF = 10000.0 ** (-np.arange(32, dtype=np.float64) / 32.0)
    cosF = np.zeros((128, B * 64), np.float64)
    sinFs = np.zeros((128, B * 64), np.float64)
    for l in range(B):
        ang = l * invF
        c, sn = np.cos(ang), np.sin(ang)
        cosF[:, l * 64:l * 64 + 32] = c[None, :]
        cosF[:, l * 64 + 32:l * 64 + 64] = c[None, :]
        sinFs[:, l * 64:l * 64 + 32] = -sn[None, :]
        sinFs[:, l * 64 + 32:l * 64 + 64] = sn[None, :]
    t["cosF"] = cosF.astype(ml_dtypes.bfloat16)
    t["sinFs"] = sinFs.astype(ml_dtypes.bfloat16)
    cb = np.zeros((128, B * 1024), np.float64)
    sb = np.zeros((128, B * 1024), np.float64)
    for l in range(B):
        for h in range(16):
            cb[:, l * 1024 + h * 64:l * 1024 + h * 64 + 64] = cosF[:, l * 64:(l + 1) * 64]
            sb[:, l * 1024 + h * 64:l * 1024 + h * 64 + 64] = sinFs[:, l * 64:(l + 1) * 64]
    t["cosFb"] = cb.astype(ml_dtypes.bfloat16)
    t["sinFb"] = sb.astype(ml_dtypes.bfloat16)
    m = np.zeros((128, NCORES), np.float32)
    m[:, :core_id] = 1.0
    t["maskc"] = m
    e = np.zeros((128, 4), np.float32)
    e[:, 0] = EPS_LN
    e[:, 1] = EPS_ATTN
    e[:, 2] = EPS_FFN
    t["epsb"] = e
    return t


def _shard_inputs(inputs):
    x = np.asarray(inputs["L0_attn_x"], np.float32)
    c32 = lambda a: np.ascontiguousarray(np.asarray(a, np.float32))
    wqT = c32(np.asarray(inputs["L0_attn_Wq"], np.float32).T)
    wkT = c32(np.asarray(inputs["L0_attn_Wk"], np.float32).T)
    wvT = c32(np.asarray(inputs["L0_attn_Wv"], np.float32).T)
    woT = c32(np.asarray(inputs["L0_attn_Wo"], np.float32).T)
    fwq = c32(inputs["L0_ffn_Wq"])
    fwk = c32(inputs["L0_ffn_Wk"])
    fwv = c32(inputs["L0_ffn_Wv"])
    fwvT = c32(np.asarray(inputs["L0_ffn_Wv"], np.float32).T)
    fwg = c32(inputs["L0_ffn_Wg"])
    fwout = c32(inputs["L0_ffn_Wout"])
    cw = np.asarray(inputs["L0_ffn_conv_w"], np.float32)
    cvw = np.zeros((128, 32), np.float32)
    cvw[:, 0:16] = cw[:, 0].reshape(16, 128).T
    cvw[:, 16:32] = cw[:, 1].reshape(16, 128).T
    rw = np.asarray(inputs["L0_ffn_router_w"], np.float32)
    rwfm = np.zeros((128, 32), np.float32)
    for h in range(16):
        rwfm[:, 2 * h:2 * h + 2] = rw[h]
    in_maps = []
    for c in range(NCORES):
        xs = np.ascontiguousarray(x[:, c * SC:(c + 1) * SC, :].reshape(T, D))
        m = {"x": xs, "wqT": wqT, "wkT": wkT, "wvT": wvT, "woT": woT,
             "fwq": fwq, "fwk": fwk, "fwv": fwv, "fwvT": fwvT, "fwg": fwg,
             "fwout": fwout, "cvw": cvw, "rwfm": rwfm}
        ht = _host_tables(c)
        m.update({("d_" + k if k in ("ident", "triuT", "cosA", "sinAs", "cosF", "sinFs",
                                     "cosFb", "sinFb") else k): v
                  for k, v in ht.items()})
        in_maps.append(m)
    return in_maps



ENGINES = ("tensor", "vector", "scalar", "gpsimd", "sync")


class Sem:
    def __init__(self, handle, owner=None):
        self.h = handle
        self.n = 0
        self.owner = owner


class Prog:
    def __init__(self):
        self.ops = {e: [] for e in ENGINES}
        self.maxwait = {e: {} for e in ENGINES}

    def emit(self, eng, fn, waits=(), inc=None, inck=1):
        ws = []
        for w in waits:
            if w is None:
                continue
            sem, cnt = w
            if cnt <= 0 or sem.owner == eng:
                continue
            prev = self.maxwait[eng].get(id(sem), -1)
            if cnt > prev:
                self.maxwait[eng][id(sem)] = cnt
                ws.append((sem, cnt))
        dep = None
        if inc is not None:
            inc.n += inck
            dep = (inc, inc.n)
        self.ops[eng].append((fn, ws, inc, inck))
        return dep

    def marks(self, sems):
        return [(s, s.n) for s in sems]

    def gate(self, marks):
        for e in ENGINES:
            self.emit(e, None, waits=marks)

    def flush(self, block):
        for e in ENGINES:
            ops = self.ops[e]
            def body(eng, _ops=ops):
                for fn, ws, inc, inck in _ops:
                    for sem, cnt in ws:
                        eng.wait_ge(sem.h, cnt)
                    if fn is None:
                        continue
                    ins = fn(eng)
                    if inc is not None:
                        ins.then_inc(inc.h, inck)
            getattr(block, e)(body)


def rows_fm(w_dram, ncols):
    return w_dram.rearrange("(dc p) n -> p dc n", p=128)[:, :, 0:ncols]



def build_nc(debug=False):
    nc = bass.Bass("TRN2", target_bir_lowering=False)

    def din(name, shape, dt=F32):
        return nc.dram_tensor(name, shape, dt, kind="ExternalInput").ap()

    x = din("x", [T, D])
    wqT = din("wqT", [D, 2048]); wkT = din("wkT", [D, 512]); wvT = din("wvT", [D, 512])
    woT = din("woT", [2048, D])
    fwq = din("fwq", [D, 1024]); fwk = din("fwk", [D, 1024]); fwv = din("fwv", [D, 2048])
    fwvT = din("fwvT", [2048, D]); fwg = din("fwg", [D, 2048]); fwout = din("fwout", [2048, D])
    cvw_d = din("cvw", [128, 32]); rwfm_d = din("rwfm", [128, 32])
    identd = din("d_ident", [128, 128], BF16); triuTd = din("d_triuT", [128, 128], BF16)
    cosAd = din("d_cosA", [128, NSC * 128], BF16); sinAsd = din("d_sinAs", [128, NSC * 128], BF16)
    cosFd = din("d_cosF", [128, B * 64], BF16); sinFsd = din("d_sinFs", [128, B * 64], BF16)
    cosFbd = din("d_cosFb", [128, B * 1024], BF16); sinFbd = din("d_sinFb", [128, B * 1024], BF16)
    maskcd = din("maskc", [128, NCORES]); epsbd = din("epsb", [128, 4])

    out = nc.dram_tensor("out", [T, D], F32, kind="ExternalOutput").ap()

    cc_in = nc.dram_tensor("cc_in", [16 * 128, 129], F32)
    cc_outs = [nc.dram_tensor(f"cc_out{b}", [NCORES * 4 * 128, 129], F32) for b in range(B)]
    mixb = nc.dram_tensor("mixb", [T, 2048], BF16).ap()
    outpb = nc.dram_tensor("outpb", [T, 2048], BF16).ap()

    dbg = {}
    def ddbg(name, shape, dt):
        if debug:
            dbg[name] = nc.dram_tensor("dbg_" + name, shape, dt, kind="ExternalOutput").ap()
    ddbg("xn", [128, 16 * T], BF16)
    ddbg("kf_fm", [128, B * 4 * SC], BF16)
    ddbg("vext", [128, NTC * 516], BF16)
    ddbg("snap", [128, 16 * 3 * 129], BF16)
    ddbg("prefall", [128, 16 * 4 * 129], BF16)
    ddbg("qf0", [128, NTC * 512], BF16)
    ddbg("wvr", [128, 512], BF16)
    ddbg("logits", [128, B * 32], F32)
    ddbg("xc", [128, 16 * T], BF16)
    ddbg("qs4", [128, B * 1024], BF16)
    ddbg("ks4", [128, B * 1024], BF16)
    ddbg("vs4", [128, B * 2048], BF16)
    ddbg("onrm", [128, B * 2048], BF16)
    ddbg("dots", [128, 10 * 16], F32)
    ddbg("qkc", [128, 10 * 16], BF16)
    ddbg("mixb", [T, 2048], BF16)
    ddbg("outpb", [T, 2048], BF16)

    es = ExitStack()
    alloc = lambda name, shape, dt=BF16: es.enter_context(nc.sbuf_tensor(name, shape, dt))

    ident = alloc("ident", [128, 128])
    triuT = alloc("triuT", [128, 128])
    cosA = alloc("cosA", [128, NSC * 128])
    sinAs = alloc("sinAs", [128, NSC * 128])
    cosF = alloc("cosF", [128, B * 64])
    sinFs = alloc("sinFs", [128, B * 64])
    cvws = alloc("cvws", [128, 32], F32)
    rwfm = alloc("rwfm_s", [128, 32])
    wvr = alloc("wvr", [128, 512])
    junk = alloc("junk", [128, 704], F32)
    stat = junk[:, 672:704]
    maskc = junk[:, 516:524]
    epsb = junk[:, 524:528]
    epsb2 = junk[:, 368:372]

    pb = [es.enter_context(nc.psum_tensor(f"pb{i}", [128, 512], F32)) for i in range(8)]
    bank_w = [None] * 8
    bank_r = [[] for _ in range(8)]

    sems = {}
    for nm, owner in [("sWL", None), ("sXL", None), ("sST", None), ("sPE", "tensor"),
                      ("sV", "vector"), ("sSc", "scalar"), ("sG", "gpsimd"), ("sCC", None)]:
        sems[nm] = Sem(es.enter_context(nc.semaphore(nm)), owner)
    sWL, sXL, sST, sPE, sV, sSc, sG, sCC = (sems[n] for n in
        ["sWL", "sXL", "sST", "sPE", "sV", "sSc", "sG", "sCC"])
    allsems = [sWL, sXL, sST, sPE, sV, sSc, sCC]

    p = Prog()

    def mm(bank_i, c0, c1, lhsT, rhs, start, stop, waits=()):
        ws = list(waits) + bank_r[bank_i]
        def fn(t, _o=pb[bank_i][:, c0:c1], _l=lhsT, _r=rhs, _s=start, _p=stop):
            return t.matmul(_o, _l, _r, start=_s, stop=_p, skip_group_check=True)
        dep = p.emit("tensor", fn, waits=ws, inc=sPE)
        bank_r[bank_i] = []
        bank_w[bank_i] = dep
        return dep

    def bank_read(bank_i, eng, fn, waits=()):
        ws = list(waits)
        if bank_w[bank_i] is not None:
            ws.append(bank_w[bank_i])
        inc = {"vector": sV, "scalar": sSc, "gpsimd": sG}[eng]
        dep = p.emit(eng, fn, waits=ws, inc=inc)
        bank_r[bank_i].append(dep)
        return dep

    def dma(eng, out_ap, in_ap, sem, waits=()):
        def fn(e, _o=out_ap, _i=in_ap):
            return e.dma_start(out=_o, in_=_i)
        return p.emit(eng, fn, waits=waits, inc=sem, inck=16)

    def vop(fn, waits=()):
        return p.emit("vector", fn, waits=waits, inc=sV)

    def sop(fn, waits=()):
        return p.emit("scalar", fn, waits=waits, inc=sSc)

    def vopd(fn, waits=()):
        d = p.emit("vector", fn, waits=waits, inc=sV)
        p.emit("vector", lambda v: v.drain())
        return d

    def sopd(fn, waits=()):
        d = p.emit("scalar", fn, waits=waits, inc=sSc)
        p.emit("scalar", lambda s: s.drain())
        return d

    for dst, src in [(ident, identd), (triuT, triuTd), (cosA, cosAd),
                     (sinAs, sinAsd), (cosF, cosFd), (sinFs, sinFsd),
                     (cvws, cvw_d), (maskc, maskcd), (epsb, epsbd)]:
        dma("sync", dst[:, :], src, sXL)
    rw_dep = dma("gpsimd", rwfm[:, :], rwfm_d, sWL)
    p.gate(p.marks([sXL]) + [rw_dep])

    es_xn = ExitStack()
    xn = es_xn.enter_context(nc.sbuf_tensor("xn", [128, 16 * T], BF16))
    es_w = ExitStack()
    wpA = es_w.enter_context(nc.sbuf_tensor("wpA", [128, 16 * 512], BF16))
    wpB = es_w.enter_context(nc.sbuf_tensor("wpB", [128, 16 * 512], BF16))

    wvr_mm_deps = []
    for h in range(16):
        buf = wpB[:, (h % 2) * 4096:(h % 2) * 4096 + 2048]
        ws = [rw_dep] + ([wvr_mm_deps[h - 2]] if h >= 2 else [])
        ld = dma("gpsimd", buf, fwvT[h * 128:(h + 1) * 128, :], sWL, waits=ws)
        dep = None
        for dc in range(16):
            dep = mm(7, dc * 32 + h * 2, dc * 32 + h * 2 + 2,
                     buf[:, dc * 128:(dc + 1) * 128], rwfm[:, 2 * h:2 * h + 2],
                     start=True, stop=True, waits=[ld])
        wvr_mm_deps.append(dep)
    wvr_dep = bank_read(7, "vector", lambda v: v.tensor_copy(wvr[:, :], pb[7][:, :]))
    if debug:
        dma("sync", dbg["wvr"], wvr[:, :], sST, waits=[wvr_dep])

    def xn_chunk(dc, tc):
        return xn[:, dc * T + tc * 128: dc * T + (tc + 1) * 128]

    es_attn = ExitStack()
    aallo = lambda name, shape, dt=BF16: es_attn.enter_context(nc.sbuf_tensor(name, shape, dt))
    wpC = aallo("wpC", [128, 16 * 512])
    kf_fm = aallo("kf_fm", [128, B * 4 * SC])
    vext = aallo("vext", [128, NTC * 516])
    prefall = aallo("prefall", [128, 16 * 4 * 129])
    t1b = aallo("t1b", [128, 512])
    t2b = aallo("t2b", [128, 512])
    rawb = aallo("rawb", [128, 512])
    mnb = aallo("mnb", [128, 512])
    exb = aallo("exb", [128, 512])
    es_sn = ExitStack()
    stot = es_sn.enter_context(nc.sbuf_tensor("stot", [128, 4 * 129], F32))

    es_s1 = ExitStack()
    xb1 = [es_s1.enter_context(nc.sbuf_tensor(f"xb1{i}", [128, D], F32)) for i in range(2)]
    xnb = es_s1.enter_context(nc.sbuf_tensor("xnb", [128, D], BF16))
    ktmb = es_s1.enter_context(nc.sbuf_tensor("ktmb", [128, 512], BF16))

    ones_view = vext[:, :].rearrange("p (t k c) -> p t k c", k=4, c=129)[:, :, :, 128:129]
    ones_dep = vop(lambda v: v.memset(ones_view, 1.0))
    snap_deps = []

    wk_dep = []
    for q4 in range(4):
        wk_dep.append(dma("gpsimd",
            wpA[:, :].rearrange("p (dc n) -> p dc n", n=512)[:, q4 * 4:q4 * 4 + 4, :],
            rows_fm(wkT, 512)[:, q4 * 4:q4 * 4 + 4, :], sWL,
            waits=[wvr_mm_deps[14], wvr_mm_deps[15]]))
    wv_dep = []
    for q4 in range(4):
        wv_dep.append(dma("gpsimd",
            wpB[:, :].rearrange("p (dc n) -> p dc n", n=512)[:, q4 * 4:q4 * 4 + 4, :],
            rows_fm(wvT, 512)[:, q4 * 4:q4 * 4 + 4, :], sWL,
            waits=[wvr_mm_deps[14], wvr_mm_deps[15], wvr_dep]))
    pre_panel_deps = []
    for q4 in range(4):
        pre_panel_deps.append(dma("gpsimd",
            wpC[:, :].rearrange("p (dc n) -> p dc n", n=512)[:, q4 * 4:q4 * 4 + 4, :],
            rows_fm(wqT, 2048)[:, q4 * 4:q4 * 4 + 4, 0:512], sWL))

    rope_free = [None]

    def rope_elu(sc_i, out_ap, raw_waits, bank_i):
        raw_dep = bank_read(bank_i, "scalar",
                            lambda sce, _b=bank_i: sce.copy(rawb[:, 0:512], pb[_b][:, :]),
                            waits=list(raw_waits) + [rope_free[0]])
        rv = rawb[:, 0:512].rearrange("p (u d) -> p u d", d=128)
        cosb = cosA[:, sc_i * 128:(sc_i + 1) * 128][:, None, :].broadcast_to([128, 4, 128])
        sin_lo = sinAs[:, sc_i * 128:sc_i * 128 + 64][:, None, :].broadcast_to([128, 4, 64])
        sin_hi = sinAs[:, sc_i * 128 + 64:sc_i * 128 + 128][:, None, :].broadcast_to([128, 4, 64])
        t1v = t1b[:, 0:512].rearrange("p (u d) -> p u d", d=128)
        t2v = t2b[:, 0:512].rearrange("p (u d) -> p u d", d=128)
        vop(lambda v, _a=t1v, _b=rv, _c=cosb: v.tensor_tensor(_a, _b, _c, ALU.mult),
            waits=[raw_dep])
        vop(lambda v, _a=t2v, _b=rv, _c=sin_lo: v.tensor_tensor(
            _a[:, :, 0:64], _b[:, :, 64:128], _c, ALU.mult))
        vop(lambda v, _a=t2v, _b=rv, _c=sin_hi: v.tensor_tensor(
            _a[:, :, 64:128], _b[:, :, 0:64], _c, ALU.mult))
        vop(lambda v: v.tensor_tensor(mnb[:, 0:512], t1b[:, 0:512], t2b[:, 0:512], ALU.add))
        vop(lambda v: v.tensor_scalar(t1b[:, 0:512], mnb[:, 0:512], 0.0, None, ALU.min))
        mn_mark = (sV, sV.n)
        ex_dep = sop(lambda sce: sce.activation(exb[:, 0:512], t1b[:, 0:512], AF.Exp),
                     waits=[mn_mark])
        out_dep = vop(lambda v, _o=out_ap: v.scalar_tensor_tensor(
            _o, mnb[:, 0:512], 0.0, exb[:, 0:512], ALU.max, ALU.add),
            waits=[ex_dep, (sPE, sPE.n)])
        rope_free[0] = out_dep
        return out_dep

    xn_ready = [[] for _ in range(NTC)]
    x_cons = [None] * NTC
    xnb_free = [None]

    def ln_chain(tc):
        par = tc % 2
        xb = xb1[par]
        def st(k):
            return stat[:, par * 8 + k:par * 8 + k + 1]
        ws = [x_cons[tc - 2]] if tc >= 2 else []
        xld = dma("sync", xb[:, :], x[tc * 128:(tc + 1) * 128, :], sXL, waits=ws)
        sop(lambda sce: sce.activation(xnb[:, :], xb[:, :], AF.Identity,
                                       accum_out=st(0)),
            waits=[xld, xnb_free[0]])
        sopd(lambda sce: sce.activation(xnb[:, :], xb[:, :], AF.Square,
                                        accum_out=st(1)))
        sopd(lambda sce: sce.activation(st(2), st(0), AF.Identity, scale=1.0 / D))
        sopd(lambda sce: sce.activation(st(4), st(0), AF.Square, scale=1.0 / D))
        sopd(lambda sce: sce.activation(st(5), st(1), AF.Identity, scale=1.0 / D,
                                        bias=epsb[:, 0:1]))
        sd_dep = sopd(lambda sce: sce.activation(st(6), st(4), AF.Sqrt, scale=-1.0,
                                                 bias=st(5)))
        rs_dep = vopd(lambda v: v.reciprocal(st(7), st(6)), waits=[sd_dep])
        nrm_dep = vop(lambda v: v.tensor_scalar(
            xnb[:, :], xb[:, :], st(2), st(7), ALU.subtract, ALU.mult))
        x_cons[tc] = nrm_dep
        tr_dep = None
        for grp in range(4):
            bk = 6 if grp % 2 == 0 else 7
            for k in range(4):
                dcc = grp * 4 + k
                tr_dep = mm(bk, k * 128, (k + 1) * 128,
                            xnb[:, dcc * 128:(dcc + 1) * 128], ident[:, :],
                            start=True, stop=(k == 3), waits=[nrm_dep])
            outv = xn[:, :].rearrange("p (dc t) -> p dc t", t=T)[
                :, grp * 4:grp * 4 + 4, tc * 128:(tc + 1) * 128]
            pv = pb[bk][:, :].rearrange("p (k t) -> p k t", t=128)
            if grp % 2 == 0:
                ed = bank_read(bk, "vector", lambda v, _o=outv, _i=pv: v.tensor_copy(_o, _i))
            else:
                ed = bank_read(bk, "scalar", lambda sce, _o=outv, _i=pv: sce.copy(_o, _i))
            xn_ready[tc].append(ed)
        xnb_free[0] = tr_dep

    kfm_ready = [None] * NTC
    vext_ready = [None] * NTC
    ccin_deps = []
    cc_deps = []
    ln_chain(0)
    for tc in range(NTC):
        if tc + 1 < NTC:
            ln_chain(tc + 1)
        b, sc_i = tc // 4, tc % 4
        for dc in range(16):
            mm(4, 0, 512, xn_chunk(dc, tc), wpA[:, dc * 512:(dc + 1) * 512],
               start=(dc == 0), stop=(dc == 15),
               waits=[wk_dep[dc // 4]] + (xn_ready[tc] if dc == 0 else []))
        kf_dep = rope_elu(sc_i, ktmb[:, 0:512], [], 4)
        for dc in range(16):
            mm(5, 0, 512, xn_chunk(dc, tc), wpB[:, dc * 512:(dc + 1) * 512],
               start=(dc == 0), stop=(dc == 15),
               waits=[wv_dep[dc // 4]] + (xn_ready[tc] if dc == 0 else []))
        vout = vext[:, :].rearrange("p (t k c) -> p t k c", k=4, c=129)[
            :, tc:tc + 1, :, 0:128]
        pv5 = pb[5][:, :].rearrange("p (k d) -> p k d", d=128)[:, None, :, :]
        vext_ready[tc] = bank_read(5, "scalar",
                                   lambda sce, _o=vout, _i=pv5: sce.copy(_o, _i))
        for kvh in range(4):
            mm(6, kvh * 128, (kvh + 1) * 128, ktmb[:, kvh * 128:(kvh + 1) * 128],
               ident[:, :], start=True, stop=(kvh == 3),
               waits=[kf_dep] if kvh == 0 else [])
        kfm_out = kf_fm[:, :].rearrange("p (u s) -> p u s", s=SC)[
            :, b * 4:b * 4 + 4, sc_i * 128:(sc_i + 1) * 128]
        pv6 = pb[6][:, :].rearrange("p (k d) -> p k d", d=128)
        kfm_ready[tc] = bank_read(6, "vector",
                                  lambda v, _o=kfm_out, _i=pv6: v.tensor_copy(_o, _i))
        for kvh in range(4):
            bk = kvh
            if sc_i > 0:
                sn_out = prefall[:, ((b * 4 + kvh) * 4 + sc_i) * 129:
                                 ((b * 4 + kvh) * 4 + sc_i) * 129 + 129]
                sdep = bank_read(bk, "scalar",
                    lambda sce, _o=sn_out, _b=bk: sce.copy(_o, pb[_b][:, 0:129]))
                snap_deps.append(sdep)
            mm(bk, 0, 129, ktmb[:, kvh * 128:(kvh + 1) * 128],
               vext[:, tc * 516 + kvh * 129:tc * 516 + kvh * 129 + 129],
               start=(sc_i == 0), stop=(sc_i == 3),
               waits=[kf_dep, vext_ready[tc], ones_dep])
        if sc_i == 3:
            ev_deps = []
            for kvh in range(4):
                ev_deps.append(bank_read(kvh, "scalar",
                    lambda sce, _k=kvh: sce.copy(stot[:, _k * 129:(_k + 1) * 129],
                                                 pb[_k][:, 0:129]),
                    waits=[ccin_deps[-1]] if ccin_deps else []))
            ccv = cc_in.ap().rearrange("(u p) c -> p u c", p=128)[:, b * 4:b * 4 + 4, :]
            sv = stot[:, :].rearrange("p (k c) -> p k c", c=129)
            ccin_deps.append(dma("sync", ccv, sv, sST, waits=ev_deps))

            def cc_fn(g, _b=b):
                return g.collective_compute(
                    "AllGather", ALU.bypass, replica_groups=[list(range(NCORES))],
                    ins=[cc_in.ap()[_b * 4 * 128:(_b + 1) * 4 * 128, :].opt()],
                    outs=[cc_outs[_b].ap().opt()])
            cc_deps.append(p.emit("gpsimd", cc_fn, waits=[ccin_deps[-1]], inc=sCC))

    if debug:
        gm = p.marks(allsems)
        dma("sync", dbg["xn"], xn[:, :], sST, waits=gm)
    es_s1.close()
    es_snap = ExitStack()
    gsum = es_snap.enter_context(nc.sbuf_tensor("gsum", [128, 16 * 129], BF16))
    es_snap2 = ExitStack()
    gtmp8 = es_snap2.enter_context(nc.sbuf_tensor("gtmp8", [128, NCORES * 4 * 129], F32))

    wp_last_use = {"A": (sPE, sPE.n), "B": (sPE, sPE.n)}
    vprev = None
    gsum_deps = [None] * B
    s2_tail = [ccin_deps[-1], (sPE, sPE.n), (sSc, sSc.n), x_cons[NTC - 1]]
    for b in range(B):
        jview = junk[:, 0:4 * 129]
        gsv = gsum[:, b * 4 * 129:(b + 1) * 4 * 129]
        ccall = cc_outs[b].ap().rearrange("(j u p) c -> p j u c", p=128, u=4)
        gv8 = gtmp8[:, :].rearrange("p (j u c) -> p j u c", j=NCORES, c=129)
        ld = dma("sync", gv8, ccall, sXL,
                 waits=[cc_deps[b]] + ([vprev] if vprev else s2_tail))
        for j in range(NCORES):
            gj = gtmp8[:, j * 516:(j + 1) * 516]
            if j == 0:
                vprev = vop(lambda v, _j1=jview, _g=gj: v.tensor_scalar(
                    _j1, _g, maskc[:, 0:1], None, ALU.mult), waits=[ld])
            else:
                s_, d_ = (jview, gsv) if j % 2 == 1 else (gsv, jview)
                vprev = vop(lambda v, _j=j, _s=s_, _d=d_, _g=gj: v.scalar_tensor_tensor(
                    _d, _g, maskc[:, _j:_j + 1], _s, ALU.mult, ALU.add))
        gsum_deps[b] = vprev
    eps2_dep = dma("sync", epsb2, epsbd[:, 0:4], sXL, waits=[gsum_deps[3]])
    pref_ready = [None] * 16
    for u in range(16):
        gvv = gsum[:, u * 129:(u + 1) * 129][:, None, :].broadcast_to([128, 3, 129])
        ovv = prefall[:, (u * 4 + 1) * 129:(u + 1) * 4 * 129].rearrange("p (s c) -> p s c", c=129)
        ws = ([gsum_deps[u // 4]] + snap_deps) if u == 0 else [gsum_deps[u // 4]]
        vop(lambda v, _o=prefall[:, u * 4 * 129:(u * 4 + 1) * 129],
            _g=gsum[:, u * 129:(u + 1) * 129]: v.tensor_copy(_o, _g), waits=ws)
        pref_ready[u] = vop(lambda v, _o=ovv, _g=gvv: v.tensor_tensor(
            _o, _o, _g, ALU.add))
    if debug:
        dma("sync", dbg["prefall"], prefall[:, :], sST, waits=pref_ready)
        dma("sync", dbg["kf_fm"], kf_fm[:, :], sST, waits=p.marks([sV]))
        dma("sync", dbg["vext"], vext[:, :], sST, waits=p.marks([sSc]))
    es_snap2.close()

    es_scan = ExitStack()
    qft = [es_scan.enter_context(nc.sbuf_tensor("qft0", [128, 512], BF16))] * 2
    qffm = es_scan.enter_context(nc.sbuf_tensor("qffm", [128, NTC * 512], BF16))
    mixs2 = [es_scan.enter_context(nc.sbuf_tensor(f"mixs{i}", [128, 4 * 512], BF16))
             for i in range(2)]
    Abb = [es_scan.enter_context(nc.sbuf_tensor(f"Ab{i}", [128, 128], BF16)) for i in range(6)]
    dens = junk[:, 0:64]
    rcps = junk[:, 64:128]
    nums = es_scan.enter_context(nc.sbuf_tensor("nums", [128, 8 * 128], BF16))

    prev_scan_end = None
    prev_mix_dma = None
    prev_mix_dma2 = [None, None]
    mixb_done = None
    for nb in range(4):
        if nb == 0:
            panel_deps = pre_panel_deps
            wq_buf = wpC
        else:
            wq_buf = wpA
            panel_deps = []
            for q4 in range(4):
                panel_deps.append(dma("gpsimd",
                    wpA[:, :].rearrange("p (dc n) -> p dc n", n=512)[:, q4 * 4:q4 * 4 + 4, :],
                    rows_fm(wqT, 2048)[:, q4 * 4:q4 * 4 + 4, nb * 512:(nb + 1) * 512],
                    sWL, waits=[wp_last_use.get("A")]))
        qffm_ready = [None] * NTC
        last_q_mm = None
        for tc in range(NTC):
            bkq = 4 if tc % 2 == 0 else 5
            for dc in range(16):
                last_q_mm = mm(bkq, 0, 512, xn_chunk(dc, tc), wq_buf[:, dc * 512:(dc + 1) * 512],
                               start=(dc == 0), stop=(dc == 15),
                               waits=[panel_deps[dc // 4]] + (xn_ready[tc] if dc == 0 else []))
            qf_dep = rope_elu(tc % 4, qft[tc % 2][:, :],
                              [prev_scan_end], bkq)
            bkt = 6 if tc % 2 == 0 else 7
            for hh in range(4):
                mm(bkt, hh * 128, (hh + 1) * 128,
                   qft[tc % 2][:, hh * 128:hh * 128 + 128],
                   ident[:, :], start=True, stop=(hh == 3),
                   waits=[qf_dep] if hh == 0 else [])
            qffm_ready[tc] = bank_read(bkt, "scalar",
                lambda sce, _b=bkt, _o=qffm[:, tc * 512:(tc + 1) * 512]:
                sce.copy(_o, pb[_b][:, :]))
        wp_last_use["A"] = (sPE, sPE.n)
        if debug and nb == 0:
            dma("sync", dbg["qf0"], qffm[:, :], sST, waits=[(sSc, sSc.n)])
        its = [(b, hh, sc_i) for b in range(B) for hh in range(4) for sc_i in range(NSC)]
        LOOK = 4
        mask_done = [None] * 64
        out_read = [None] * 64
        num_ready = [None] * 64
        mul_done = [None] * 64
        mix_deps_b = [[] for _ in range(B)]
        for i in range(64 + LOOK):
            if i < 64:
                b, hh, sc_i = its[i]
                tc = b * 4 + sc_i
                u = b * 4 + nb
                qv = qffm[:, tc * 512 + hh * 128:tc * 512 + hh * 128 + 128]
                bka = 4 + i % 4
                mm(bka, 0, 128, kf_fm[:, u * SC + sc_i * 128:u * SC + (sc_i + 1) * 128],
                   qv, start=True, stop=True,
                   waits=[qffm_ready[tc], kfm_ready[tc]])
                mask_done[i] = bank_read(bka, "vector",
                    lambda v, _o=Abb[i % 6], _b=bka: v.tensor_tensor(
                        _o[:, :], pb[_b][:, 0:128], triuT[:, :], ALU.mult),
                    waits=[out_read[i - 6] if i >= 6 else None])
            j = i - LOOK
            if 0 <= j < 64:
                b, hh, sc_i = its[j]
                tc = b * 4 + sc_i
                u = b * 4 + nb
                qv = qffm[:, tc * 512 + hh * 128:tc * 512 + hh * 128 + 128]
                bko = j % 4
                out_read[j] = mm(bko, 0, 129, Abb[j % 6][:, :],
                   vext[:, tc * 516 + nb * 129:tc * 516 + nb * 129 + 129],
                   start=True, stop=False, waits=[mask_done[j], vext_ready[tc]])
                mm(bko, 0, 129, qv,
                   prefall[:, (u * 4 + sc_i) * 129:(u * 4 + sc_i) * 129 + 129],
                   start=False, stop=True, waits=[pref_ready[u]])
                den_dep = bank_read(bko, "vector",
                          lambda v, _b=bko, _j=j: v.tensor_scalar(
                              dens[:, _j:_j + 1], pb[_b][:, 128:129], epsb[:, 1:2],
                              None, ALU.add))
                num_ready[j] = bank_read(bko, "scalar",
                               lambda sce, _b=bko,
                               _o=nums[:, (j % 8) * 128:(j % 8) * 128 + 128]:
                               sce.copy(_o, pb[_b][:, 0:128]),
                               waits=[den_dep, mul_done[j - 8] if j >= 8 else None])
                if (j + 1) % 8 == 0:
                    hq = j // 8
                    p.emit("vector", lambda v: v.drain())
                    rc_dep = vopd(lambda v, _h=hq: v.reciprocal(
                        rcps[:, _h * 8:(_h + 1) * 8], dens[:, _h * 8:(_h + 1) * 8]))
                    for jj in range(hq * 8, hq * 8 + 8):
                        b2, hh2, sc2 = its[jj]
                        md = p.emit("vector",
                            lambda v, _jj=jj,
                            _o=mixs2[b2 % 2][:, sc2 * 512 + hh2 * 128:sc2 * 512 + hh2 * 128 + 128],
                            _n=nums[:, (jj % 8) * 128:(jj % 8) * 128 + 128]:
                            v.tensor_scalar(_o, _n, rcps[:, _jj:_jj + 1], None, ALU.mult),
                            waits=[rc_dep, num_ready[jj], prev_mix_dma2[b2 % 2]], inc=sV)
                        mul_done[jj] = md
                        mix_deps_b[b2].append(md)
                    if len(mix_deps_b[its[j][0]]) == 16:
                        bq = its[j][0]
                        mixs = mixs2[bq % 2]
                        mv = mixs[:, :].rearrange("p (t c) -> p t c", c=512)
                        ov = mixb.rearrange("(t p) c -> p t c", p=128)[
                            :, bq * 4:bq * 4 + 4, nb * 512:(nb + 1) * 512]
                        prev_mix_dma2[bq % 2] = dma("sync", ov, mv, sST, waits=mix_deps_b[bq])
        prev_scan_end = (sPE, sPE.n)
        prev_mix_dma = prev_mix_dma2[1]
    mixb_done = (sST, sST.n)
    if debug:
        dma("sync", dbg["mixb"], mixb, sST, waits=[mixb_done])
    p.gate(p.marks(allsems))
    es_scan.close()
    es_snap.close()
    es_sn.close()
    es_attn.close()

    es_ffn = ExitStack()
    fallo = lambda name, shape, dt=BF16: es_ffn.enter_context(nc.sbuf_tensor(name, shape, dt))
    qs4 = fallo("qs4", [128, B * 1024])
    ks4 = fallo("ks4", [128, B * 1024])
    vs4 = fallo("vs4", [128, B * 2048])

    onrm = fallo("onrm", [128, B * 2048])
    slt = [fallo(f"slt{i}", [128, 2048]) for i in range(3)]
    lgs = fallo("lgs", [128, B * 32], F32)
    sml = junk[:, 0:24 * 16]
    mt4 = junk[:, 24 * 16:24 * 16 + 160]
    dots = junk[:, 24 * 16 + 160:24 * 16 + 320]
    qkc = fallo("qkc", [128, 160])
    sgt = fallo("sgt", [128, 16 * 512])
    cva = [fallo(f"cva{i}", [128, 2048]) for i in range(3)]
    ol4 = cva[2]
    ropeF_tab = None
    opg = [fallo(f"opg{i}", [128, 512]) for i in range(2)]
    ft1 = slt[0][:, 0:1024]
    ft2 = slt[2][:, 0:1024]

    def zv(l): return sml[:, (0 + l) * 16:(0 + l) * 16 + 16]
    def pmv(l): return sml[:, (4 + l) * 16:(4 + l) * 16 + 16]
    def cfv(l): return sml[:, (8 + l) * 16:(8 + l) * 16 + 16]
    SM_DIFF, SM_AB, SM_T1, SM_T2, SM_T3, SM_SO, SM_SO2, SM_M, SM_VAR, SM_SD, SM_RS = \
        [(12 + i) * 16 for i in range(11)]
    def sm(off): return sml[:, off:off + 16]

    PAIRS = [(l, m) for l in range(B) for m in range(l + 1)]
    PIDX = {pm_: i for i, pm_ in enumerate(PAIRS)}
    def mtv(pi): return mt4[:, pi * 16:(pi + 1) * 16]
    def dtv(pi): return dots[:, pi * 16:(pi + 1) * 16]
    def qkv(pi): return qkc[:, pi * 16:(pi + 1) * 16]
    def MEANv(l): return mt4[:, l * 16:(l + 1) * 16]
    VARSb = sml[:, 19 * 16:23 * 16]
    def VARSv(l): return sml[:, (19 + l) * 16:(20 + l) * 16]
    SDb = dots[:, 0:64]
    RSTDb = dots[:, 64:128]
    def RSTDv(l): return dots[:, 64 + l * 16:64 + (l + 1) * 16]

    c1b = cvws[:, 16:32][:, :, None].broadcast_to([128, 16, 128])
    c0b = cvws[:, 0:16][:, :, None].broadcast_to([128, 16, 128])
    xn3 = xn[:, :].rearrange("p (dc t) -> p dc t", t=T)

    prev_opg_dma = [None, None]
    prev_og = None
    panel_rr = [0]
    wp_last_use["C"] = wp_last_use["A"]
    s4bk = [0]

    def nxbk():
        b = s4bk[0] % 8
        s4bk[0] += 1
        return b

    def emit_conv(g):
        tcs = [l * 4 + g for l in range(B)]
        xc_dep = [None] * (B + 1)
        cA = cva[0][:, :].rearrange("p (dc t) -> p dc t", t=128)
        cB = cva[1][:, :].rearrange("p (dc t) -> p dc t", t=128)
        cC = cva[2][:, :].rearrange("p (dc t) -> p dc t", t=128)
        for l in range(B - 1, -1, -1):
            xl = xn3[:, :, tcs[l] * 128:(tcs[l] + 1) * 128]
            v1 = vop(lambda v, _x=xl: v.tensor_tensor(cA, _x, c1b, ALU.mult),
                     waits=[xc_dep[l + 1], conv_free[0]])
            if l > 0:
                xp = xn3[:, :, tcs[l - 1] * 128:(tcs[l - 1] + 1) * 128]
                vop(lambda v, _x=xp: v.tensor_tensor(cB, _x, c0b, ALU.mult))
                pre_dep = vop(lambda v: v.tensor_tensor(cC, cA, cB, ALU.add))
                src_ = cC
            else:
                pre_dep = v1
                src_ = cA
            xc_dep[l] = sop(lambda sce, _o=xl, _s=src_: sce.activation(_o, _s, AF.Silu),
                            waits=[pre_dep])
        conv_free[0] = xc_dep[0]
        return xc_dep

    def emit_panels(g, wd, n_out, npan, evac, xc_dep, extra_evac_wait):
        tcs = [l * 4 + g for l in range(B)]
        deps = []
        for pp in range(npan):
            key = ["A", "B"][panel_rr[0] % 2]
            wp = {"A": wpA, "B": wpB}[key]
            panel_rr[0] += 1
            pdep = []
            for q4 in range(4):
                pdep.append(dma("gpsimd",
                    wp[:, :].rearrange("p (dc n) -> p dc n", n=512)[:, q4 * 4:q4 * 4 + 4, :],
                    rows_fm(wd, n_out)[:, q4 * 4:q4 * 4 + 4, pp * 512:(pp + 1) * 512],
                    sWL, waits=[wp_last_use.get(key)]))
            for l in range(B):
                bk = nxbk()
                for dc in range(16):
                    mm(bk, 0, 512, xn_chunk(dc, tcs[l]), wp[:, dc * 512:(dc + 1) * 512],
                       start=(dc == 0), stop=(dc == 15),
                       waits=[pdep[dc // 4], xc_dep[l]])
                deps.append(evac(pp, l, bk, extra_evac_wait))
            wp_last_use[key] = (sPE, sPE.n)
        return deps

    def emit_gwave(g, pps, xc_dep, sg_ready, gate_slot):
        tcs = [l * 4 + g for l in range(B)]
        for pp in pps:
            key = ["A", "B"][panel_rr[0] % 2]
            wp = {"A": wpA, "B": wpB}[key]
            panel_rr[0] += 1
            pdep = []
            for q4 in range(4):
                pdep.append(dma("gpsimd",
                    wp[:, :].rearrange("p (dc n) -> p dc n", n=512)[:, q4 * 4:q4 * 4 + 4, :],
                    rows_fm(fwg, 2048)[:, q4 * 4:q4 * 4 + 4, pp * 512:(pp + 1) * 512],
                    sWL, waits=[wp_last_use.get(key)]))
            for l in range(B):
                ti = pp * B + l
                bk = nxbk()
                for dc in range(16):
                    mm(bk, 0, 512, xn_chunk(dc, tcs[l]), wp[:, dc * 512:(dc + 1) * 512],
                       start=(dc == 0), stop=(dc == 15), waits=[pdep[dc // 4], xc_dep[l]])
                sg_ready[(pp, l)] = bank_read(bk, "scalar",
                    lambda sce, _b=bk, _o=sgt[:, ti * 512:ti * 512 + 512]:
                    sce.activation(_o, pb[_b][:, :], AF.Silu),
                    waits=[gate_slot.get(ti)])
            wp_last_use[key] = (sPE, sPE.n)

    def emit_proj(g, xc_dep, gate_slot, dots_prev, micro_prev):
        tcs = [l * 4 + g for l in range(B)]
        q_deps = emit_panels(g, fwq, 1024, 2, lambda pp, l, bk, w: bank_read(bk, "scalar",
            lambda sce, _o=qs4[:, l * 1024 + pp * 512:l * 1024 + pp * 512 + 512], _b=bk:
            sce.activation(_o, pb[_b][:, :], AF.Silu), waits=[w]), xc_dep, dots_prev)
        k_deps = emit_panels(g, fwk, 1024, 2, lambda pp, l, bk, w: bank_read(bk, "scalar",
            lambda sce, _o=ks4[:, l * 1024 + pp * 512:l * 1024 + pp * 512 + 512], _b=bk:
            sce.copy(_o, pb[_b][:, :]), waits=[w]), xc_dep, dots_prev)
        lg_deps = []
        for l in range(B):
            bk = nxbk()
            for dc in range(16):
                mm(bk, 0, 32, xn_chunk(dc, tcs[l]), wvr[:, dc * 32:(dc + 1) * 32],
                   start=(dc == 0), stop=(dc == 15), waits=[xc_dep[l]])
            lg_deps.append(bank_read(bk, "scalar",
                lambda sce, _o=lgs[:, l * 32:(l + 1) * 32], _b=bk:
                sce.copy(_o, pb[_b][:, 0:32]), waits=[dots_prev]))
        v_deps = emit_panels(g, fwv, 2048, 4, lambda pp, l, bk, w: bank_read(bk, "scalar",
            lambda sce, _o=vs4[:, l * 2048 + pp * 512:l * 2048 + pp * 512 + 512], _b=bk:
            sce.copy(_o, pb[_b][:, :]), waits=[w]), xc_dep, micro_prev)
        sg_ready = {}
        emit_gwave(g, [0, 1], xc_dep, sg_ready, gate_slot)
        return q_deps, k_deps, lg_deps, v_deps, sg_ready

    def emit_rope_coeff_dots(g, q_deps, k_deps, lg_deps):
        rope4_deps = [None] * B
        for l in range(B):
            cb1 = cosF[:, l * 64:(l + 1) * 64][:, None, :].broadcast_to([128, 16, 64])
            f1 = ft1[:, :].rearrange("p (h d) -> p h d", d=64)
            f2 = ft2[:, :].rearrange("p (h d) -> p h d", d=64)
            slo = sinFs[:, l * 64:l * 64 + 32][:, None, :].broadcast_to([128, 16, 32])
            shi = sinFs[:, l * 64 + 32:l * 64 + 64][:, None, :].broadcast_to([128, 16, 32])
            for buf, dps in [(qs4, q_deps), (ks4, k_deps)]:
                bv = buf[:, l * 1024:(l + 1) * 1024].rearrange("p (h d) -> p h d", d=64)
                vop(lambda v, _b=bv, _c=cb1, _f=f1: v.tensor_tensor(_f, _b, _c, ALU.mult),
                    waits=list(dps))
                vop(lambda v, _b=bv, _c=slo, _f=f2: v.tensor_tensor(
                    _f[:, :, 0:32], _b[:, :, 32:64], _c, ALU.mult))
                vop(lambda v, _b=bv, _c=shi, _f=f2: v.tensor_tensor(
                    _f[:, :, 32:64], _b[:, :, 0:32], _c, ALU.mult))
                rope4_deps[l] = vop(lambda v, _o=buf[:, l * 1024:(l + 1) * 1024]:
                                    v.tensor_tensor(_o, ft1[:, :], ft2[:, :], ALU.add))
        for l in range(B):
            lg = lgs[:, l * 32:(l + 1) * 32].rearrange("p (h r) -> p h r", r=2)
            vopd(lambda v, _l=lg: v.tensor_tensor(sm(SM_DIFF), _l[:, :, 0], _l[:, :, 1],
                                                 ALU.subtract), waits=[lg_deps[l]])
            vopd(lambda v, _z=zv(l): v.tensor_scalar(_z, sm(SM_DIFF), 0.0, None, ALU.is_ge))
            ab_dep = sopd(lambda sce: sce.activation(sm(SM_AB), sm(SM_DIFF), AF.Abs),
                         waits=[(sV, sV.n)])
            pm_dep = sopd(lambda sce, _o=pmv(l): sce.activation(_o, sm(SM_AB), AF.Sigmoid))
            if l == 0:
                vopd(lambda v, _o=cfv(0), _p=pmv(0): v.tensor_scalar(
                    _o, _p, 0.125, None, ALU.mult), waits=[pm_dep])
            else:
                for m_ in range(l):
                    pi = PIDX[(l, m_)]
                    vopd(lambda v, _a=zv(l), _b=zv(m_): v.tensor_tensor(sm(SM_T1), _a, _b, ALU.mult))
                    vopd(lambda v, _a=zv(l), _b=zv(m_): v.tensor_tensor(sm(SM_T2), _a, _b, ALU.add))
                    vopd(lambda v: v.tensor_scalar(sm(SM_T3), sm(SM_T1), 2.0, 1.0, ALU.mult, ALU.add))
                    vopd(lambda v, _o=mtv(pi): v.tensor_tensor(_o, sm(SM_T3), sm(SM_T2), ALU.subtract))
                vopd(lambda v, _m=mtv(PIDX[(l, 0)]): v.tensor_scalar(sm(SM_T1), _m, 1.0, None, ALU.add))
                cur = SM_T1
                for m_ in range(1, l):
                    nxt = SM_T2 if cur == SM_T1 else SM_T1
                    vopd(lambda v, _m=mtv(PIDX[(l, m_)]), _c=cur, _n=nxt:
                        v.tensor_tensor(sm(_n), sm(_c), _m, ALU.add))
                    cur = nxt
                vopd(lambda v, _c=cur: v.reciprocal(sm(SM_T3), sm(_c)))
                vopd(lambda v, _p=pmv(l): v.tensor_tensor(sm(SM_T1), _p, sm(SM_T3), ALU.mult),
                    waits=[pm_dep])
                vopd(lambda v, _o=cfv(l): v.tensor_scalar(_o, sm(SM_T1), 0.125, None, ALU.mult))
        for l in range(B):
            for m_ in range(l + 1):
                pi = PIDX[(l, m_)]
                qv4 = qs4[:, l * 1024:(l + 1) * 1024].rearrange("p (h d) -> p h d", d=64)
                kv4 = ks4[:, m_ * 1024:(m_ + 1) * 1024].rearrange("p (h d) -> p h d", d=64)
                f1 = ft1[:, :].rearrange("p (h d) -> p h d", d=64)
                vop(lambda v, _q=qv4, _k=kv4, _f=f1: v.tensor_tensor(_f, _q, _k, ALU.mult),
                    waits=[rope4_deps[l], rope4_deps[m_]])
                vopd(lambda v, _o=dtv(pi), _f=f1: v.tensor_reduce(_o, _f, AX.X, ALU.add))
                if m_ < l:
                    vopd(lambda v, _d=dtv(pi), _m=mtv(pi): v.tensor_tensor(
                        sm(SM_T1), _d, _m, ALU.mult))
                    vopd(lambda v, _o=qkv(pi), _c=cfv(l): v.tensor_tensor(
                        _o, sm(SM_T1), _c, ALU.mult))
                else:
                    vopd(lambda v, _o=qkv(pi), _d=dtv(pi), _c=cfv(l): v.tensor_tensor(
                        _o, _d, _c, ALU.mult))

    def emit_micro_ln(g, v_deps, conv_sil, sg_ready, gate_slot):
        tcs = [l * 4 + g for l in range(B)]
        o3 = ol4[:, :].rearrange("p (h d) -> p h d", d=128)
        s0 = slt[0][:, :].rearrange("p (h d) -> p h d", d=128)
        s1 = slt[1][:, :].rearrange("p (h d) -> p h d", d=128)
        s2 = slt[2][:, :].rearrange("p (h d) -> p h d", d=128)
        c3 = onrm[:, 3 * 2048:4 * 2048].rearrange("p (h d) -> p h d", d=128)
        def qkb(pi):
            return qkc[:, pi * 16:(pi + 1) * 16].rearrange("p (h o) -> p h o", o=1) \
                .broadcast_to([128, 16, 128])
        def vv4(m_):
            return vs4[:, m_ * 2048:(m_ + 1) * 2048].rearrange("p (h d) -> p h d", d=128)
        sq_dep = None
        for l in range(B):
            vw = list(v_deps)
            if l == 0:
                odep = vop(lambda v, _v=vv4(0), _q=qkb(PIDX[(0, 0)]):
                           v.tensor_tensor(o3, _v, _q, ALU.mult),
                           waits=vw + [sq_dep, conv_sil])
            elif l == 1:
                vop(lambda v, _v=vv4(0), _q=qkb(PIDX[(1, 0)]):
                    v.tensor_tensor(s0, _v, _q, ALU.mult), waits=vw)
                vop(lambda v, _v=vv4(1), _q=qkb(PIDX[(1, 1)]):
                    v.tensor_tensor(s1, _v, _q, ALU.mult))
                odep = vop(lambda v: v.tensor_tensor(o3, s0, s1, ALU.add),
                           waits=[sq_dep, conv_sil])
            elif l == 2:
                vop(lambda v, _v=vv4(0), _q=qkb(PIDX[(2, 0)]):
                    v.tensor_tensor(s0, _v, _q, ALU.mult), waits=vw)
                vop(lambda v, _v=vv4(1), _q=qkb(PIDX[(2, 1)]):
                    v.tensor_tensor(s1, _v, _q, ALU.mult))
                vop(lambda v: v.tensor_tensor(s2, s0, s1, ALU.add))
                vop(lambda v, _v=vv4(2), _q=qkb(PIDX[(2, 2)]):
                    v.tensor_tensor(s0, _v, _q, ALU.mult))
                odep = vop(lambda v: v.tensor_tensor(o3, s2, s0, ALU.add),
                           waits=[sq_dep, conv_sil])
            else:
                vop(lambda v, _v=vv4(0), _q=qkb(PIDX[(3, 0)]):
                    v.tensor_tensor(s0, _v, _q, ALU.mult), waits=vw)
                vop(lambda v, _v=vv4(1), _q=qkb(PIDX[(3, 1)]):
                    v.tensor_tensor(s1, _v, _q, ALU.mult))
                vop(lambda v: v.tensor_tensor(s2, s0, s1, ALU.add))
                vop(lambda v, _v=vv4(2), _q=qkb(PIDX[(3, 2)]):
                    v.tensor_tensor(s0, _v, _q, ALU.mult))
                vop(lambda v, _v=vv4(3), _q=qkb(PIDX[(3, 3)]):
                    v.tensor_tensor(s1, _v, _q, ALU.mult))
                vop(lambda v: v.tensor_tensor(c3, s0, s1, ALU.add))
                odep = vop(lambda v: v.tensor_tensor(o3, s2, c3, ALU.add),
                           waits=[sq_dep, conv_sil])
            vopd(lambda v: v.tensor_reduce(sm(SM_SO), o3, AX.X, ALU.add))
            vop(lambda v: v.tensor_tensor(s1, o3, o3, ALU.mult))
            vopd(lambda v: v.tensor_reduce(sm(SM_SO2), s1, AX.X, ALU.add))
            vopd(lambda v, _l=l: v.tensor_scalar(MEANv(_l), sm(SM_SO), 1.0 / 128,
                                                 None, ALU.mult))
            vopd(lambda v: v.tensor_scalar(sm(SM_T1), sm(SM_SO2), 1.0 / 128,
                                           None, ALU.mult))
            vopd(lambda v, _l=l: v.tensor_tensor(sm(SM_T2), MEANv(_l), MEANv(_l), ALU.mult))
            vopd(lambda v, _l=l: v.tensor_tensor(VARSv(_l), sm(SM_T1), sm(SM_T2),
                                                 ALU.subtract))
            mb_ = MEANv(l)[:, :, None].broadcast_to([128, 16, 128])
            vop(lambda v, _m=mb_,
                _o=onrm[:, l * 2048:(l + 1) * 2048].rearrange("p (h d) -> p h d", d=128):
                v.tensor_tensor(_o, o3, _m, ALU.subtract))
            sq_dep = (sV, sV.n)
        sd_dep = sopd(lambda sce: sce.activation(SDb, VARSb, AF.Sqrt,
                                                 bias=epsb2[:, 2:3]),
                      waits=[sq_dep, eps2_dep])
        rs_dep = vopd(lambda v: v.reciprocal(RSTDb, SDb), waits=[sd_dep])
        for l in range(B):
            rb_ = RSTDv(l)[:, :, None].broadcast_to([128, 16, 128])
            vop(lambda v, _r=rb_,
                _o=onrm[:, l * 2048:(l + 1) * 2048].rearrange("p (h d) -> p h d", d=128):
                v.tensor_tensor(_o, _o, _r, ALU.mult),
                waits=[rs_dep] if l == 0 else [])
            onrm_l = (sV, sV.n)
            for pp in range(2):
                ti = pp * B + l
                oi = ti % 2
                og_dep = vop(lambda v, _o=opg[oi],
                             _s=sgt[:, ti * 512:ti * 512 + 512],
                             _n=onrm[:, l * 2048 + pp * 512:l * 2048 + pp * 512 + 512]:
                             v.tensor_tensor(_o[:, :], _s, _n, ALU.mult),
                             waits=[sg_ready[(pp, l)], onrm_l, prev_opg_dma[oi]])
                gate_slot[ti] = og_dep
                prev_opg_dma[oi] = dma("sync",
                    outpb[tcs[l] * 128:(tcs[l] + 1) * 128, pp * 512:(pp + 1) * 512],
                    opg[oi][:, :], sST, waits=[og_dep])
        return (sV, sV.n)

    def emit_gates_wave2(g, sg_ready, onrm_mark, gate_slot):
        tcs = [l * 4 + g for l in range(B)]
        for pp in (2, 3):
            for l in range(B):
                ti = pp * B + l
                oi = ti % 2
                og_dep = vop(lambda v, _o=opg[oi],
                             _s=sgt[:, ti * 512:ti * 512 + 512],
                             _n=onrm[:, l * 2048 + pp * 512:l * 2048 + pp * 512 + 512]:
                             v.tensor_tensor(_o[:, :], _s, _n, ALU.mult),
                             waits=[sg_ready[(pp, l)], onrm_mark, prev_opg_dma[oi]])
                gate_slot[ti] = og_dep
                prev_opg_dma[oi] = dma("sync",
                    outpb[tcs[l] * 128:(tcs[l] + 1) * 128, pp * 512:(pp + 1) * 512],
                    opg[oi][:, :], sST, waits=[og_dep])

    conv_free = [None]
    xc_dep_cur = emit_conv(0)
    gate_slot = {}
    dots_prev = None
    micro_prev = None
    for g in range(4):
        xc_this = xc_dep_cur
        q_deps, k_deps, lg_deps, v_deps, sg_ready = emit_proj(
            g, xc_this, gate_slot, dots_prev, micro_prev)
        emit_gwave(g, [2, 3], xc_this, sg_ready, gate_slot)
        emit_rope_coeff_dots(g, q_deps, k_deps, lg_deps)
        dots_prev = (sV, sV.n)
        if g < 3:
            xc_dep_cur = emit_conv(g + 1)
        micro_prev = emit_micro_ln(g, v_deps, xc_dep_cur[0] if g < 3 else None,
                                   sg_ready, gate_slot)
        emit_gates_wave2(g, sg_ready, micro_prev, gate_slot)
        if debug and g == 0:
            gm = p.marks(allsems)
            for nmm, buf in [("logits", lgs), ("qs4", qs4), ("ks4", ks4), ("vs4", vs4),
                             ("onrm", onrm), ("dots", dots), ("qkc", qkc)]:
                dma("sync", dbg[nmm], buf[:, :], sST, waits=gm)
    if debug:
        dma("sync", dbg["xc"], xn[:, :], sST, waits=p.marks(allsems))
        dma("sync", dbg["outpb"], outpb, sST, waits=p.marks(allsems))

    s4_end = [(sPE, sPE.n), (sV, sV.n), (sSc, sSc.n), (sWL, sWL.n)]
    p.gate(p.marks(allsems))
    es_ffn.close()
    es_w.close()
    es_xn.close()

    es_s5 = ExitStack()
    s5a = lambda name, shape, dt=BF16: es_s5.enter_context(nc.sbuf_tensor(name, shape, dt))
    w5o = s5a("w5o", [128, 16 * 2048])
    w5f = s5a("w5f", [128, 16 * 2048])
    mtm = s5a("mtm", [128, 2048])
    otm = s5a("otm", [128, 2048])
    mfm = s5a("mfm", [128, 2048])
    ofm = s5a("ofm", [128, 2048])
    xb5 = [s5a(f"xb5{i}", [128, D], F32) for i in range(2)]
    outs = [s5a(f"outs{i}", [128, D], F32) for i in range(2)]

    w5p_deps = []
    for pn in range(4):
        d1 = dma("gpsimd",
                 w5o[:, pn * 8192:(pn + 1) * 8192].rearrange("p (mc n) -> p mc n", n=512),
                 rows_fm(woT, 2048)[:, :, pn * 512:(pn + 1) * 512], sWL,
                 waits=s4_end if pn == 0 else [])
        d2 = dma("gpsimd",
                 w5f[:, pn * 8192:(pn + 1) * 8192].rearrange("p (mc n) -> p mc n", n=512),
                 rows_fm(fwout, 2048)[:, :, pn * 512:(pn + 1) * 512], sWL)
        w5p_deps.append([d1, d2])

    out_deps = []
    prev_fm = None
    prev_ev = None
    for tc in range(NTC):
        mld = dma("sync", mtm[:, :], mixb[tc * 128:(tc + 1) * 128, :], sST,
                  waits=[mixb_done, prev_fm])
        old = dma("sync", otm[:, :], outpb[tc * 128:(tc + 1) * 128, :], sST,
                  waits=[prev_opg_dma[0], prev_opg_dma[1], prev_fm])
        xld = dma("sync", xb5[tc % 2][:, :], x[tc * 128:(tc + 1) * 128, :], sXL,
                  waits=[prev_ev])
        fm_deps = []
        for src, ldep, dst, bks in [(mtm, mld, mfm, (4, 5)), (otm, old, ofm, (6, 7))]:
            for halfq in range(4):
                bk = bks[halfq % 2]
                for k in range(4):
                    mcc = halfq * 4 + k
                    mm(bk, k * 128, (k + 1) * 128, src[:, mcc * 128:(mcc + 1) * 128],
                       ident[:, :], start=True, stop=(k == 3), waits=[ldep])
                pv = pb[bk][:, :].rearrange("p (k t) -> p k t", t=128)
                dv = dst[:, :].rearrange("p (mc t) -> p mc t", t=128)[
                    :, halfq * 4:halfq * 4 + 4, :]
                fm_deps.append(bank_read(bk, "vector",
                    lambda v, _o=dv, _i=pv: v.tensor_copy(_o, _i)))
        prev_fm = fm_deps[-1]
        ev_deps = []
        for pn in range(4):
            bko = pn % 2
            for mc in range(16):
                mm(bko, 0, 512, mfm[:, mc * 128:(mc + 1) * 128],
                   w5o[:, pn * 8192 + mc * 512:pn * 8192 + mc * 512 + 512],
                   start=(mc == 0), stop=False,
                   waits=(w5p_deps[pn] + fm_deps) if mc == 0 else [])
            for mc in range(16):
                mm(bko, 0, 512, ofm[:, mc * 128:(mc + 1) * 128],
                   w5f[:, pn * 8192 + mc * 512:pn * 8192 + mc * 512 + 512],
                   start=False, stop=(mc == 15))
            ev_deps.append(bank_read(bko, "vector",
                lambda v, _b=bko, _o=outs[tc % 2][:, pn * 512:pn * 512 + 512],
                _x=xb5[tc % 2][:, pn * 512:pn * 512 + 512]:
                v.scalar_tensor_tensor(_o, pb[_b][:, :], 1.0, _x, ALU.mult, ALU.add),
                waits=[xld, out_deps[tc - 2] if tc >= 2 else None]))
        prev_ev = ev_deps[-1]
        out_deps.append(dma("sync", out[tc * 128:(tc + 1) * 128, :], outs[tc % 2][:, :],
                            sST, waits=ev_deps))

    p.emit("sync", None, waits=[out_deps[-1]])

    with nc.Block() as block:
        p.flush(block)

    es_s5.close()
    es.close()
    return nc, dbg



def kernel(**inputs):
    in_maps = _shard_inputs(inputs)
    nc, _ = build_nc(debug=False)
    res = run_bass_kernel_spmd(nc, in_maps, core_ids=list(range(NCORES)))
    outf = np.empty((B, S, D), np.float32)
    for c in range(NCORES):
        outf[:, c * SC:(c + 1) * SC, :] = np.asarray(res.results[c]["out"]).reshape(B, SC, D)
    return outf
